# version 06
# baseline (speedup 1.0000x reference)
import numpy as np
import concourse.bass as bass
import concourse.mybir as mybir
from concourse.bass_utils import run_bass_kernel_spmd

F32 = mybir.dt.float32
BF16 = mybir.dt.bfloat16
AF = mybir.ActivationFunctionType
ALU = mybir.AluOpType

D = 1024
SQ = 8192
SK = 4096
TOK = 2048
NSTRIP = 16
KSTRIP = 8
DHID = 4096

ENGINES = ("sync", "gpsimd", "tensor", "scalar", "vector")
SEMS = ("LD", "WL", "SP", "ST", "MM", "SC", "VC",
        "LWQ", "LWK", "LWV", "LX0", "LX1", "LXN", "LW10", "LW11", "LW20", "LW21",
        "LHB0", "LHB1", "LQB", "LKB", "LVB", "SPA0", "SPA1", "SPE0", "SPE1",
        "LP4X0", "LP4X1", "LP4F0", "LP4F1", "LP4A0", "LP4A1")


class Plan:

    def __init__(self, sems):
        self.sems = sems
        self.ev = {e: [] for e in ENGINES}
        self.cnt = {k: 0 for k in SEMS}

    def n(self, sem):
        return self.cnt[sem]

    def emit(self, eng, fn, waits=(), inc=None, inc_n=1):
        ws = tuple((s, v) for (s, v) in waits if v > 0)
        self.ev[eng].append((fn, ws, inc, inc_n))
        if inc is not None:
            self.cnt[inc] += inc_n

    def barrier_waits(self):
        return {s: self.cnt[s] for s in SEMS}

    def flush(self, block):
        for e in ENGINES:
            evs = self.ev[e]
            if not evs:
                continue
            self.ev[e] = []

            def body(eng, evs=evs):
                last_wait = {}
                for fn, waits, inc, inc_n in evs:
                    for s, v in waits:
                        if last_wait.get(s, -1) < v:
                            eng.wait_ge(self.sems[s], v)
                            last_wait[s] = v
                    r = fn(eng)
                    if inc is not None:
                        r.then_inc(self.sems[inc], inc_n)
            getattr(block, e)(body)


def build_nc(taps=False):
    nc = bass.Bass("TRN2", target_bir_lowering=False)

    xfm_d = nc.dram_tensor("xfm", [NSTRIP, 128, 8, 512], F32, kind="ExternalInput").ap()
    xtm_d = nc.dram_tensor("xtm", [TOK, D], F32, kind="ExternalInput").ap()
    xofm_d = nc.dram_tensor("xofm", [4, 128, 8, 512], F32, kind="ExternalInput").ap()
    kft_d = nc.dram_tensor("kft", [KSTRIP, 128, 8, 512], F32, kind="ExternalInput").ap()
    vft_d = nc.dram_tensor("vft", [KSTRIP, 128, 8, 512], F32, kind="ExternalInput").ap()
    wq_d = nc.dram_tensor("wq", [128, 2048], F32, kind="ExternalInput").ap()
    wk_d = nc.dram_tensor("wk", [128, 2048], F32, kind="ExternalInput").ap()
    wv_d = nc.dram_tensor("wv", [128, 2048], F32, kind="ExternalInput").ap()
    qb_d = nc.dram_tensor("qb", [128, 2], F32, kind="ExternalInput").ap()
    kb_d = nc.dram_tensor("kb", [128, 2], F32, kind="ExternalInput").ap()
    vb_d = nc.dram_tensor("vb", [1, 256], F32, kind="ExternalInput").ap()
    w1_d = nc.dram_tensor("w1p", [32, 128, 1024], F32, kind="ExternalInput").ap()
    w2_d = nc.dram_tensor("w2p", [32, 128, 1024], F32, kind="ExternalInput").ap()
    h1_d = nc.dram_tensor("h1p", [32, 128, 4096], F32, kind="ExternalInput").ap()
    h2_d = nc.dram_tensor("h2p", [32, 128, 4096], F32, kind="ExternalInput").ap()
    w3_d = nc.dram_tensor("w3p", [8, 128, 4096], F32, kind="ExternalInput").ap()
    s1_d = nc.dram_tensor("s1c", [128, 32], F32, kind="ExternalInput").ap()
    s2_d = nc.dram_tensor("s2c", [128, 32], F32, kind="ExternalInput").ap()
    sh1_d = nc.dram_tensor("sh1c", [128, 32], F32, kind="ExternalInput").ap()
    sh2_d = nc.dram_tensor("sh2c", [128, 32], F32, kind="ExternalInput").ap()
    s3_d = nc.dram_tensor("s3c", [128, 8], F32, kind="ExternalInput").ap()

    out_d = nc.dram_tensor("out", [TOK, D], F32, kind="ExternalOutput").ap()
    scratch_kind = "ExternalOutput" if taps else "Internal"
    av_scr = nc.dram_tensor("av_scr", [4, 64, SQ], F32, kind=scratch_kind).ap()
    stat_scr = nc.dram_tensor("stat_scr", [2, TOK], F32, kind="Internal").ap()
    q_scr = nc.dram_tensor("q_scr", [2, 128, SQ], BF16, kind="Internal").ap()
    k_scr = nc.dram_tensor("k_scr", [2, 128, SK], BF16, kind="Internal").ap()
    v_scr = nc.dram_tensor("v_scr", [2, 128, 4160], BF16, kind="Internal").ap()
    xn_scr = nc.dram_tensor("xn_scr", [128, 8 * TOK], BF16, kind="Internal").ap()
    ffn_scr = nc.dram_tensor("ffn_scr", [TOK, D], F32, kind=scratch_kind).ap()
    if taps:
        qtap_d = nc.dram_tensor("q_tap", [128, 2 * SQ], BF16, kind="ExternalOutput").ap()
        xntap_d = nc.dram_tensor("xn_tap", [128, 8 * TOK], BF16, kind="ExternalOutput").ap()
        ktap_d = nc.dram_tensor("k_tap", [128, 2 * SK], BF16, kind="ExternalOutput").ap()
        vtap_d = nc.dram_tensor("v_tap", [128, 4 * 32 * 65], BF16, kind="ExternalOutput").ap()
        mtap_d = nc.dram_tensor("m_tap", [1, 2 * TOK], F32, kind="ExternalOutput").ap()
        gtap_d = nc.dram_tensor("g_tap", [128, 32 * 1024], BF16, kind="ExternalOutput").ap()
        hmtap_d = nc.dram_tensor("hm_tap", [128, 32 * 1024], BF16, kind="ExternalOutput").ap()
        g2tap_d = nc.dram_tensor("g2_tap", [128, 32 * 1024], BF16, kind="ExternalOutput").ap()
        ctap_d = nc.dram_tensor("c_tap", [128, 48], F32, kind="ExternalOutput").ap()

    from contextlib import ExitStack
    with ExitStack() as ctx:
        sems = {s: ctx.enter_context(nc.semaphore(s)) for s in SEMS}
        P = Plan(sems)

        ident = ctx.enter_context(nc.sbuf_tensor("ident", [128, 128], F32))
        ones_col = ctx.enter_context(nc.sbuf_tensor("ones_col", [128, 1], BF16))
        ones_row = ctx.enter_context(nc.sbuf_tensor("ones_row", [1, 128], F32))
        eps1 = ctx.enter_context(nc.sbuf_tensor("eps1", [1, 1], F32))
        qb_sb = ctx.enter_context(nc.sbuf_tensor("qb_sb", [128, 2], F32))
        kb_sb = ctx.enter_context(nc.sbuf_tensor("kb_sb", [128, 2], F32))
        vb_sb = ctx.enter_context(nc.sbuf_tensor("vb_sb", [1, 256], F32))
        s1_sb = ctx.enter_context(nc.sbuf_tensor("s1_sb", [128, 32], F32))
        s2_sb = ctx.enter_context(nc.sbuf_tensor("s2_sb", [128, 32], F32))
        sh1_sb = ctx.enter_context(nc.sbuf_tensor("sh1_sb", [128, 32], F32))
        sh2_sb = ctx.enter_context(nc.sbuf_tensor("sh2_sb", [128, 32], F32))
        s3_sb = ctx.enter_context(nc.sbuf_tensor("s3_sb", [128, 8], F32))
        ones2 = ctx.enter_context(nc.sbuf_tensor("ones2", [128, 128], F32))

        block = ctx.enter_context(nc.Block())

        def full_barrier():
            w = P.barrier_waits()
            waits = tuple(w.items())
            for e in ENGINES:
                P.emit(e, lambda eng: eng.nop(), waits=waits)

        P.emit("sync", lambda sy: sy.dma_start(out=qb_sb[:, :], in_=qb_d), inc="LD", inc_n=16)
        P.emit("sync", lambda sy: sy.dma_start(out=kb_sb[:, :], in_=kb_d), inc="LD", inc_n=16)
        P.emit("sync", lambda sy: sy.dma_start(out=vb_sb[:, :], in_=vb_d), inc="LD", inc_n=16)
        P.emit("sync", lambda sy: sy.dma_start(out=s1_sb[:, :], in_=s1_d), inc="LD", inc_n=16)
        P.emit("sync", lambda sy: sy.dma_start(out=s2_sb[:, :], in_=s2_d), inc="LD", inc_n=16)
        P.emit("sync", lambda sy: sy.dma_start(out=sh1_sb[:, :], in_=sh1_d), inc="LD", inc_n=16)
        P.emit("sync", lambda sy: sy.dma_start(out=sh2_sb[:, :], in_=sh2_d), inc="LD", inc_n=16)
        P.emit("sync", lambda sy: sy.dma_start(out=s3_sb[:, :], in_=s3_d), inc="LD", inc_n=16)
        P.emit("vector", lambda v: v.memset(ones_col[:, :], 1.0), inc="VC")
        P.emit("vector", lambda v: v.memset(ones_row[:, :], 1.0), inc="VC")
        P.emit("vector", lambda v: v.memset(eps1[:, :], 1e-5), inc="VC")
        P.emit("vector", lambda v: v.memset(ones2[:, :], 1.0), inc="VC")
        P.emit("gpsimd", lambda g: g.memset(ident[:, :], 0.0), inc="WL")
        P.emit("gpsimd", lambda g: g.affine_select(
            out=ident[:, :], in_=ident[:, :], compare_op=ALU.not_equal,
            fill=1.0, base=0, pattern=[[-1, 128]], channel_multiplier=1), inc="WL")
        ld_bias = P.n("LD")
        vc_init = P.n("VC")

        with (
            nc.sbuf_tensor("q_fm", [128, 2 * SQ], BF16) as q_fm,
            nc.sbuf_tensor("k_fm", [128, 2 * SK], BF16) as k_fm,
            nc.sbuf_tensor("v_tm", [128, 4 * 32 * 65], BF16) as v_tm,
            nc.sbuf_tensor("xn_own", [128, 8 * TOK], BF16) as xn_own,
        ):
            with (
                nc.sbuf_tensor("xbuf", [128, 2 * 8192], BF16) as xbuf,
                nc.sbuf_tensor("x2buf", [128, 8192], BF16) as x2buf,
                nc.sbuf_tensor("wq_sb", [128, 2048], BF16) as wq_sb,
                nc.sbuf_tensor("wk_sb", [128, 2048], BF16) as wk_sb,
                nc.sbuf_tensor("wv_sb", [128, 2048], BF16) as wv_sb,
                nc.sbuf_tensor("rows", [128, 6144], F32) as rows,
                nc.sbuf_tensor("rows2", [128, 2 * TOK], F32) as rows2,
                nc.sbuf_tensor("rbc", [128, 2 * 512], BF16) as rbc,
                nc.sbuf_tensor("mrbc", [128, 2 * 512], BF16) as mrbc,
                nc.sbuf_tensor("tmpb", [128, 512], BF16) as tmpb,
                nc.psum_tensor("psQ", [128, 1024], F32) as psQ,
            ):
                m_own = rows2[0:1, 0:TOK]
                rr_own = rows2[0:1, TOK:2 * TOK]
                P.emit("gpsimd", lambda g: g.dma_start(out=wq_sb[:, :], in_=wq_d), inc="LWQ", inc_n=16)
                P.emit("gpsimd", lambda g: g.dma_start(out=wk_sb[:, :], in_=wk_d), inc="LWK", inc_n=16)
                P.emit("gpsimd", lambda g: g.dma_start(out=wv_sb[:, :], in_=wv_d), inc="LWV", inc_n=16)

                mm_last_reader = {}
                chainst = {}
                vc_evacS = {}
                mm_sums = {}
                sc_qevac = {}
                strip_ev = {}

                with (
                    nc.psum_tensor("psS", [1, 2048], F32) as psS,
                    nc.psum_tensor("psB", [128, 1024], F32) as psB,
                ):
                    def emit_chain(c):
                        own = c >= 8
                        slot = c % 2
                        base = slot * 8192
                        srcs = []
                        for st in range(2):
                            if own:
                                srcs.append(xofm_d[2 * (c - 8) + st, :, :, :].rearrange(
                                    "p k c -> p (k c)"))
                            else:
                                srcs.append(xfm_d[2 * c + st, :, :, :].rearrange(
                                    "p k c -> p (k c)"))
                        lx = f"LX{slot}"
                        P.emit("gpsimd", lambda g, src=srcs[0], base=base: g.dma_start(
                            out=xbuf[:, base:base + 4096], in_=src),
                            waits=[("MM", mm_last_reader.get(c - 2, 0))], inc=lx, inc_n=16)
                        P.emit("gpsimd", lambda g, src=srcs[1], base=base: g.dma_start(
                            out=xbuf[:, base + 4096:base + 8192], in_=src),
                            inc=lx, inc_n=16)
                        wl_c = P.n(lx)
                        P.emit("scalar", lambda sc, base=base: sc.activation(
                            x2buf[:, :], xbuf[:, base:base + 8192], AF.Square),
                            waits=[(lx, wl_c), ("MM", mm_sums.get(c - 1, 0))], inc="SC")
                        sc_x2 = P.n("SC")

                        def mk_sums(t, base=base):
                            for st in range(2):
                                for ch in range(8):
                                    t.matmul(psS[:, st * 512:(st + 1) * 512], ones_col[:, :],
                                             xbuf[:, base + st * 4096 + ch * 512:
                                                  base + st * 4096 + (ch + 1) * 512],
                                             start=(ch == 0), stop=(ch == 7))
                            for st in range(2):
                                for ch in range(8):
                                    r = t.matmul(psS[:, 1024 + st * 512: 1024 + (st + 1) * 512],
                                                 ones_col[:, :],
                                                 x2buf[:, st * 4096 + ch * 512:
                                                       st * 4096 + (ch + 1) * 512],
                                                 start=(ch == 0), stop=(ch == 7))
                            return r
                        P.emit("tensor", mk_sums,
                               waits=[(lx, wl_c), ("SC", sc_x2), ("VC", vc_evacS.get(c - 1, 0))],
                               inc="MM")
                        mm_sums[c] = P.n("MM")
                        P.emit("vector", lambda v, slot=slot: v.tensor_copy(
                            rows[0:1, slot * 2048:(slot + 1) * 2048], psS[:, :]),
                            waits=[("MM", mm_sums[c])], inc="VC")
                        vc_evacS[c] = P.n("VC")

                        def mk_stats(v, slot=slot, own=own, c=c):
                            sb_ = slot * 2048
                            so_ = slot * 1024
                            mdst = m_own[:, (c - 8) * 1024:(c - 7) * 1024] if own \
                                else rows[64:65, so_:so_ + 1024]
                            msq = rows[32:33, so_:so_ + 1024]
                            ex2 = rows[32:33, 2048 + so_:2048 + so_ + 1024]
                            var = rows[32:33, 4096 + so_:4096 + so_ + 1024]
                            v.tensor_scalar_mul(mdst, rows[0:1, sb_:sb_ + 1024], 1.0 / 1024)
                            v.tensor_scalar_mul(ex2, rows[0:1, sb_ + 1024:sb_ + 2048], 1.0 / 1024)
                            v.tensor_mul(msq, mdst, mdst)
                            v.tensor_sub(var, ex2, msq)
                            return v.tensor_scalar_add(var, var, 1e-5)
                        P.emit("vector", mk_stats, waits=[("VC", vc_init)], inc="VC")
                        vc_stats = P.n("VC")
                        P.emit("scalar", lambda sc, slot=slot: sc.activation(
                            rows[64:65, 2048 + slot * 1024: 2048 + (slot + 1) * 1024],
                            rows[32:33, 4096 + slot * 1024: 4096 + (slot + 1) * 1024], AF.Sqrt),
                            waits=[("VC", vc_stats)], inc="SC")
                        sc_sd = P.n("SC")

                        def mk_rr(v, slot=slot, own=own, c=c):
                            so_ = slot * 1024
                            rdst = rr_own[:, (c - 8) * 1024:(c - 7) * 1024] if own \
                                else rows[64:65, 4096 + so_:4096 + so_ + 1024]
                            mdst = m_own[:, (c - 8) * 1024:(c - 7) * 1024] if own \
                                else rows[64:65, so_:so_ + 1024]
                            v.reciprocal(rdst, rows[64:65, 2048 + so_:2048 + so_ + 1024])
                            return v.tensor_mul(rows[0:1, 4096 + so_:4096 + so_ + 1024],
                                                mdst, rdst)
                        P.emit("vector", mk_rr, waits=[("SC", sc_sd)], inc="VC")
                        chainst[c] = (slot, base, P.n("VC"))

                    def emit_chainB(c):
                        own = c >= 8
                        slot, base, vc_rr = chainst[c]
                        mm_last = 0
                        for st in range(2):
                            sg = (c - 8) * 2 + st if own else 2 * c + st
                            s2_ = (2 * c + st) % 2

                            def mk_bcast(t, slot=slot, st=st, own=own, c=c):
                                so_ = slot * 1024
                                if own:
                                    rsrc = rr_own[:, (c - 8) * 1024 + st * 512:
                                                  (c - 8) * 1024 + (st + 1) * 512]
                                    lh = ones_row[:, :]
                                else:
                                    rsrc = rows[64:65, 4096 + so_ + st * 512:
                                                4096 + so_ + (st + 1) * 512]
                                    lh = ones2[64:65, 0:128]
                                t.matmul(psB[:, 0:512], lh, rsrc, start=True, stop=True)
                                return t.matmul(psB[:, 512:1024], ones_row[:, :],
                                                rows[0:1, 4096 + so_ + st * 512:
                                                     4096 + so_ + (st + 1) * 512],
                                                start=True, stop=True)
                            P.emit("tensor", mk_bcast,
                                   waits=[("VC", vc_rr), ("VC", strip_ev.get(("evB", c, st - 1),
                                          strip_ev.get(("evB", c - 1, 1), 0)))], inc="MM")
                            mm_bc = P.n("MM")
                            P.emit("vector", lambda v, s2_=s2_: (
                                v.tensor_copy(rbc[:, s2_ * 512:(s2_ + 1) * 512], psB[:, 0:512]),
                                v.tensor_copy(mrbc[:, s2_ * 512:(s2_ + 1) * 512],
                                              psB[:, 512:1024]))[-1],
                                waits=[("MM", mm_bc)], inc="VC")
                            strip_ev[("evB", c, st)] = P.n("VC")

                            def mk_xn(v, base=base, st=st, s2_=s2_, own=own, c=c):
                                for ch in range(8):
                                    xsl = xbuf[:, base + st * 4096 + ch * 512:
                                               base + st * 4096 + (ch + 1) * 512]
                                    v.tensor_mul(tmpb[:, :], xsl, rbc[:, s2_ * 512:(s2_ + 1) * 512])
                                    if own:
                                        tpos = (c - 8) * 1024 + st * 512
                                        dst = xn_own[:, ch * TOK + tpos: ch * TOK + tpos + 512]
                                    else:
                                        dst = xsl
                                    r = v.tensor_sub(dst, tmpb[:, :],
                                                     mrbc[:, s2_ * 512:(s2_ + 1) * 512])
                                return r
                            P.emit("vector", mk_xn, waits=[("VC", strip_ev[("evB", c, st)])],
                                   inc="VC")
                            vc_xn = P.n("VC")
                            strip_ev[("xn", c, st)] = vc_xn
                            if not own:
                                def mk_q(t, base=base, st=st):
                                    for mc in range(2):
                                        for kc in range(8):
                                            r = t.matmul(
                                                psQ[:, mc * 512:(mc + 1) * 512],
                                                wq_sb[:, kc * 256 + mc * 128: kc * 256 + (mc + 1) * 128],
                                                xbuf[:, base + st * 4096 + kc * 512:
                                                     base + st * 4096 + (kc + 1) * 512],
                                                start=(kc == 0), stop=(kc == 7))
                                    return r
                                P.emit("tensor", mk_q,
                                       waits=[("VC", vc_xn), ("LWQ", 16),
                                              ("SC", sc_qevac.get(sg - 1, 0))], inc="MM")
                                mm_last = P.n("MM")

                                def mk_qevac(sc, sg=sg):
                                    for mc in range(2):
                                        r = sc.activation(
                                            q_fm[:, mc * SQ + sg * 512: mc * SQ + (sg + 1) * 512],
                                            psQ[:, mc * 512:(mc + 1) * 512], AF.Relu,
                                            bias=qb_sb[:, mc:mc + 1])
                                    return r
                                P.emit("scalar", mk_qevac,
                                       waits=[("MM", mm_last), ("LD", ld_bias)], inc="SC")
                                sc_qevac[sg] = P.n("SC")
                        mm_last_reader[c] = P.n("MM") if own else mm_last
                        if own:
                            mm_last_reader[c] = mm_sums[c]
                            strip_ev[("ownxn", c)] = P.n("VC")


                    for c in range(10):
                        emit_chain(c)
                        if c >= 1:
                            emit_chainB(c - 1)
                    emit_chainB(9)

                    own_xn_vc = P.n("VC")
                    sc_q_all = P.n("SC")

                sc_kevac = {}
                mm_kread = {}
                with nc.psum_tensor("psV", [128, 256], F32) as psV:
                    for c in range(4):
                        slot = c % 2
                        base = slot * 8192
                        w = [("MM", mm_kread.get(c - 2, 0))]
                        if c < 2:
                            w = [("VC", own_xn_vc), ("MM", mm_sums[8 + c])]
                        lx = f"LX{slot}"
                        P.emit("gpsimd", lambda g, c=c, base=base: g.dma_start(
                            out=xbuf[:, base:base + 4096],
                            in_=kft_d[2 * c, :, :, :].rearrange("p k c -> p (k c)")),
                            waits=w, inc=lx, inc_n=16)
                        P.emit("gpsimd", lambda g, c=c, base=base: g.dma_start(
                            out=xbuf[:, base + 4096:base + 8192],
                            in_=kft_d[2 * c + 1, :, :, :].rearrange("p k c -> p (k c)")),
                            inc=lx, inc_n=16)
                        wl_c = P.n(lx)
                        for st in range(2):
                            sg = 2 * c + st

                            def mk_k(t, base=base, st=st):
                                for mc in range(2):
                                    for kc in range(8):
                                        r = t.matmul(
                                            psQ[:, mc * 512:(mc + 1) * 512],
                                            wk_sb[:, kc * 256 + mc * 128: kc * 256 + (mc + 1) * 128],
                                            xbuf[:, base + st * 4096 + kc * 512:
                                                 base + st * 4096 + (kc + 1) * 512],
                                            start=(kc == 0), stop=(kc == 7))
                                return r
                            P.emit("tensor", mk_k,
                                   waits=[(lx, wl_c), ("LWK", 16),
                                          ("SC", sc_kevac.get(sg - 1, sc_q_all))], inc="MM")
                            mm_k = P.n("MM")

                            def mk_kevac(sc, sg=sg):
                                for mc in range(2):
                                    r = sc.activation(
                                        k_fm[:, mc * SK + sg * 512: mc * SK + (sg + 1) * 512],
                                        psQ[:, mc * 512:(mc + 1) * 512], AF.Relu,
                                        bias=kb_sb[:, mc:mc + 1])
                                return r
                            P.emit("scalar", mk_kevac, waits=[("MM", mm_k)], inc="SC")
                            sc_kevac[sg] = P.n("SC")
                        mm_kread[c] = P.n("MM")

                    vc_vevac = {}
                    mm_vread = {}
                    for c in range(4):
                        slot = c % 2
                        base = slot * 8192
                        w = [("MM", mm_vread.get(c - 2, mm_kread.get(2 + c, 0)))]
                        lx = f"LX{slot}"
                        P.emit("gpsimd", lambda g, c=c, base=base: g.dma_start(
                            out=xbuf[:, base:base + 4096],
                            in_=vft_d[2 * c, :, :, :].rearrange("p k c -> p (k c)")),
                            waits=w, inc=lx, inc_n=16)
                        P.emit("gpsimd", lambda g, c=c, base=base: g.dma_start(
                            out=xbuf[:, base + 4096:base + 8192],
                            in_=vft_d[2 * c + 1, :, :, :].rearrange("p k c -> p (k c)")),
                            inc=lx, inc_n=16)
                        wl_c = P.n(lx)
                        for tsub in range(8):
                            kt = c * 8 + tsub
                            off = base + (tsub // 4) * 4096 + (tsub % 4 * 0)

                            def mk_v(t, base=base, tsub=tsub):
                                stp = tsub // 4
                                sub = tsub % 4
                                for kc in range(8):
                                    t.matmul(psV[:, :],
                                             xbuf[:, base + stp * 4096 + kc * 512 + sub * 128:
                                                  base + stp * 4096 + kc * 512 + (sub + 1) * 128],
                                             wv_sb[:, kc * 256:(kc + 1) * 256],
                                             start=(kc == 0), stop=False)
                                return t.matmul(psV[:, :], ones_row[:, :], vb_sb[:, :],
                                                start=False, stop=True)
                            P.emit("tensor", mk_v,
                                   waits=[(lx, wl_c), ("LWV", 16), ("LD", ld_bias),
                                          ("VC", vc_vevac.get(kt - 1, 0))], inc="MM")
                            mm_v = P.n("MM")

                            def mk_vevac(v, kt=kt):
                                for hl in range(4):
                                    r = v.tensor_relu(
                                        v_tm[:, hl * 2080 + kt * 65: hl * 2080 + kt * 65 + 64],
                                        psV[:, hl * 64:(hl + 1) * 64])
                                return r
                            P.emit("vector", mk_vevac, waits=[("MM", mm_v)], inc="VC")
                            vc_vevac[kt] = P.n("VC")
                        mm_vread[c] = P.n("MM")

                    def mk_vones(v):
                        for hl in range(4):
                            r = v.memset(v_tm[:, hl * 2080 + 64: (hl + 1) * 2080: 65], 1.0)
                        return r
                    P.emit("vector", mk_vones, inc="VC")

                P.emit("gpsimd", lambda g: g.dma_start(out=stat_scr[0, :], in_=m_own[:, :]),
                       waits=[("VC", P.n("VC"))], inc="SP", inc_n=16)
                P.emit("gpsimd", lambda g: g.dma_start(out=stat_scr[1, :], in_=rr_own[:, :]),
                       inc="SP", inc_n=16)
                wsp = tuple(P.barrier_waits().items())
                P.emit("gpsimd", lambda g: g.dma_start(out=q_scr[0], in_=q_fm[:, 0:SQ]),
                       waits=wsp, inc="SP", inc_n=16)
                P.emit("gpsimd", lambda g: g.dma_start(out=q_scr[1], in_=q_fm[:, SQ:2 * SQ]),
                       inc="SP", inc_n=16)
                P.emit("gpsimd", lambda g: g.dma_start(out=k_scr[0], in_=k_fm[:, 0:SK]),
                       inc="SP", inc_n=16)
                P.emit("gpsimd", lambda g: g.dma_start(out=k_scr[1], in_=k_fm[:, SK:2 * SK]),
                       inc="SP", inc_n=16)
                P.emit("gpsimd", lambda g: g.dma_start(out=v_scr[0], in_=v_tm[:, 0:4160]),
                       inc="SP", inc_n=16)
                P.emit("gpsimd", lambda g: g.dma_start(out=v_scr[1], in_=v_tm[:, 4160:8320]),
                       inc="SP", inc_n=16)
                P.emit("gpsimd", lambda g: g.dma_start(out=xn_scr, in_=xn_own[:, :]),
                       inc="SP", inc_n=16)
                sp_spill = P.n("SP")

                if taps:
                    w = P.barrier_waits()
                    P.emit("gpsimd", lambda g: g.dma_start(out=qtap_d, in_=q_fm[:, :]),
                           waits=tuple(w.items()), inc="SP", inc_n=16)
                    P.emit("gpsimd", lambda g: g.dma_start(out=xntap_d, in_=xn_own[:, :]),
                           inc="SP", inc_n=16)
                    P.emit("gpsimd", lambda g: g.dma_start(out=ktap_d, in_=k_fm[:, :]),
                           inc="SP", inc_n=16)
                    P.emit("gpsimd", lambda g: g.dma_start(out=vtap_d, in_=v_tm[:, :]),
                           inc="SP", inc_n=16)
                    P.emit("gpsimd", lambda g: g.dma_start(
                        out=mtap_d[:, 0:TOK], in_=m_own[:, :]), inc="SP", inc_n=16)
                    P.emit("gpsimd", lambda g: g.dma_start(
                        out=mtap_d[:, TOK:2 * TOK], in_=rr_own[:, :]), inc="SP", inc_n=16)

                full_barrier()
                P.flush(block)


        GROUPS2 = [[2 * i, 2 * i + 1] for i in range(16)]
        with (
            nc.sbuf_tensor("g_sb", [128, 32 * 1024], BF16) as g_sb,
            nc.sbuf_tensor("hbuf", [128, 2 * 4096], BF16) as hbuf,
            nc.sbuf_tensor("qbuf", [128, SQ], BF16) as qbuf,
            nc.sbuf_tensor("kbuf", [128, SK], BF16) as kbuf,
            nc.sbuf_tensor("vbuf", [128, 4160], BF16) as vbuf,
            nc.sbuf_tensor("ring", [128, 4096], BF16) as ring,
            nc.sbuf_tensor("av_stage", [64, 4 * 512], F32) as av_stage,
            nc.sbuf_tensor("rec_row", [1, 1024], F32) as rec_row,
            nc.sbuf_tensor("recb_sb", [64, 1024], F32) as recb_sb,
            nc.psum_tensor("psF", [128, 1024], F32) as psF,
            nc.psum_tensor("psA", [128, 1024], F32) as psA,
            nc.psum_tensor("psBq", [128, 1024], F32) as psBq,
            nc.psum_tensor("psAvA", [128, 512], F32) as psAvA,
            nc.psum_tensor("psAvB", [128, 512], F32) as psAvB,
        ):
            sp_ffn = {}
            sp_unit = {}
            sc_exp = {}
            mm_score = {}
            mm_av = {}
            AST = dict(vc_avtt=0, sc_expq=0, ld_pair=0)

            def build_att_events(pi):
                ev = []
                for qt in range(16):
                    u = pi * 16 + qt

                    def e_score(gi, qt=qt, u=u):
                        grp = GROUPS2[gi]

                        def mk(t):
                            for j, kc in enumerate(grp):
                                t.matmul(psA[:, j * 512:(j + 1) * 512],
                                         kbuf[0:64, kc * 128:(kc + 1) * 128],
                                         qbuf[0:64, qt * 512:(qt + 1) * 512],
                                         start=True, stop=True, tile_position=(0, 0))
                                r = t.matmul(psBq[:, j * 512:(j + 1) * 512],
                                             kbuf[64:128, kc * 128:(kc + 1) * 128],
                                             qbuf[64:128, qt * 512:(qt + 1) * 512],
                                             start=True, stop=True, tile_position=(64, 0))
                            return r
                        P.emit("tensor", mk,
                               waits=[("SC", sc_exp.get((u, gi - 1), AST["sc_expq"])),
                                      ("VC", AST["vc_avtt"]), ("LQB", AST["ld_pair"]),
                                      ("LKB", AST["ld_pair"]), ("LVB", AST["ld_pair"])],
                               inc="MM")
                        mm_score[(u, gi)] = P.n("MM")

                        def mkA(sc, gi=gi):
                            return sc.activation(ring[:, (gi % 2) * 1024:(gi % 2) * 1024 + 1024],
                                                 psA[:, :], AF.Exp, scale=0.125)

                        def mkB(sc, gi=gi):
                            return sc.activation(
                                ring[:, 2048 + (gi % 2) * 1024: 2048 + (gi % 2) * 1024 + 1024],
                                psBq[:, :], AF.Exp, scale=0.125)
                        ring_free = mm_av.get((u, gi - 2), 0)
                        P.emit("scalar", mkA,
                               waits=[("MM", max(mm_score[(u, gi)], ring_free))], inc="SC")
                        P.emit("scalar", mkB, inc="SC")
                        sc_exp[(u, gi)] = P.n("SC")

                    def e_av(g, u=u):
                        def mk(t):
                            for j, kc in enumerate(GROUPS2[g]):
                                t.matmul(psAvA[0:65, :],
                                         vbuf[:, kc * 65:(kc + 1) * 65],
                                         ring[:, (g % 2) * 1024 + j * 512:
                                              (g % 2) * 1024 + (j + 1) * 512],
                                         start=(kc == 0), stop=(kc == 31))
                                r = t.matmul(psAvB[0:65, :],
                                             vbuf[:, 2080 + kc * 65: 2080 + (kc + 1) * 65],
                                             ring[:, 2048 + (g % 2) * 1024 + j * 512:
                                                  2048 + (g % 2) * 1024 + (j + 1) * 512],
                                             start=(kc == 0), stop=(kc == 31))
                            return r
                        P.emit("tensor", mk,
                               waits=[("SC", sc_exp[(u, g)]), ("VC", AST["vc_avtt"])],
                               inc="MM")
                        mm_av[(u, g)] = P.n("MM")

                    def e_tail(qt=qt, u=u, pi=pi):
                        def mk_rec(v):
                            v.reciprocal(rec_row[:, 0:512], psAvA[64:65, :])
                            return v.reciprocal(rec_row[:, 512:1024], psAvB[64:65, :])
                        P.emit("vector", mk_rec, waits=[("MM", mm_av[(u, 15)])], inc="VC")
                        vc_rec = P.n("VC")

                        def mk_recb(t):
                            t.matmul(psA[0:64, 0:512], ones_row[:, 0:64], rec_row[:, 0:512],
                                     start=True, stop=True)
                            return t.matmul(psA[0:64, 512:1024], ones_row[:, 0:64],
                                            rec_row[:, 512:1024], start=True, stop=True)
                        P.emit("tensor", mk_recb,
                               waits=[("VC", vc_rec), ("SC", sc_exp[(u, 15)])], inc="MM")
                        mm_recb = P.n("MM")

                        def mk_recb_copy(v):
                            v.tensor_copy(recb_sb[:, 0:512], psA[0:64, 0:512])
                            return v.tensor_copy(recb_sb[:, 512:1024], psA[0:64, 512:1024])
                        P.emit("vector", mk_recb_copy, waits=[("MM", mm_recb)], inc="VC")
                        ping = u % 2

                        def mk_avtt(v, ping=ping):
                            v.tensor_mul(av_stage[:, ping * 1024: ping * 1024 + 512],
                                         psAvA[0:64, :], recb_sb[:, 0:512])
                            return v.tensor_mul(av_stage[:, ping * 1024 + 512: ping * 1024 + 1024],
                                                psAvB[0:64, :], recb_sb[:, 512:1024])
                        P.emit("vector", mk_avtt,
                               waits=[(f"SPA{u % 2}", P.n(f"SPA{u % 2}"))], inc="VC")
                        AST["vc_avtt"] = P.n("VC")
                        AST["sc_expq"] = sc_exp[(u, 15)]

                        def mk_spillA(g, qt=qt, ping=ping, pi=pi):
                            return g.dma_start(out=av_scr[2 * pi, :, qt * 512:(qt + 1) * 512],
                                               in_=av_stage[:, ping * 1024: ping * 1024 + 512])

                        def mk_spillB(g, qt=qt, ping=ping, pi=pi):
                            return g.dma_start(out=av_scr[2 * pi + 1, :, qt * 512:(qt + 1) * 512],
                                               in_=av_stage[:, ping * 1024 + 512: ping * 1024 + 1024])
                        P.emit("gpsimd", mk_spillA, waits=[("VC", AST["vc_avtt"])],
                               inc=f"SPA{u % 2}", inc_n=16)
                        P.emit("gpsimd", mk_spillB, inc=f"SPA{u % 2}", inc_n=16)
                        sp_unit[u] = P.n(f"SPA{u % 2}")

                    for gi in range(16):
                        ev.append(lambda gi=gi, f=e_score: f(gi))
                        if gi >= 1:
                            ev.append(lambda g=gi - 1, f=e_av: f(g))
                    ev.append(lambda f=e_av: f(15))
                    ev.append(e_tail)
                return ev

            WV = dict(q=None, credit=0.0, rate=0.0)

            def weave(dur_us):
                WV["credit"] += dur_us * WV["rate"]
                k = int(WV["credit"])
                WV["credit"] -= k
                for _ in range(k):
                    if WV["q"]:
                        WV["q"].pop(0)()

            mm_hb_prev = {}
            sc_gate = 0
            vc_gate = 0
            for hf in range(2):
                wqkv = [("SP", sp_spill)] if hf == 0 else                     [("VC", AST["vc_avtt"]), ("SP", sp_unit.get(16 * hf - 1, 0))]
                P.emit("sync", lambda sy, hf=hf: sy.dma_start(out=qbuf[:, :], in_=q_scr[hf]),
                       waits=wqkv, inc="LQB", inc_n=16)
                P.emit("sync", lambda sy, hf=hf: sy.dma_start(out=kbuf[:, :], in_=k_scr[hf]),
                       inc="LKB", inc_n=16)
                P.emit("sync", lambda sy, hf=hf: sy.dma_start(out=vbuf[:, :], in_=v_scr[hf]),
                       inc="LVB", inc_n=16)
                AST["ld_pair"] = 16 * (hf + 1)
                WV["q"] = build_att_events(hf)
                WV["rate"] = len(WV["q"]) / 1260.0
                WV["credit"] = 0.0

                with (
                    nc.sbuf_tensor(f"xnh_{hf}", [128, 8192], BF16) as xnh,
                    nc.sbuf_tensor(f"w1b_{hf}", [128, 2048], BF16) as w1buf,
                    nc.sbuf_tensor(f"w2b_{hf}", [128, 2048], BF16) as w2buf,
                    nc.sbuf_tensor(f"t1b_{hf}", [128, 1024], F32) as t1b,
                ):
                    P.emit("gpsimd", lambda g, hf=hf: g.dma_start(
                        out=xnh[:, :],
                        in_=xn_scr.rearrange("p (ch t) -> p ch t", t=TOK)[
                            :, :, hf * 1024:(hf + 1) * 1024]),
                        waits=[("SP", sp_spill)], inc="LXN", inc_n=16)
                    wl_xnh = P.n("LXN")
                    mm_a = {}
                    vc_stt = {}
                    sc_silu = {}
                    for m in range(32):
                        slot = m % 2
                        P.emit("gpsimd", lambda g, m=m, slot=slot: g.dma_start(
                            out=w1buf[:, slot * 1024:(slot + 1) * 1024], in_=w1_d[m, :, :]),
                            waits=[("MM", mm_a.get(m - 2, 0))], inc=f"LW1{slot}", inc_n=16)
                        P.emit("gpsimd", lambda g, m=m, slot=slot: g.dma_start(
                            out=w2buf[:, slot * 1024:(slot + 1) * 1024], in_=w2_d[m, :, :]),
                            inc=f"LW2{slot}", inc_n=16)
                        wl_w1 = P.n(f"LW1{slot}")
                        wl_w2 = P.n(f"LW2{slot}")

                        def mk_w1(t, slot=slot):
                            for nt in range(2):
                                for kc in range(8):
                                    r = t.matmul(psF[:, nt * 512:(nt + 1) * 512],
                                                 w1buf[:, slot * 1024 + kc * 128:
                                                       slot * 1024 + (kc + 1) * 128],
                                                 xnh[:, kc * 1024 + nt * 512:
                                                     kc * 1024 + (nt + 1) * 512],
                                                 start=(kc == 0), stop=(kc == 7))
                            return r
                        P.emit("tensor", mk_w1,
                               waits=[(f"LW1{slot}", wl_w1), (f"LW2{slot}", wl_w2),
                                      ("LXN", wl_xnh),
                                      ("VC", max(vc_stt.get(m - 1, 0), vc_gate)),
                                      ("SC", sc_gate)], inc="MM")
                        weave(3.6)
                        mm_w1 = P.n("MM")
                        P.emit("scalar", lambda sc, m=m: sc.activation(
                            t1b[:, :], psF[:, :], AF.Silu, scale=s1_sb[:, m:m + 1]),
                            waits=[("MM", mm_w1), ("VC", vc_stt.get(m - 1, 0))], inc="SC")
                        sc_silu[m] = P.n("SC")

                        def mk_w2(t, slot=slot):
                            for nt in range(2):
                                for kc in range(8):
                                    r = t.matmul(psF[:, nt * 512:(nt + 1) * 512],
                                                 w2buf[:, slot * 1024 + kc * 128:
                                                       slot * 1024 + (kc + 1) * 128],
                                                 xnh[:, kc * 1024 + nt * 512:
                                                     kc * 1024 + (nt + 1) * 512],
                                                 start=(kc == 0), stop=(kc == 7))
                            return r
                        P.emit("tensor", mk_w2,
                               waits=[("SC", sc_silu[m])], inc="MM")
                        weave(3.6)
                        mm_a[m] = P.n("MM")
                        P.emit("vector", lambda v, m=m: v.scalar_tensor_tensor(
                            g_sb[:, m * 1024:(m + 1) * 1024], psF[:, :], s2_sb[:, m:m + 1],
                            t1b[:, :], ALU.mult, ALU.mult),
                            waits=[("MM", mm_a[m])], inc="VC")
                        vc_stt[m] = P.n("VC")
                    vc_gate = vc_stt[31]

                with nc.sbuf_tensor(f"hm_sb_{hf}", [128, 32 * 1024], BF16) as hm_sb:
                    for stage, (wd, src, dst, ssb) in enumerate((
                            (h1_d, g_sb, hm_sb, sh1_sb),
                            (h2_d, hm_sb, g_sb, sh2_sb))):
                        if stage == 1:
                            sc_gate = sc_b[31]
                            mm_hb_prev = mm_b
                        mm_b = {}
                        sc_b = {}
                        for m in range(32):
                            slot = m % 2
                            P.emit("gpsimd", lambda g, m=m, slot=slot, wd=wd: g.dma_start(
                                out=hbuf[:, slot * 4096:(slot + 1) * 4096], in_=wd[m, :, :]),
                                waits=[("MM", max(mm_b.get(m - 2, 0),
                                                  mm_hb_prev.get(30 + m, 0)))],
                                inc=f"LHB{slot}", inc_n=16)
                            wl_m = P.n(f"LHB{slot}")
                            for nt in range(2):
                                def mk_h(t, slot=slot, src=src, nt=nt):
                                    for kc in range(32):
                                        r = t.matmul(
                                            psF[:, nt * 512:(nt + 1) * 512],
                                            hbuf[:, slot * 4096 + kc * 128:
                                                 slot * 4096 + (kc + 1) * 128],
                                            src[:, kc * 1024 + nt * 512: kc * 1024 + (nt + 1) * 512],
                                            start=(kc == 0), stop=(kc == 31))
                                    return r
                                P.emit("tensor", mk_h,
                                       waits=[(f"LHB{slot}", wl_m),
                                              ("SC", max(sc_b.get(m - 1, 0), sc_gate)),
                                              ("VC", vc_gate)], inc="MM")
                                weave(7.2)
                            mm_b[m] = P.n("MM")
                            P.emit("scalar", lambda sc, m=m, dst=dst, ssb=ssb: sc.activation(
                                dst[:, m * 1024:(m + 1) * 1024], psF[:, :], AF.Gelu,
                                scale=ssb[:, m:m + 1]),
                                waits=[("MM", mm_b[m])], inc="SC")
                            sc_b[m] = P.n("SC")
                    sc_gate = sc_b[31]
                    mm_hb_prev = mm_b

                with (
                    nc.sbuf_tensor(f"ffn_fm_{hf}", [128, 8192], F32) as ffn_fm,
                    nc.sbuf_tensor(f"ffn_stage_{hf}", [128, 2 * 1024], F32) as ffn_stage,
                ):
                    mm_d = {}
                    vc_d = {}
                    for m3 in range(8):
                        slot = m3 % 2
                        P.emit("gpsimd", lambda g, m3=m3, slot=slot: g.dma_start(
                            out=hbuf[:, slot * 4096:(slot + 1) * 4096], in_=w3_d[m3, :, :]),
                            waits=[("MM", max(mm_d.get(m3 - 2, 0),
                                              mm_hb_prev.get(30 + m3, 0)))],
                            inc="WL", inc_n=16)
                        wl_m = P.n("WL")
                        for nt in range(2):
                            def mk_w3(t, slot=slot, nt=nt):
                                for kc in range(32):
                                    r = t.matmul(
                                        psF[:, nt * 512:(nt + 1) * 512],
                                        hbuf[:, slot * 4096 + kc * 128:
                                             slot * 4096 + (kc + 1) * 128],
                                        g_sb[:, kc * 1024 + nt * 512: kc * 1024 + (nt + 1) * 512],
                                        start=(kc == 0), stop=(kc == 31))
                                return r
                            P.emit("tensor", mk_w3,
                                   waits=[("WL", wl_m), ("SC", sc_gate),
                                          ("VC", max(vc_d.get(m3 - 1, 0), vc_gate))],
                                   inc="MM")
                            weave(7.2)
                        mm_d[m3] = P.n("MM")
                        P.emit("vector", lambda v, m3=m3: v.tensor_scalar_mul(
                            ffn_fm[:, m3 * 1024:(m3 + 1) * 1024], psF[:, :],
                            s3_sb[:, m3:m3 + 1]),
                            waits=[("MM", mm_d[m3])], inc="VC")
                        vc_d[m3] = P.n("VC")

                    vc_e = {}
                    sp_e = {}
                    for tt in range(8):
                        slot = tt % 2

                        def mk_tr(t, tt=tt):
                            for fc in range(8):
                                r = t.transpose(
                                    psF[:, fc * 128:(fc + 1) * 128],
                                    ffn_fm[:, fc * 1024 + tt * 128: fc * 1024 + (tt + 1) * 128],
                                    ident[:, :])
                            return r
                        P.emit("tensor", mk_tr,
                               waits=[("VC", max(vc_e.get(tt - 1, 0), vc_d[7]))], inc="MM")
                        weave(2.4)
                        mm_tt = P.n("MM")
                        P.emit("vector", lambda v, slot=slot: v.tensor_copy(
                            ffn_stage[:, slot * 1024:(slot + 1) * 1024], psF[:, :]),
                            waits=[("MM", mm_tt), ("SP", sp_e.get(tt - 2, 0))], inc="VC")
                        vc_e[tt] = P.n("VC")
                        P.emit("gpsimd", lambda g, tt=tt, slot=slot, hf=hf: g.dma_start(
                            out=ffn_scr[hf * 1024 + tt * 128: hf * 1024 + (tt + 1) * 128, :],
                            in_=ffn_stage[:, slot * 1024:(slot + 1) * 1024]),
                            waits=[("VC", vc_e[tt])], inc="SP", inc_n=16)
                        sp_e[tt] = P.n("SP")
                    vc_gate = vc_e[7]
                    sc_gate = P.n("SC")
                while WV["q"]:
                    WV["q"].pop(0)()

            P.flush(block)

        with (
            nc.sbuf_tensor("xtile", [128, 2 * 1024], F32) as xtile,
            nc.sbuf_tensor("ffntile", [128, 2 * 1024], F32) as ffntile,
            nc.sbuf_tensor("avtile", [128, 2 * 1024], F32) as avtile,
            nc.sbuf_tensor("x2t", [128, 1024], F32) as x2t,
            nc.sbuf_tensor("outstage", [128, 2 * 1024], F32) as outstage,
            nc.sbuf_tensor("cols", [128, 8], F32) as cols,
            nc.sbuf_tensor("junk", [128, 1024], F32) as junk,
        ):
            def filler(v):
                return v.tensor_copy(junk[:, 0:512], junk[:, 512:1024])
            full_barrier()
            P.emit("vector", lambda v: v.memset(junk[:, :], 0.0), inc="VC")

            vc_out = {}
            st_out = {}
            for ot in range(16):
                slot = ot % 2
                hl = ot // 4
                d0 = 16 * (ot % 4)
                w = [("VC", vc_out.get(ot - 2, 0))]
                P.emit("sync", lambda sy, ot=ot, slot=slot: sy.dma_start(
                    out=xtile[:, slot * 1024:(slot + 1) * 1024],
                    in_=xtm_d[ot * 128:(ot + 1) * 128, :]), waits=w,
                    inc=f"LP4X{slot}", inc_n=16)
                P.emit("sync", lambda sy, ot=ot, slot=slot: sy.dma_start(
                    out=ffntile[:, slot * 1024:(slot + 1) * 1024],
                    in_=ffn_scr[ot * 128:(ot + 1) * 128, :]), inc=f"LP4F{slot}", inc_n=16)
                P.emit("sync", lambda sy, slot=slot, hl=hl, d0=d0: sy.dma_start(
                    out=avtile[:, slot * 1024:(slot + 1) * 1024],
                    in_=av_scr[hl, d0:d0 + 16, :].rearrange("i (j c) -> (i j) c", c=1024)),
                    inc=f"LP4A{slot}", inc_n=16)
                ld_t = [(f"LP4X{slot}", P.n(f"LP4X{slot}")),
                        (f"LP4F{slot}", P.n(f"LP4F{slot}")),
                        (f"LP4A{slot}", P.n(f"LP4A{slot}"))]
                P.emit("vector", lambda v, slot=slot: v.reduce_sum(
                    cols[:, 0:1], xtile[:, slot * 1024:(slot + 1) * 1024],
                    axis=mybir.AxisListType.X), waits=list(ld_t), inc="VC")
                vc_s1 = P.n("VC")
                P.emit("scalar", lambda sc, slot=slot: sc.activation(
                    x2t[:, :], xtile[:, slot * 1024:(slot + 1) * 1024], AF.Square),
                    waits=list(ld_t) + [("VC", vc_s1)], inc="SC")
                sc_sq = P.n("SC")
                def mk_stats4(v, slot=slot):
                    v.reduce_sum(cols[:, 1:2], x2t[:, :], axis=mybir.AxisListType.X)
                    v.tensor_scalar_mul(cols[:, 2:3], cols[:, 0:1], 1.0 / 1024)
                    v.tensor_scalar_mul(cols[:, 3:4], cols[:, 1:2], 1.0 / 1024)
                    filler(v)
                    v.tensor_mul(cols[:, 4:5], cols[:, 2:3], cols[:, 2:3])
                    filler(v)
                    v.tensor_sub(cols[:, 3:4], cols[:, 3:4], cols[:, 4:5])
                    filler(v)
                    return v.tensor_scalar_add(cols[:, 3:4], cols[:, 3:4], 1e-5)
                P.emit("vector", mk_stats4, waits=[("SC", sc_sq)], inc="VC")
                vc_v = P.n("VC")
                P.emit("scalar", lambda sc: sc.activation(
                    cols[:, 4:5], cols[:, 3:4], AF.Sqrt), waits=[("VC", vc_v)], inc="SC")
                sc_sd = P.n("SC")
                def mk_r4(v):
                    v.reciprocal(cols[:, 5:6], cols[:, 4:5])
                    filler(v)
                    return v.scalar_tensor_tensor(cols[:, 6:7], cols[:, 2:3], -1.0,
                                                  cols[:, 5:6], ALU.mult, ALU.mult)
                P.emit("vector", mk_r4, waits=[("SC", sc_sd)], inc="VC")
                vc_r = P.n("VC")
                P.emit("scalar", lambda sc, slot=slot: sc.activation(
                    x2t[:, :], xtile[:, slot * 1024:(slot + 1) * 1024], AF.Identity,
                    bias=cols[:, 6:7], scale=cols[:, 5:6]), waits=[("VC", vc_r)], inc="SC")
                sc_xn = P.n("SC")
                def mk_out(v, slot=slot):
                    v.tensor_add(outstage[:, slot * 1024:(slot + 1) * 1024],
                                 xtile[:, slot * 1024:(slot + 1) * 1024], x2t[:, :])
                    v.tensor_add(outstage[:, slot * 1024:(slot + 1) * 1024],
                                 outstage[:, slot * 1024:(slot + 1) * 1024],
                                 avtile[:, slot * 1024:(slot + 1) * 1024])
                    return v.tensor_add(outstage[:, slot * 1024:(slot + 1) * 1024],
                                        outstage[:, slot * 1024:(slot + 1) * 1024],
                                        ffntile[:, slot * 1024:(slot + 1) * 1024])
                P.emit("vector", mk_out,
                       waits=[("SC", sc_xn), ("ST", st_out.get(ot - 2, 0))], inc="VC")
                vc_out[ot] = P.n("VC")
                P.emit("gpsimd", lambda g, ot=ot, slot=slot: g.dma_start(
                    out=out_d[ot * 128:(ot + 1) * 128, :],
                    in_=outstage[:, slot * 1024:(slot + 1) * 1024]),
                    waits=[("VC", vc_out[ot])], inc="ST", inc_n=16)
                st_out[ot] = P.n("ST")
            P.emit("gpsimd", lambda g: g.nop(), waits=[("ST", P.n("ST"))])
            P.flush(block)

    return nc


def _prep_inputs(inputs):
    x = np.ascontiguousarray(inputs["L0_attn_queries"])
    keys = np.ascontiguousarray(inputs["L0_attn_keys"][0])
    vals = np.ascontiguousarray(inputs["L0_attn_values"][0])

    def strips(aT, nstrip):
        return np.ascontiguousarray(
            aT.reshape(8, 128, nstrip, 512).transpose(2, 1, 0, 3))

    kft = strips(keys.T, KSTRIP)
    vft = strips(vals.T, KSTRIP)

    def wtile(wT, KD, MD):
        return np.ascontiguousarray(
            wT.reshape(KD // 128, 128, MD // 128, 128).transpose(2, 1, 0, 3)
        ).reshape(MD // 128, 128, KD)

    w1p = wtile(inputs["L0_ffn_w1_w"].T, D, DHID)
    w2p = wtile(inputs["L0_ffn_w2_w"].T, D, DHID)
    h1p = wtile(inputs["L0_ffn_h1_w"].T, DHID, DHID)
    h2p = wtile(inputs["L0_ffn_h2_w"].T, DHID, DHID)
    w3p = wtile(inputs["L0_ffn_w3_w"].T, DHID, D)
    scol = lambda s, n: np.ascontiguousarray(np.asarray(s).reshape(n, 128).T)
    s1c = scol(inputs["L0_ffn_w1_s"], 32)
    s2c = scol(inputs["L0_ffn_w2_s"], 32)
    sh1c = scol(inputs["L0_ffn_h1_s"], 32)
    sh2c = scol(inputs["L0_ffn_h2_s"], 32)
    s3c = scol(inputs["L0_ffn_w3_s"], 8)

    WqT = np.ascontiguousarray(inputs["L0_attn_Wq_w"].T)
    WkT = np.ascontiguousarray(inputs["L0_attn_Wk_w"].T)
    WvT = np.ascontiguousarray(inputs["L0_attn_Wv_w"].T)

    in_maps = []
    for c in range(8):
        b, g = c // 4, c % 4
        xT = x[b].T
        xfm = strips(xT, NSTRIP)
        wslice = lambda WT: np.ascontiguousarray(
            WT[:, 256 * g:256 * (g + 1)].reshape(8, 128, 256).transpose(1, 0, 2)
        ).reshape(128, 2048)
        bias_col = lambda bvec: np.ascontiguousarray(
            np.asarray(bvec)[256 * g:256 * (g + 1)].reshape(2, 128).T)
        in_maps.append(dict(
            xfm=np.ascontiguousarray(xfm),
            xofm=np.ascontiguousarray(xfm[4 * g: 4 * g + 4]),
            xtm=np.ascontiguousarray(x[b, TOK * g: TOK * (g + 1)]),
            kft=kft, vft=vft,
            wq=wslice(WqT), wk=wslice(WkT), wv=wslice(WvT),
            qb=bias_col(inputs["L0_attn_Wq_b"]),
            kb=bias_col(inputs["L0_attn_Wk_b"]),
            vb=np.ascontiguousarray(
                np.asarray(inputs["L0_attn_Wv_b"])[256 * g:256 * (g + 1)].reshape(1, 256)),
            w1p=w1p, w2p=w2p, h1p=h1p, h2p=h2p, w3p=w3p,
            s1c=s1c, s2c=s2c, sh1c=sh1c, sh2c=sh2c, s3c=s3c,
        ))
    return in_maps


def kernel(**inputs):
    in_maps = _prep_inputs(inputs)
    nc = build_nc(taps=False)
    res = run_bass_kernel_spmd(nc, in_maps, core_ids=list(range(8)))
    out = np.empty((2, SQ, D), dtype=np.float32)
    for c in range(8):
        b, g = c // 4, c % 4
        out[b, TOK * g: TOK * (g + 1)] = res.results[c]["out"]
    return out


# version 07
# speedup vs baseline: 1.0779x; 1.0779x over previous
import numpy as np
import concourse.bass as bass
import concourse.mybir as mybir
from concourse.bass_utils import run_bass_kernel_spmd

F32 = mybir.dt.float32
BF16 = mybir.dt.bfloat16
AF = mybir.ActivationFunctionType
ALU = mybir.AluOpType

D = 1024
SQ = 8192
SK = 4096
TOK = 2048
NSTRIP = 16
KSTRIP = 8
DHID = 4096

ENGINES = ("sync", "gpsimd", "tensor", "scalar", "vector")
SEMS = ("LD", "WL", "SP", "ST", "MM", "SC", "VC",
        "LWQ", "LWK", "LWV", "LX0", "LX1", "LXN", "LW10", "LW11", "LW20", "LW21",
        "LHB0", "LHB1", "LQB", "LKB", "LVB", "SPE0", "SPE1",
        "LP4X0", "LP4X1", "LP4F0", "LP4F1", "LP4A0", "LP4A1")


class Plan:

    def __init__(self, sems):
        self.sems = sems
        self.ev = {e: [] for e in ENGINES}
        self.cnt = {k: 0 for k in SEMS}

    def n(self, sem):
        return self.cnt[sem]

    def emit(self, eng, fn, waits=(), inc=None, inc_n=1):
        ws = tuple((s, v) for (s, v) in waits if v > 0)
        self.ev[eng].append((fn, ws, inc, inc_n))
        if inc is not None:
            self.cnt[inc] += inc_n

    def barrier_waits(self):
        return {s: self.cnt[s] for s in SEMS}

    def flush(self, block):
        for e in ENGINES:
            evs = self.ev[e]
            if not evs:
                continue
            self.ev[e] = []

            def body(eng, evs=evs):
                last_wait = {}
                for fn, waits, inc, inc_n in evs:
                    for s, v in waits:
                        if last_wait.get(s, -1) < v:
                            eng.wait_ge(self.sems[s], v)
                            last_wait[s] = v
                    r = fn(eng)
                    if inc is not None:
                        r.then_inc(self.sems[inc], inc_n)
            getattr(block, e)(body)


def build_nc(taps=False):
    nc = bass.Bass("TRN2", target_bir_lowering=False)

    xfm_d = nc.dram_tensor("xfm", [NSTRIP, 128, 8, 512], F32, kind="ExternalInput").ap()
    xtm_d = nc.dram_tensor("xtm", [TOK, D], F32, kind="ExternalInput").ap()
    xofm_d = nc.dram_tensor("xofm", [4, 128, 8, 512], F32, kind="ExternalInput").ap()
    kft_d = nc.dram_tensor("kft", [KSTRIP, 128, 8, 512], F32, kind="ExternalInput").ap()
    vft_d = nc.dram_tensor("vft", [KSTRIP, 128, 8, 512], F32, kind="ExternalInput").ap()
    wq_d = nc.dram_tensor("wq", [128, 2048], F32, kind="ExternalInput").ap()
    wk_d = nc.dram_tensor("wk", [128, 2048], F32, kind="ExternalInput").ap()
    wv_d = nc.dram_tensor("wv", [128, 2048], F32, kind="ExternalInput").ap()
    qb_d = nc.dram_tensor("qb", [128, 2], F32, kind="ExternalInput").ap()
    kb_d = nc.dram_tensor("kb", [128, 2], F32, kind="ExternalInput").ap()
    vb_d = nc.dram_tensor("vb", [1, 256], F32, kind="ExternalInput").ap()
    w1_d = nc.dram_tensor("w1p", [32, 128, 1024], F32, kind="ExternalInput").ap()
    w2_d = nc.dram_tensor("w2p", [32, 128, 1024], F32, kind="ExternalInput").ap()
    h1_d = nc.dram_tensor("h1p", [32, 128, 4096], F32, kind="ExternalInput").ap()
    h2_d = nc.dram_tensor("h2p", [32, 128, 4096], F32, kind="ExternalInput").ap()
    w3_d = nc.dram_tensor("w3p", [8, 128, 4096], F32, kind="ExternalInput").ap()
    s1_d = nc.dram_tensor("s1c", [128, 32], F32, kind="ExternalInput").ap()
    s2_d = nc.dram_tensor("s2c", [128, 32], F32, kind="ExternalInput").ap()
    sh1_d = nc.dram_tensor("sh1c", [128, 32], F32, kind="ExternalInput").ap()
    sh2_d = nc.dram_tensor("sh2c", [128, 32], F32, kind="ExternalInput").ap()
    s3_d = nc.dram_tensor("s3c", [128, 8], F32, kind="ExternalInput").ap()

    out_d = nc.dram_tensor("out", [TOK, D], F32, kind="ExternalOutput").ap()
    scratch_kind = "ExternalOutput" if taps else "Internal"
    av_scr = nc.dram_tensor("av_scr", [4, 64, SQ], F32, kind=scratch_kind).ap()
    stat_scr = nc.dram_tensor("stat_scr", [2, TOK], F32, kind="Internal").ap()
    q_scr = nc.dram_tensor("q_scr", [2, 128, SQ], BF16, kind="Internal").ap()
    k_scr = nc.dram_tensor("k_scr", [2, 128, SK], BF16, kind="Internal").ap()
    v_scr = nc.dram_tensor("v_scr", [2, 128, 4160], BF16, kind="Internal").ap()
    xn_scr = nc.dram_tensor("xn_scr", [128, 8 * TOK], BF16, kind="Internal").ap()
    ffn_scr = nc.dram_tensor("ffn_scr", [TOK, D], F32, kind=scratch_kind).ap()
    if taps:
        qtap_d = nc.dram_tensor("q_tap", [128, 2 * SQ], BF16, kind="ExternalOutput").ap()
        xntap_d = nc.dram_tensor("xn_tap", [128, 8 * TOK], BF16, kind="ExternalOutput").ap()
        ktap_d = nc.dram_tensor("k_tap", [128, 2 * SK], BF16, kind="ExternalOutput").ap()
        vtap_d = nc.dram_tensor("v_tap", [128, 4 * 32 * 65], BF16, kind="ExternalOutput").ap()
        mtap_d = nc.dram_tensor("m_tap", [1, 2 * TOK], F32, kind="ExternalOutput").ap()
        gtap_d = nc.dram_tensor("g_tap", [128, 32 * 1024], BF16, kind="ExternalOutput").ap()
        hmtap_d = nc.dram_tensor("hm_tap", [128, 32 * 1024], BF16, kind="ExternalOutput").ap()
        g2tap_d = nc.dram_tensor("g2_tap", [128, 32 * 1024], BF16, kind="ExternalOutput").ap()
        ctap_d = nc.dram_tensor("c_tap", [128, 48], F32, kind="ExternalOutput").ap()

    from contextlib import ExitStack
    with ExitStack() as ctx:
        sems = {s: ctx.enter_context(nc.semaphore(s)) for s in SEMS}
        P = Plan(sems)

        ident = ctx.enter_context(nc.sbuf_tensor("ident", [128, 128], F32))
        ones_col = ctx.enter_context(nc.sbuf_tensor("ones_col", [128, 1], BF16))
        ones_row = ctx.enter_context(nc.sbuf_tensor("ones_row", [1, 128], F32))
        eps1 = ctx.enter_context(nc.sbuf_tensor("eps1", [1, 1], F32))
        qb_sb = ctx.enter_context(nc.sbuf_tensor("qb_sb", [128, 2], F32))
        kb_sb = ctx.enter_context(nc.sbuf_tensor("kb_sb", [128, 2], F32))
        vb_sb = ctx.enter_context(nc.sbuf_tensor("vb_sb", [1, 256], F32))
        s1_sb = ctx.enter_context(nc.sbuf_tensor("s1_sb", [128, 32], F32))
        s2_sb = ctx.enter_context(nc.sbuf_tensor("s2_sb", [128, 32], F32))
        sh1_sb = ctx.enter_context(nc.sbuf_tensor("sh1_sb", [128, 32], F32))
        sh2_sb = ctx.enter_context(nc.sbuf_tensor("sh2_sb", [128, 32], F32))
        s3_sb = ctx.enter_context(nc.sbuf_tensor("s3_sb", [128, 8], F32))
        ones2 = ctx.enter_context(nc.sbuf_tensor("ones2", [128, 128], F32))

        block = ctx.enter_context(nc.Block())

        def full_barrier():
            w = P.barrier_waits()
            waits = tuple(w.items())
            for e in ENGINES:
                P.emit(e, lambda eng: eng.nop(), waits=waits)

        P.emit("sync", lambda sy: sy.dma_start(out=qb_sb[:, :], in_=qb_d), inc="LD", inc_n=16)
        P.emit("sync", lambda sy: sy.dma_start(out=kb_sb[:, :], in_=kb_d), inc="LD", inc_n=16)
        P.emit("sync", lambda sy: sy.dma_start(out=vb_sb[:, :], in_=vb_d), inc="LD", inc_n=16)
        P.emit("sync", lambda sy: sy.dma_start(out=s1_sb[:, :], in_=s1_d), inc="LD", inc_n=16)
        P.emit("sync", lambda sy: sy.dma_start(out=s2_sb[:, :], in_=s2_d), inc="LD", inc_n=16)
        P.emit("sync", lambda sy: sy.dma_start(out=sh1_sb[:, :], in_=sh1_d), inc="LD", inc_n=16)
        P.emit("sync", lambda sy: sy.dma_start(out=sh2_sb[:, :], in_=sh2_d), inc="LD", inc_n=16)
        P.emit("sync", lambda sy: sy.dma_start(out=s3_sb[:, :], in_=s3_d), inc="LD", inc_n=16)
        P.emit("vector", lambda v: v.memset(ones_col[:, :], 1.0), inc="VC")
        P.emit("vector", lambda v: v.memset(ones_row[:, :], 1.0), inc="VC")
        P.emit("vector", lambda v: v.memset(eps1[:, :], 1e-5), inc="VC")
        P.emit("vector", lambda v: v.memset(ones2[:, :], 1.0), inc="VC")
        P.emit("gpsimd", lambda g: g.memset(ident[:, :], 0.0), inc="WL")
        P.emit("gpsimd", lambda g: g.affine_select(
            out=ident[:, :], in_=ident[:, :], compare_op=ALU.not_equal,
            fill=1.0, base=0, pattern=[[-1, 128]], channel_multiplier=1), inc="WL")
        ld_bias = P.n("LD")
        vc_init = P.n("VC")

        with (
            nc.sbuf_tensor("q_fm", [128, 2 * SQ], BF16) as q_fm,
            nc.sbuf_tensor("k_fm", [128, 2 * SK], BF16) as k_fm,
            nc.sbuf_tensor("v_tm", [128, 4 * 32 * 65], BF16) as v_tm,
            nc.sbuf_tensor("xn_own", [128, 8 * TOK], BF16) as xn_own,
        ):
            with (
                nc.sbuf_tensor("xbuf", [128, 2 * 8192], BF16) as xbuf,
                nc.sbuf_tensor("x2buf", [128, 8192], BF16) as x2buf,
                nc.sbuf_tensor("wq_sb", [128, 2048], BF16) as wq_sb,
                nc.sbuf_tensor("wk_sb", [128, 2048], BF16) as wk_sb,
                nc.sbuf_tensor("wv_sb", [128, 2048], BF16) as wv_sb,
                nc.sbuf_tensor("rows", [128, 6144], F32) as rows,
                nc.sbuf_tensor("rows2", [128, 2 * TOK], F32) as rows2,
                nc.sbuf_tensor("rbc", [128, 2 * 512], BF16) as rbc,
                nc.sbuf_tensor("mrbc", [128, 2 * 512], BF16) as mrbc,
                nc.sbuf_tensor("tmpb", [128, 512], BF16) as tmpb,
                nc.psum_tensor("psQ", [128, 1024], F32) as psQ,
            ):
                m_own = rows2[0:1, 0:TOK]
                rr_own = rows2[0:1, TOK:2 * TOK]
                P.emit("gpsimd", lambda g: g.dma_start(out=wq_sb[:, :], in_=wq_d), inc="LWQ", inc_n=16)
                P.emit("gpsimd", lambda g: g.dma_start(out=wk_sb[:, :], in_=wk_d), inc="LWK", inc_n=16)
                P.emit("gpsimd", lambda g: g.dma_start(out=wv_sb[:, :], in_=wv_d), inc="LWV", inc_n=16)

                mm_last_reader = {}
                chainst = {}
                vc_evacS = {}
                mm_sums = {}
                sc_qevac = {}
                strip_ev = {}

                with (
                    nc.psum_tensor("psS", [1, 2048], F32) as psS,
                    nc.psum_tensor("psB", [128, 1024], F32) as psB,
                ):
                    def emit_chain(c):
                        own = c >= 8
                        slot = c % 2
                        base = slot * 8192
                        srcs = []
                        for st in range(2):
                            if own:
                                srcs.append(xofm_d[2 * (c - 8) + st, :, :, :].rearrange(
                                    "p k c -> p (k c)"))
                            else:
                                srcs.append(xfm_d[2 * c + st, :, :, :].rearrange(
                                    "p k c -> p (k c)"))
                        lx = f"LX{slot}"
                        P.emit("gpsimd", lambda g, src=srcs[0], base=base: g.dma_start(
                            out=xbuf[:, base:base + 4096], in_=src),
                            waits=[("MM", mm_last_reader.get(c - 2, 0))], inc=lx, inc_n=16)
                        P.emit("gpsimd", lambda g, src=srcs[1], base=base: g.dma_start(
                            out=xbuf[:, base + 4096:base + 8192], in_=src),
                            inc=lx, inc_n=16)
                        wl_c = P.n(lx)
                        P.emit("scalar", lambda sc, base=base: sc.activation(
                            x2buf[:, :], xbuf[:, base:base + 8192], AF.Square),
                            waits=[(lx, wl_c), ("MM", mm_sums.get(c - 1, 0))], inc="SC")
                        sc_x2 = P.n("SC")

                        def mk_sums(t, base=base):
                            for st in range(2):
                                for ch in range(8):
                                    t.matmul(psS[:, st * 512:(st + 1) * 512], ones_col[:, :],
                                             xbuf[:, base + st * 4096 + ch * 512:
                                                  base + st * 4096 + (ch + 1) * 512],
                                             start=(ch == 0), stop=(ch == 7))
                            for st in range(2):
                                for ch in range(8):
                                    r = t.matmul(psS[:, 1024 + st * 512: 1024 + (st + 1) * 512],
                                                 ones_col[:, :],
                                                 x2buf[:, st * 4096 + ch * 512:
                                                       st * 4096 + (ch + 1) * 512],
                                                 start=(ch == 0), stop=(ch == 7))
                            return r
                        P.emit("tensor", mk_sums,
                               waits=[(lx, wl_c), ("SC", sc_x2), ("VC", vc_evacS.get(c - 1, 0))],
                               inc="MM")
                        mm_sums[c] = P.n("MM")
                        P.emit("vector", lambda v, slot=slot: v.tensor_copy(
                            rows[0:1, slot * 2048:(slot + 1) * 2048], psS[:, :]),
                            waits=[("MM", mm_sums[c])], inc="VC")
                        vc_evacS[c] = P.n("VC")

                        def mk_stats(v, slot=slot, own=own, c=c):
                            sb_ = slot * 2048
                            so_ = slot * 1024
                            mdst = m_own[:, (c - 8) * 1024:(c - 7) * 1024] if own \
                                else rows[64:65, so_:so_ + 1024]
                            msq = rows[32:33, so_:so_ + 1024]
                            ex2 = rows[32:33, 2048 + so_:2048 + so_ + 1024]
                            var = rows[32:33, 4096 + so_:4096 + so_ + 1024]
                            v.tensor_scalar_mul(mdst, rows[0:1, sb_:sb_ + 1024], 1.0 / 1024)
                            v.tensor_scalar_mul(ex2, rows[0:1, sb_ + 1024:sb_ + 2048], 1.0 / 1024)
                            v.tensor_mul(msq, mdst, mdst)
                            v.tensor_sub(var, ex2, msq)
                            return v.tensor_scalar_add(var, var, 1e-5)
                        P.emit("vector", mk_stats, waits=[("VC", vc_init)], inc="VC")
                        vc_stats = P.n("VC")
                        P.emit("scalar", lambda sc, slot=slot: sc.activation(
                            rows[64:65, 2048 + slot * 1024: 2048 + (slot + 1) * 1024],
                            rows[32:33, 4096 + slot * 1024: 4096 + (slot + 1) * 1024], AF.Sqrt),
                            waits=[("VC", vc_stats)], inc="SC")
                        sc_sd = P.n("SC")

                        def mk_rr(v, slot=slot, own=own, c=c):
                            so_ = slot * 1024
                            rdst = rr_own[:, (c - 8) * 1024:(c - 7) * 1024] if own \
                                else rows[64:65, 4096 + so_:4096 + so_ + 1024]
                            mdst = m_own[:, (c - 8) * 1024:(c - 7) * 1024] if own \
                                else rows[64:65, so_:so_ + 1024]
                            v.reciprocal(rdst, rows[64:65, 2048 + so_:2048 + so_ + 1024])
                            return v.tensor_mul(rows[0:1, 4096 + so_:4096 + so_ + 1024],
                                                mdst, rdst)
                        P.emit("vector", mk_rr, waits=[("SC", sc_sd)], inc="VC")
                        chainst[c] = (slot, base, P.n("VC"))

                    def emit_chainB(c):
                        own = c >= 8
                        slot, base, vc_rr = chainst[c]
                        mm_last = 0
                        for st in range(2):
                            sg = (c - 8) * 2 + st if own else 2 * c + st
                            s2_ = (2 * c + st) % 2

                            def mk_bcast(t, slot=slot, st=st, own=own, c=c):
                                so_ = slot * 1024
                                if own:
                                    rsrc = rr_own[:, (c - 8) * 1024 + st * 512:
                                                  (c - 8) * 1024 + (st + 1) * 512]
                                    lh = ones_row[:, :]
                                else:
                                    rsrc = rows[64:65, 4096 + so_ + st * 512:
                                                4096 + so_ + (st + 1) * 512]
                                    lh = ones2[64:65, 0:128]
                                t.matmul(psB[:, 0:512], lh, rsrc, start=True, stop=True)
                                return t.matmul(psB[:, 512:1024], ones_row[:, :],
                                                rows[0:1, 4096 + so_ + st * 512:
                                                     4096 + so_ + (st + 1) * 512],
                                                start=True, stop=True)
                            P.emit("tensor", mk_bcast,
                                   waits=[("VC", vc_rr), ("VC", strip_ev.get(("evB", c, st - 1),
                                          strip_ev.get(("evB", c - 1, 1), 0)))], inc="MM")
                            mm_bc = P.n("MM")
                            P.emit("vector", lambda v, s2_=s2_: (
                                v.tensor_copy(rbc[:, s2_ * 512:(s2_ + 1) * 512], psB[:, 0:512]),
                                v.tensor_copy(mrbc[:, s2_ * 512:(s2_ + 1) * 512],
                                              psB[:, 512:1024]))[-1],
                                waits=[("MM", mm_bc)], inc="VC")
                            strip_ev[("evB", c, st)] = P.n("VC")

                            def mk_xn(v, base=base, st=st, s2_=s2_, own=own, c=c):
                                for ch in range(8):
                                    xsl = xbuf[:, base + st * 4096 + ch * 512:
                                               base + st * 4096 + (ch + 1) * 512]
                                    v.tensor_mul(tmpb[:, :], xsl, rbc[:, s2_ * 512:(s2_ + 1) * 512])
                                    if own:
                                        tpos = (c - 8) * 1024 + st * 512
                                        dst = xn_own[:, ch * TOK + tpos: ch * TOK + tpos + 512]
                                    else:
                                        dst = xsl
                                    r = v.tensor_sub(dst, tmpb[:, :],
                                                     mrbc[:, s2_ * 512:(s2_ + 1) * 512])
                                return r
                            P.emit("vector", mk_xn, waits=[("VC", strip_ev[("evB", c, st)])],
                                   inc="VC")
                            vc_xn = P.n("VC")
                            strip_ev[("xn", c, st)] = vc_xn
                            if not own:
                                def mk_q(t, base=base, st=st):
                                    for mc in range(2):
                                        for kc in range(8):
                                            r = t.matmul(
                                                psQ[:, mc * 512:(mc + 1) * 512],
                                                wq_sb[:, kc * 256 + mc * 128: kc * 256 + (mc + 1) * 128],
                                                xbuf[:, base + st * 4096 + kc * 512:
                                                     base + st * 4096 + (kc + 1) * 512],
                                                start=(kc == 0), stop=(kc == 7))
                                    return r
                                P.emit("tensor", mk_q,
                                       waits=[("VC", vc_xn), ("LWQ", 16),
                                              ("SC", sc_qevac.get(sg - 1, 0))], inc="MM")
                                mm_last = P.n("MM")

                                def mk_qevac(sc, sg=sg):
                                    for mc in range(2):
                                        r = sc.activation(
                                            q_fm[:, mc * SQ + sg * 512: mc * SQ + (sg + 1) * 512],
                                            psQ[:, mc * 512:(mc + 1) * 512], AF.Relu,
                                            bias=qb_sb[:, mc:mc + 1])
                                    return r
                                P.emit("scalar", mk_qevac,
                                       waits=[("MM", mm_last), ("LD", ld_bias)], inc="SC")
                                sc_qevac[sg] = P.n("SC")
                        mm_last_reader[c] = P.n("MM") if own else mm_last
                        if own:
                            mm_last_reader[c] = mm_sums[c]
                            strip_ev[("ownxn", c)] = P.n("VC")


                    for c in range(10):
                        emit_chain(c)
                        if c >= 1:
                            emit_chainB(c - 1)
                    emit_chainB(9)

                    own_xn_vc = P.n("VC")
                    sc_q_all = P.n("SC")

                sc_kevac = {}
                mm_kread = {}
                with nc.psum_tensor("psV", [128, 256], F32) as psV:
                    for c in range(4):
                        slot = c % 2
                        base = slot * 8192
                        w = [("MM", mm_kread.get(c - 2, 0))]
                        if c < 2:
                            w = [("VC", own_xn_vc), ("MM", mm_sums[8 + c])]
                        lx = f"LX{slot}"
                        P.emit("gpsimd", lambda g, c=c, base=base: g.dma_start(
                            out=xbuf[:, base:base + 4096],
                            in_=kft_d[2 * c, :, :, :].rearrange("p k c -> p (k c)")),
                            waits=w, inc=lx, inc_n=16)
                        P.emit("gpsimd", lambda g, c=c, base=base: g.dma_start(
                            out=xbuf[:, base + 4096:base + 8192],
                            in_=kft_d[2 * c + 1, :, :, :].rearrange("p k c -> p (k c)")),
                            inc=lx, inc_n=16)
                        wl_c = P.n(lx)
                        for st in range(2):
                            sg = 2 * c + st

                            def mk_k(t, base=base, st=st):
                                for mc in range(2):
                                    for kc in range(8):
                                        r = t.matmul(
                                            psQ[:, mc * 512:(mc + 1) * 512],
                                            wk_sb[:, kc * 256 + mc * 128: kc * 256 + (mc + 1) * 128],
                                            xbuf[:, base + st * 4096 + kc * 512:
                                                 base + st * 4096 + (kc + 1) * 512],
                                            start=(kc == 0), stop=(kc == 7))
                                return r
                            P.emit("tensor", mk_k,
                                   waits=[(lx, wl_c), ("LWK", 16),
                                          ("SC", sc_kevac.get(sg - 1, sc_q_all))], inc="MM")
                            mm_k = P.n("MM")

                            def mk_kevac(sc, sg=sg):
                                for mc in range(2):
                                    r = sc.activation(
                                        k_fm[:, mc * SK + sg * 512: mc * SK + (sg + 1) * 512],
                                        psQ[:, mc * 512:(mc + 1) * 512], AF.Relu,
                                        bias=kb_sb[:, mc:mc + 1])
                                return r
                            P.emit("scalar", mk_kevac, waits=[("MM", mm_k)], inc="SC")
                            sc_kevac[sg] = P.n("SC")
                        mm_kread[c] = P.n("MM")

                    vc_vevac = {}
                    mm_vread = {}
                    for c in range(4):
                        slot = c % 2
                        base = slot * 8192
                        w = [("MM", mm_vread.get(c - 2, mm_kread.get(2 + c, 0)))]
                        lx = f"LX{slot}"
                        P.emit("gpsimd", lambda g, c=c, base=base: g.dma_start(
                            out=xbuf[:, base:base + 4096],
                            in_=vft_d[2 * c, :, :, :].rearrange("p k c -> p (k c)")),
                            waits=w, inc=lx, inc_n=16)
                        P.emit("gpsimd", lambda g, c=c, base=base: g.dma_start(
                            out=xbuf[:, base + 4096:base + 8192],
                            in_=vft_d[2 * c + 1, :, :, :].rearrange("p k c -> p (k c)")),
                            inc=lx, inc_n=16)
                        wl_c = P.n(lx)
                        for tsub in range(8):
                            kt = c * 8 + tsub
                            off = base + (tsub // 4) * 4096 + (tsub % 4 * 0)

                            def mk_v(t, base=base, tsub=tsub):
                                stp = tsub // 4
                                sub = tsub % 4
                                for kc in range(8):
                                    t.matmul(psV[:, :],
                                             xbuf[:, base + stp * 4096 + kc * 512 + sub * 128:
                                                  base + stp * 4096 + kc * 512 + (sub + 1) * 128],
                                             wv_sb[:, kc * 256:(kc + 1) * 256],
                                             start=(kc == 0), stop=False)
                                return t.matmul(psV[:, :], ones_row[:, :], vb_sb[:, :],
                                                start=False, stop=True)
                            P.emit("tensor", mk_v,
                                   waits=[(lx, wl_c), ("LWV", 16), ("LD", ld_bias),
                                          ("VC", vc_vevac.get(kt - 1, 0))], inc="MM")
                            mm_v = P.n("MM")

                            def mk_vevac(v, kt=kt):
                                for hl in range(4):
                                    r = v.tensor_relu(
                                        v_tm[:, hl * 2080 + kt * 65: hl * 2080 + kt * 65 + 64],
                                        psV[:, hl * 64:(hl + 1) * 64])
                                return r
                            P.emit("vector", mk_vevac, waits=[("MM", mm_v)], inc="VC")
                            vc_vevac[kt] = P.n("VC")
                        mm_vread[c] = P.n("MM")

                    def mk_vones(v):
                        for hl in range(4):
                            r = v.memset(v_tm[:, hl * 2080 + 64: (hl + 1) * 2080: 65], 1.0)
                        return r
                    P.emit("vector", mk_vones, inc="VC")

                P.emit("gpsimd", lambda g: g.dma_start(out=stat_scr[0, :], in_=m_own[:, :]),
                       waits=[("VC", P.n("VC"))], inc="SP", inc_n=16)
                P.emit("gpsimd", lambda g: g.dma_start(out=stat_scr[1, :], in_=rr_own[:, :]),
                       inc="SP", inc_n=16)
                wsp = tuple(P.barrier_waits().items())
                P.emit("gpsimd", lambda g: g.dma_start(out=q_scr[0], in_=q_fm[:, 0:SQ]),
                       waits=wsp, inc="SP", inc_n=16)
                P.emit("gpsimd", lambda g: g.dma_start(out=q_scr[1], in_=q_fm[:, SQ:2 * SQ]),
                       inc="SP", inc_n=16)
                P.emit("gpsimd", lambda g: g.dma_start(out=k_scr[0], in_=k_fm[:, 0:SK]),
                       inc="SP", inc_n=16)
                P.emit("gpsimd", lambda g: g.dma_start(out=k_scr[1], in_=k_fm[:, SK:2 * SK]),
                       inc="SP", inc_n=16)
                P.emit("gpsimd", lambda g: g.dma_start(out=v_scr[0], in_=v_tm[:, 0:4160]),
                       inc="SP", inc_n=16)
                P.emit("gpsimd", lambda g: g.dma_start(out=v_scr[1], in_=v_tm[:, 4160:8320]),
                       inc="SP", inc_n=16)
                P.emit("gpsimd", lambda g: g.dma_start(out=xn_scr, in_=xn_own[:, :]),
                       inc="SP", inc_n=16)
                sp_spill = P.n("SP")

                if taps:
                    w = P.barrier_waits()
                    P.emit("gpsimd", lambda g: g.dma_start(out=qtap_d, in_=q_fm[:, :]),
                           waits=tuple(w.items()), inc="SP", inc_n=16)
                    P.emit("gpsimd", lambda g: g.dma_start(out=xntap_d, in_=xn_own[:, :]),
                           inc="SP", inc_n=16)
                    P.emit("gpsimd", lambda g: g.dma_start(out=ktap_d, in_=k_fm[:, :]),
                           inc="SP", inc_n=16)
                    P.emit("gpsimd", lambda g: g.dma_start(out=vtap_d, in_=v_tm[:, :]),
                           inc="SP", inc_n=16)
                    P.emit("gpsimd", lambda g: g.dma_start(
                        out=mtap_d[:, 0:TOK], in_=m_own[:, :]), inc="SP", inc_n=16)
                    P.emit("gpsimd", lambda g: g.dma_start(
                        out=mtap_d[:, TOK:2 * TOK], in_=rr_own[:, :]), inc="SP", inc_n=16)

                full_barrier()
                P.flush(block)


        GROUPS2 = [[2 * i, 2 * i + 1] for i in range(16)]
        with ExitStack() as mctx:
            g_sb = mctx.enter_context(nc.sbuf_tensor("g_sb", [128, 32 * 1024], BF16))
            hbuf = mctx.enter_context(nc.sbuf_tensor("hbuf", [128, 2 * 4096], BF16))
            qbuf = mctx.enter_context(nc.sbuf_tensor("qbuf", [128, SQ], BF16))
            kbuf = mctx.enter_context(nc.sbuf_tensor("kbuf", [128, SK], BF16))
            vbuf = mctx.enter_context(nc.sbuf_tensor("vbuf", [128, 4160], BF16))
            ring = mctx.enter_context(nc.sbuf_tensor("ring", [128, 4096], BF16))
            av_stage = mctx.enter_context(nc.sbuf_tensor("av_stage", [64, 2 * 512], F32))
            rec_row = mctx.enter_context(nc.sbuf_tensor("rec_row", [1, 1024], F32))
            recb_sb = mctx.enter_context(nc.sbuf_tensor("recb_sb", [64, 1024], F32))
            psF = mctx.enter_context(nc.psum_tensor("psF", [128, 1024], F32))
            psA = mctx.enter_context(nc.psum_tensor("psA", [128, 1024], F32))
            psBq = mctx.enter_context(nc.psum_tensor("psBq", [128, 1024], F32))
            psAvA = mctx.enter_context(nc.psum_tensor("psAvA", [128, 512], F32))
            psAvB = mctx.enter_context(nc.psum_tensor("psAvB", [128, 512], F32))
            sp_ffn = {}
            sp_unit = {}
            sc_exp = {}
            mm_score = {}
            mm_av = {}
            AST = dict(vc_avtt=0, sc_expq=0, ld_pair=0)

            def build_att_events(pi):
                ev = []
                for qt in range(16):
                    u = pi * 16 + qt

                    def e_score(gi, qt=qt, u=u):
                        grp = GROUPS2[gi]

                        def mk(t):
                            for j, kc in enumerate(grp):
                                t.matmul(psA[:, j * 512:(j + 1) * 512],
                                         kbuf[0:64, kc * 128:(kc + 1) * 128],
                                         qbuf[0:64, qt * 512:(qt + 1) * 512],
                                         start=True, stop=True, tile_position=(0, 0))
                                r = t.matmul(psBq[:, j * 512:(j + 1) * 512],
                                             kbuf[64:128, kc * 128:(kc + 1) * 128],
                                             qbuf[64:128, qt * 512:(qt + 1) * 512],
                                             start=True, stop=True, tile_position=(64, 0))
                            return r
                        P.emit("tensor", mk,
                               waits=[("SC", sc_exp.get((u, gi - 1), AST["sc_expq"])),
                                      ("VC", AST["vc_avtt"]), ("LQB", AST["ld_pair"]),
                                      ("LKB", AST["ld_pair"]), ("LVB", AST["ld_pair"])],
                               inc="MM")
                        mm_score[(u, gi)] = P.n("MM")

                        def mkA(sc, gi=gi):
                            return sc.activation(ring[:, (gi % 2) * 1024:(gi % 2) * 1024 + 1024],
                                                 psA[:, :], AF.Exp, scale=0.125)

                        def mkB(sc, gi=gi):
                            return sc.activation(
                                ring[:, 2048 + (gi % 2) * 1024: 2048 + (gi % 2) * 1024 + 1024],
                                psBq[:, :], AF.Exp, scale=0.125)
                        ring_free = mm_av.get((u, gi - 2), 0)
                        P.emit("scalar", mkA,
                               waits=[("MM", max(mm_score[(u, gi)], ring_free))], inc="SC")
                        P.emit("scalar", mkB, inc="SC")
                        sc_exp[(u, gi)] = P.n("SC")

                    def e_av(g, u=u):
                        def mk(t):
                            for j, kc in enumerate(GROUPS2[g]):
                                t.matmul(psAvA[0:65, :],
                                         vbuf[:, kc * 65:(kc + 1) * 65],
                                         ring[:, (g % 2) * 1024 + j * 512:
                                              (g % 2) * 1024 + (j + 1) * 512],
                                         start=(kc == 0), stop=(kc == 31))
                                r = t.matmul(psAvB[0:65, :],
                                             vbuf[:, 2080 + kc * 65: 2080 + (kc + 1) * 65],
                                             ring[:, 2048 + (g % 2) * 1024 + j * 512:
                                                  2048 + (g % 2) * 1024 + (j + 1) * 512],
                                             start=(kc == 0), stop=(kc == 31))
                            return r
                        P.emit("tensor", mk,
                               waits=[("SC", sc_exp[(u, g)]), ("VC", AST["vc_avtt"])],
                               inc="MM")
                        mm_av[(u, g)] = P.n("MM")

                    def e_tail(qt=qt, u=u, pi=pi):
                        def mk_rec(v):
                            v.reciprocal(rec_row[:, 0:512], psAvA[64:65, :])
                            return v.reciprocal(rec_row[:, 512:1024], psAvB[64:65, :])
                        P.emit("vector", mk_rec, waits=[("MM", mm_av[(u, 15)])], inc="VC")
                        vc_rec = P.n("VC")

                        def mk_recb(t):
                            t.matmul(psA[0:64, 0:512], ones_row[:, 0:64], rec_row[:, 0:512],
                                     start=True, stop=True)
                            return t.matmul(psA[0:64, 512:1024], ones_row[:, 0:64],
                                            rec_row[:, 512:1024], start=True, stop=True)
                        P.emit("tensor", mk_recb,
                               waits=[("VC", vc_rec), ("SC", sc_exp[(u, 15)])], inc="MM")
                        mm_recb = P.n("MM")

                        def mk_recb_copy(v):
                            v.tensor_copy(recb_sb[:, 0:512], psA[0:64, 0:512])
                            return v.tensor_copy(recb_sb[:, 512:1024], psA[0:64, 512:1024])
                        P.emit("vector", mk_recb_copy, waits=[("MM", mm_recb)], inc="VC")
                        def mk_avtt(v):
                            v.tensor_mul(av_stage[:, 0:512],
                                         psAvA[0:64, :], recb_sb[:, 0:512])
                            return v.tensor_mul(av_stage[:, 512:1024],
                                                psAvB[0:64, :], recb_sb[:, 512:1024])
                        P.emit("vector", mk_avtt,
                               waits=[("SP", P.n("SP"))], inc="VC")
                        AST["vc_avtt"] = P.n("VC")
                        AST["sc_expq"] = sc_exp[(u, 15)]

                        def mk_spillA(g, qt=qt, pi=pi):
                            return g.dma_start(out=av_scr[2 * pi, :, qt * 512:(qt + 1) * 512],
                                               in_=av_stage[:, 0:512])

                        def mk_spillB(g, qt=qt, pi=pi):
                            return g.dma_start(out=av_scr[2 * pi + 1, :, qt * 512:(qt + 1) * 512],
                                               in_=av_stage[:, 512:1024])
                        P.emit("gpsimd", mk_spillA, waits=[("VC", AST["vc_avtt"])],
                               inc="SP", inc_n=16)
                        P.emit("gpsimd", mk_spillB, inc="SP", inc_n=16)
                        sp_unit[u] = P.n("SP")

                    for gi in range(16):
                        ev.append(lambda gi=gi, f=e_score: f(gi))
                        if gi >= 1:
                            ev.append(lambda g=gi - 1, f=e_av: f(g))
                    ev.append(lambda f=e_av: f(15))
                    ev.append(e_tail)
                return ev

            WV = dict(q=None, credit=0.0, rate=0.0)

            def weave(dur_us):
                WV["credit"] += dur_us * WV["rate"]
                k = int(WV["credit"])
                WV["credit"] -= k
                for _ in range(k):
                    if WV["q"]:
                        WV["q"].pop(0)()

            mm_hb_prev = {}
            sc_gate = 0
            vc_gate = 0
            for hf in range(2):
                wqkv = [("SP", sp_spill)] if hf == 0 else                     [("VC", AST["vc_avtt"]), ("SP", sp_unit.get(16 * hf - 1, 0))]
                P.emit("sync", lambda sy, hf=hf: sy.dma_start(out=qbuf[:, :], in_=q_scr[hf]),
                       waits=wqkv, inc="LQB", inc_n=16)
                P.emit("sync", lambda sy, hf=hf: sy.dma_start(out=kbuf[:, :], in_=k_scr[hf]),
                       inc="LKB", inc_n=16)
                P.emit("sync", lambda sy, hf=hf: sy.dma_start(out=vbuf[:, :], in_=v_scr[hf]),
                       inc="LVB", inc_n=16)
                AST["ld_pair"] = 16 * (hf + 1)
                WV["q"] = build_att_events(hf)
                WV["rate"] = len(WV["q"]) / 1260.0
                WV["credit"] = 0.0

                with ExitStack() as actx:
                    xnh = actx.enter_context(nc.sbuf_tensor(f"xnh_{hf}", [128, 8192], BF16))
                    w1buf = actx.enter_context(nc.sbuf_tensor(f"w1b_{hf}", [128, 2048], BF16))
                    w2buf = actx.enter_context(nc.sbuf_tensor(f"w2b_{hf}", [128, 2048], BF16))
                    stgA1 = actx.enter_context(nc.sbuf_tensor(f"sga_{hf}", [128, 3072], F32))
                    stgA2 = actx.enter_context(nc.sbuf_tensor(f"sgb_{hf}", [128, 3072], F32))
                    t1b = actx.enter_context(nc.sbuf_tensor(f"t1b_{hf}", [128, 2048], F32))
                    galias = [("SP", sp_spill), ("VC", vc_gate),
                              ("SPE0", P.n("SPE0")), ("SPE1", P.n("SPE1")), ("MM", P.n("MM"))]
                    P.emit("gpsimd", lambda g, hf=hf: g.dma_start(
                        out=xnh[:, :],
                        in_=xn_scr.rearrange("p (ch t) -> p ch t", t=TOK)[
                            :, :, hf * 1024:(hf + 1) * 1024]),
                        waits=galias, inc="LXN", inc_n=16)
                    wl_xnh = P.n("LXN")
                    mm_a = {}
                    vc_c1 = {}
                    vc_c2 = {}
                    sc_silu = {}
                    vc_stt = {}

                    def emit_a_tail(m):
                        if m < 0 or m not in vc_c1:
                            return
                        s4 = (m % 3) * 1024
                        s2_ = (m % 2) * 1024
                        P.emit("scalar", lambda sc, m=m, s4=s4, s2_=s2_: sc.activation(
                            t1b[:, s2_:s2_ + 1024], stgA1[:, s4:s4 + 1024], AF.Silu,
                            scale=s1_sb[:, m:m + 1]),
                            waits=[("VC", max(vc_c1[m], vc_stt.get(m - 2, 0)))], inc="SC")
                        sc_silu[m] = P.n("SC")
                        P.emit("vector", lambda v, m=m, s4=s4, s2_=s2_: v.scalar_tensor_tensor(
                            g_sb[:, m * 1024:(m + 1) * 1024], stgA2[:, s4:s4 + 1024],
                            s2_sb[:, m:m + 1], t1b[:, s2_:s2_ + 1024], ALU.mult, ALU.mult),
                            waits=[("SC", sc_silu[m])], inc="VC")
                        vc_stt[m] = P.n("VC")

                    for m in range(32):
                        slot = m % 2
                        wl_alias = galias if m < 2 else []
                        P.emit("gpsimd", lambda g, m=m, slot=slot: g.dma_start(
                            out=w1buf[:, slot * 1024:(slot + 1) * 1024], in_=w1_d[m, :, :]),
                            waits=[("MM", mm_a.get(m - 2, 0))] + wl_alias,
                            inc=f"LW1{slot}", inc_n=16)
                        P.emit("gpsimd", lambda g, m=m, slot=slot: g.dma_start(
                            out=w2buf[:, slot * 1024:(slot + 1) * 1024], in_=w2_d[m, :, :]),
                            inc=f"LW2{slot}", inc_n=16)
                        wl_w1 = P.n(f"LW1{slot}")
                        wl_w2 = P.n(f"LW2{slot}")

                        def mk_wx(t, slot=slot, wb=None):
                            for nt in range(2):
                                for kc in range(8):
                                    r = t.matmul(psF[:, nt * 512:(nt + 1) * 512],
                                                 wb[:, slot * 1024 + kc * 128:
                                                    slot * 1024 + (kc + 1) * 128],
                                                 xnh[:, kc * 1024 + nt * 512:
                                                     kc * 1024 + (nt + 1) * 512],
                                                 start=(kc == 0), stop=(kc == 7))
                            return r
                        import functools
                        P.emit("tensor", functools.partial(mk_wx, wb=w1buf),
                               waits=[(f"LW1{slot}", wl_w1), (f"LW2{slot}", wl_w2),
                                      ("LXN", wl_xnh),
                                      ("VC", max(vc_c2.get(m - 1, 0), vc_gate)),
                                      ("SC", sc_gate)], inc="MM")
                        mm_w1 = P.n("MM")
                        weave(3.6)
                        s4 = (m % 3) * 1024
                        P.emit("vector", lambda v, s4=s4: v.tensor_copy(
                            stgA1[:, s4:s4 + 1024], psF[:, :]),
                            waits=[("MM", mm_w1), ("SC", sc_silu.get(m - 3, 0))], inc="VC")
                        vc_c1[m] = P.n("VC")
                        P.emit("tensor", functools.partial(mk_wx, wb=w2buf),
                               waits=[("VC", vc_c1[m])], inc="MM")
                        mm_a[m] = P.n("MM")
                        weave(3.6)
                        P.emit("vector", lambda v, s4=s4: v.tensor_copy(
                            stgA2[:, s4:s4 + 1024], psF[:, :]),
                            waits=[("MM", mm_a[m]), ("VC", vc_stt.get(m - 3, 0))], inc="VC")
                        vc_c2[m] = P.n("VC")
                        emit_a_tail(m - 2)
                    emit_a_tail(30)
                    emit_a_tail(31)
                    vc_gate = vc_stt.get(31, vc_gate)
                    sc_gate = P.n("SC")

                with ExitStack() as bctx:
                    hm_sb = bctx.enter_context(nc.sbuf_tensor(f"hm_sb_{hf}", [128, 32 * 1024], BF16))
                    stgB = bctx.enter_context(nc.sbuf_tensor(f"sgh_{hf}", [128, 3072], BF16))
                    for stage, (wd, src, dst, ssb) in enumerate((
                            (h1_d, g_sb, hm_sb, sh1_sb),
                            (h2_d, hm_sb, g_sb, sh2_sb))):
                        if stage == 1:
                            sc_gate = sc_b.get(31, sc_gate)
                            mm_hb_prev = mm_b
                        mm_b = {}
                        sc_b = {}
                        vc_cb = {}

                        def emit_bc_tail(m, dst=dst, ssb=ssb):
                            if m < 0 or m not in vc_cb:
                                return
                            s4 = (m % 3) * 1024
                            P.emit("scalar", lambda sc, m=m, s4=s4: sc.activation(
                                dst[:, m * 1024:(m + 1) * 1024], stgB[:, s4:s4 + 1024],
                                AF.Gelu, scale=ssb[:, m:m + 1]),
                                waits=[("VC", vc_cb[m])], inc="SC")
                            sc_b[m] = P.n("SC")

                        for m in range(32):
                            slot = m % 2
                            P.emit("gpsimd", lambda g, m=m, slot=slot, wd=wd: g.dma_start(
                                out=hbuf[:, slot * 4096:(slot + 1) * 4096], in_=wd[m, :, :]),
                                waits=[("MM", max(mm_b.get(m - 2, 0),
                                                  mm_hb_prev.get(30 + m, 0)))],
                                inc=f"LHB{slot}", inc_n=16)
                            wl_m = P.n(f"LHB{slot}")
                            for nt in range(2):
                                def mk_h(t, slot=slot, src=src, nt=nt):
                                    for kc in range(32):
                                        r = t.matmul(
                                            psF[:, nt * 512:(nt + 1) * 512],
                                            hbuf[:, slot * 4096 + kc * 128:
                                                 slot * 4096 + (kc + 1) * 128],
                                            src[:, kc * 1024 + nt * 512: kc * 1024 + (nt + 1) * 512],
                                            start=(kc == 0), stop=(kc == 31))
                                    return r
                                P.emit("tensor", mk_h,
                                       waits=[(f"LHB{slot}", wl_m),
                                              ("VC", max(vc_cb.get(m - 1, 0), vc_gate)),
                                              ("SC", sc_gate)], inc="MM")
                            mm_b[m] = P.n("MM")
                            weave(14.4)
                            s4 = (m % 3) * 1024
                            P.emit("vector", lambda v, s4=s4: v.tensor_copy(
                                stgB[:, s4:s4 + 1024], psF[:, :]),
                                waits=[("MM", mm_b[m]), ("SC", sc_b.get(m - 3, 0))], inc="VC")
                            vc_cb[m] = P.n("VC")
                            emit_bc_tail(m - 2)
                        emit_bc_tail(30)
                        emit_bc_tail(31)
                    sc_gate = sc_b.get(31, sc_gate)
                    mm_hb_prev = mm_b

                with ExitStack() as dctx:
                    ffn_fm = dctx.enter_context(nc.sbuf_tensor(f"ffn_fm_{hf}", [128, 8192], F32))
                    ffn_stage = dctx.enter_context(nc.sbuf_tensor(f"ffn_stage_{hf}", [128, 2 * 1024], F32))
                    mm_d = {}
                    vc_d = {}
                    for m3 in range(8):
                        slot = m3 % 2
                        P.emit("gpsimd", lambda g, m3=m3, slot=slot: g.dma_start(
                            out=hbuf[:, slot * 4096:(slot + 1) * 4096], in_=w3_d[m3, :, :]),
                            waits=[("MM", max(mm_d.get(m3 - 2, 0),
                                              mm_hb_prev.get(30 + m3, 0)))],
                            inc=f"LHB{slot}", inc_n=16)
                        wl_m = P.n(f"LHB{slot}")
                        for nt in range(2):
                            def mk_w3(t, slot=slot, nt=nt):
                                for kc in range(32):
                                    r = t.matmul(
                                        psF[:, nt * 512:(nt + 1) * 512],
                                        hbuf[:, slot * 4096 + kc * 128:
                                             slot * 4096 + (kc + 1) * 128],
                                        g_sb[:, kc * 1024 + nt * 512: kc * 1024 + (nt + 1) * 512],
                                        start=(kc == 0), stop=(kc == 31))
                                return r
                            P.emit("tensor", mk_w3,
                                   waits=[(f"LHB{slot}", wl_m), ("SC", sc_gate),
                                          ("VC", max(vc_d.get(m3 - 1, 0), vc_gate))],
                                   inc="MM")
                        mm_d[m3] = P.n("MM")
                        weave(14.4)
                        P.emit("vector", lambda v, m3=m3: v.tensor_scalar_mul(
                            ffn_fm[:, m3 * 1024:(m3 + 1) * 1024], psF[:, :],
                            s3_sb[:, m3:m3 + 1]),
                            waits=[("MM", mm_d[m3])], inc="VC")
                        vc_d[m3] = P.n("VC")

                    vc_e = {}
                    for tt in range(8):
                        slot = tt % 2

                        def mk_tr(t, tt=tt):
                            for fc in range(8):
                                r = t.transpose(
                                    psF[:, fc * 128:(fc + 1) * 128],
                                    ffn_fm[:, fc * 1024 + tt * 128: fc * 1024 + (tt + 1) * 128],
                                    ident[:, :])
                            return r
                        P.emit("tensor", mk_tr,
                               waits=[("VC", max(vc_e.get(tt - 1, 0), vc_d.get(7, 0)))], inc="MM")
                        mm_tt = P.n("MM")
                        weave(2.4)
                        P.emit("vector", lambda v, slot=slot: v.tensor_copy(
                            ffn_stage[:, slot * 1024:(slot + 1) * 1024], psF[:, :]),
                            waits=[("MM", mm_tt), (f"SPE{slot}", P.n(f"SPE{slot}"))],
                            inc="VC")
                        vc_e[tt] = P.n("VC")
                        P.emit("gpsimd", lambda g, tt=tt, slot=slot, hf=hf: g.dma_start(
                            out=ffn_scr[hf * 1024 + tt * 128: hf * 1024 + (tt + 1) * 128, :],
                            in_=ffn_stage[:, slot * 1024:(slot + 1) * 1024]),
                            waits=[("VC", vc_e[tt])], inc=f"SPE{slot}", inc_n=16)
                    vc_gate = vc_e.get(7, vc_gate)
                    sc_gate = P.n("SC")

                while WV["q"]:
                    WV["q"].pop(0)()

            P.flush(block)

        with (
            nc.sbuf_tensor("xtile", [128, 2 * 1024], F32) as xtile,
            nc.sbuf_tensor("ffntile", [128, 2 * 1024], F32) as ffntile,
            nc.sbuf_tensor("avtile", [128, 2 * 1024], F32) as avtile,
            nc.sbuf_tensor("x2t", [128, 1024], F32) as x2t,
            nc.sbuf_tensor("outstage", [128, 2 * 1024], F32) as outstage,
            nc.sbuf_tensor("cols", [128, 8], F32) as cols,
            nc.sbuf_tensor("junk", [128, 1024], F32) as junk,
        ):
            def filler(v):
                return v.tensor_copy(junk[:, 0:512], junk[:, 512:1024])
            full_barrier()
            P.emit("vector", lambda v: v.memset(junk[:, :], 0.0), inc="VC")

            vc_out = {}
            st_out = {}
            for ot in range(16):
                slot = ot % 2
                hl = ot // 4
                d0 = 16 * (ot % 4)
                w = [("VC", vc_out.get(ot - 2, 0))]
                P.emit("sync", lambda sy, ot=ot, slot=slot: sy.dma_start(
                    out=xtile[:, slot * 1024:(slot + 1) * 1024],
                    in_=xtm_d[ot * 128:(ot + 1) * 128, :]), waits=w,
                    inc=f"LP4X{slot}", inc_n=16)
                P.emit("sync", lambda sy, ot=ot, slot=slot: sy.dma_start(
                    out=ffntile[:, slot * 1024:(slot + 1) * 1024],
                    in_=ffn_scr[ot * 128:(ot + 1) * 128, :]), inc=f"LP4F{slot}", inc_n=16)
                P.emit("sync", lambda sy, slot=slot, hl=hl, d0=d0: sy.dma_start(
                    out=avtile[:, slot * 1024:(slot + 1) * 1024],
                    in_=av_scr[hl, d0:d0 + 16, :].rearrange("i (j c) -> (i j) c", c=1024)),
                    inc=f"LP4A{slot}", inc_n=16)
                ld_t = [(f"LP4X{slot}", P.n(f"LP4X{slot}")),
                        (f"LP4F{slot}", P.n(f"LP4F{slot}")),
                        (f"LP4A{slot}", P.n(f"LP4A{slot}"))]
                P.emit("vector", lambda v, slot=slot: v.reduce_sum(
                    cols[:, 0:1], xtile[:, slot * 1024:(slot + 1) * 1024],
                    axis=mybir.AxisListType.X), waits=list(ld_t), inc="VC")
                vc_s1 = P.n("VC")
                P.emit("scalar", lambda sc, slot=slot: sc.activation(
                    x2t[:, :], xtile[:, slot * 1024:(slot + 1) * 1024], AF.Square),
                    waits=list(ld_t) + [("VC", vc_s1)], inc="SC")
                sc_sq = P.n("SC")
                def mk_stats4(v, slot=slot):
                    v.reduce_sum(cols[:, 1:2], x2t[:, :], axis=mybir.AxisListType.X)
                    v.tensor_scalar_mul(cols[:, 2:3], cols[:, 0:1], 1.0 / 1024)
                    v.tensor_scalar_mul(cols[:, 3:4], cols[:, 1:2], 1.0 / 1024)
                    filler(v)
                    v.tensor_mul(cols[:, 4:5], cols[:, 2:3], cols[:, 2:3])
                    filler(v)
                    v.tensor_sub(cols[:, 3:4], cols[:, 3:4], cols[:, 4:5])
                    filler(v)
                    return v.tensor_scalar_add(cols[:, 3:4], cols[:, 3:4], 1e-5)
                P.emit("vector", mk_stats4, waits=[("SC", sc_sq)], inc="VC")
                vc_v = P.n("VC")
                P.emit("scalar", lambda sc: sc.activation(
                    cols[:, 4:5], cols[:, 3:4], AF.Sqrt), waits=[("VC", vc_v)], inc="SC")
                sc_sd = P.n("SC")
                def mk_r4(v):
                    v.reciprocal(cols[:, 5:6], cols[:, 4:5])
                    filler(v)
                    return v.scalar_tensor_tensor(cols[:, 6:7], cols[:, 2:3], -1.0,
                                                  cols[:, 5:6], ALU.mult, ALU.mult)
                P.emit("vector", mk_r4, waits=[("SC", sc_sd)], inc="VC")
                vc_r = P.n("VC")
                P.emit("scalar", lambda sc, slot=slot: sc.activation(
                    x2t[:, :], xtile[:, slot * 1024:(slot + 1) * 1024], AF.Identity,
                    bias=cols[:, 6:7], scale=cols[:, 5:6]), waits=[("VC", vc_r)], inc="SC")
                sc_xn = P.n("SC")
                def mk_out(v, slot=slot):
                    v.tensor_add(outstage[:, slot * 1024:(slot + 1) * 1024],
                                 xtile[:, slot * 1024:(slot + 1) * 1024], x2t[:, :])
                    v.tensor_add(outstage[:, slot * 1024:(slot + 1) * 1024],
                                 outstage[:, slot * 1024:(slot + 1) * 1024],
                                 avtile[:, slot * 1024:(slot + 1) * 1024])
                    return v.tensor_add(outstage[:, slot * 1024:(slot + 1) * 1024],
                                        outstage[:, slot * 1024:(slot + 1) * 1024],
                                        ffntile[:, slot * 1024:(slot + 1) * 1024])
                P.emit("vector", mk_out,
                       waits=[("SC", sc_xn), ("ST", st_out.get(ot - 2, 0))], inc="VC")
                vc_out[ot] = P.n("VC")
                P.emit("gpsimd", lambda g, ot=ot, slot=slot: g.dma_start(
                    out=out_d[ot * 128:(ot + 1) * 128, :],
                    in_=outstage[:, slot * 1024:(slot + 1) * 1024]),
                    waits=[("VC", vc_out[ot])], inc="ST", inc_n=16)
                st_out[ot] = P.n("ST")
            P.emit("gpsimd", lambda g: g.nop(), waits=[("ST", P.n("ST"))])
            P.flush(block)

    return nc


def _prep_inputs(inputs):
    x = np.ascontiguousarray(inputs["L0_attn_queries"])
    keys = np.ascontiguousarray(inputs["L0_attn_keys"][0])
    vals = np.ascontiguousarray(inputs["L0_attn_values"][0])

    def strips(aT, nstrip):
        return np.ascontiguousarray(
            aT.reshape(8, 128, nstrip, 512).transpose(2, 1, 0, 3))

    kft = strips(keys.T, KSTRIP)
    vft = strips(vals.T, KSTRIP)

    def wtile(wT, KD, MD):
        return np.ascontiguousarray(
            wT.reshape(KD // 128, 128, MD // 128, 128).transpose(2, 1, 0, 3)
        ).reshape(MD // 128, 128, KD)

    w1p = wtile(inputs["L0_ffn_w1_w"].T, D, DHID)
    w2p = wtile(inputs["L0_ffn_w2_w"].T, D, DHID)
    h1p = wtile(inputs["L0_ffn_h1_w"].T, DHID, DHID)
    h2p = wtile(inputs["L0_ffn_h2_w"].T, DHID, DHID)
    w3p = wtile(inputs["L0_ffn_w3_w"].T, DHID, D)
    scol = lambda s, n: np.ascontiguousarray(np.asarray(s).reshape(n, 128).T)
    s1c = scol(inputs["L0_ffn_w1_s"], 32)
    s2c = scol(inputs["L0_ffn_w2_s"], 32)
    sh1c = scol(inputs["L0_ffn_h1_s"], 32)
    sh2c = scol(inputs["L0_ffn_h2_s"], 32)
    s3c = scol(inputs["L0_ffn_w3_s"], 8)

    WqT = np.ascontiguousarray(inputs["L0_attn_Wq_w"].T)
    WkT = np.ascontiguousarray(inputs["L0_attn_Wk_w"].T)
    WvT = np.ascontiguousarray(inputs["L0_attn_Wv_w"].T)

    in_maps = []
    for c in range(8):
        b, g = c // 4, c % 4
        xT = x[b].T
        xfm = strips(xT, NSTRIP)
        wslice = lambda WT: np.ascontiguousarray(
            WT[:, 256 * g:256 * (g + 1)].reshape(8, 128, 256).transpose(1, 0, 2)
        ).reshape(128, 2048)
        bias_col = lambda bvec: np.ascontiguousarray(
            np.asarray(bvec)[256 * g:256 * (g + 1)].reshape(2, 128).T)
        in_maps.append(dict(
            xfm=np.ascontiguousarray(xfm),
            xofm=np.ascontiguousarray(xfm[4 * g: 4 * g + 4]),
            xtm=np.ascontiguousarray(x[b, TOK * g: TOK * (g + 1)]),
            kft=kft, vft=vft,
            wq=wslice(WqT), wk=wslice(WkT), wv=wslice(WvT),
            qb=bias_col(inputs["L0_attn_Wq_b"]),
            kb=bias_col(inputs["L0_attn_Wk_b"]),
            vb=np.ascontiguousarray(
                np.asarray(inputs["L0_attn_Wv_b"])[256 * g:256 * (g + 1)].reshape(1, 256)),
            w1p=w1p, w2p=w2p, h1p=h1p, h2p=h2p, w3p=w3p,
            s1c=s1c, s2c=s2c, sh1c=sh1c, sh2c=sh2c, s3c=s3c,
        ))
    return in_maps


def kernel(**inputs):
    in_maps = _prep_inputs(inputs)
    nc = build_nc(taps=False)
    res = run_bass_kernel_spmd(nc, in_maps, core_ids=list(range(8)))
    out = np.empty((2, SQ, D), dtype=np.float32)
    for c in range(8):
        b, g = c // 4, c % 4
        out[b, TOK * g: TOK * (g + 1)] = res.results[c]["out"]
    return out


# version 09
# speedup vs baseline: 1.1297x; 1.0480x over previous
import numpy as np
import concourse.bass as bass
import concourse.mybir as mybir
from concourse.bass_utils import run_bass_kernel_spmd

F32 = mybir.dt.float32
BF16 = mybir.dt.bfloat16
AF = mybir.ActivationFunctionType
ALU = mybir.AluOpType

D = 1024
SQ = 8192
SK = 4096
TOK = 2048
NSTRIP = 16
KSTRIP = 8
DHID = 4096

ENGINES = ("sync", "gpsimd", "tensor", "scalar", "vector")
SEMS = ("LD", "WL", "SP", "ST", "MM", "SC", "VC")


class Plan:

    def __init__(self, sems):
        self.sems = sems
        self.ev = {e: [] for e in ENGINES}
        self.cnt = {k: 0 for k in SEMS}

    def n(self, sem):
        return self.cnt[sem]

    def emit(self, eng, fn, waits=(), inc=None, inc_n=1):
        ws = tuple((s, v) for (s, v) in waits if v > 0)
        self.ev[eng].append((fn, ws, inc, inc_n))
        if inc is not None:
            self.cnt[inc] += inc_n

    def barrier_waits(self):
        return {s: self.cnt[s] for s in SEMS}

    def flush(self, block):
        for e in ENGINES:
            evs = self.ev[e]
            if not evs:
                continue
            self.ev[e] = []

            def body(eng, evs=evs):
                last_wait = {}
                for fn, waits, inc, inc_n in evs:
                    for s, v in waits:
                        if last_wait.get(s, -1) < v:
                            eng.wait_ge(self.sems[s], v)
                            last_wait[s] = v
                    r = fn(eng)
                    if inc is not None:
                        r.then_inc(self.sems[inc], inc_n)
            getattr(block, e)(body)


def build_nc(taps=False):
    nc = bass.Bass("TRN2", target_bir_lowering=False)

    xfm_d = nc.dram_tensor("xfm", [NSTRIP, 128, 8, 512], F32, kind="ExternalInput").ap()
    xtm_d = nc.dram_tensor("xtm", [TOK, D], F32, kind="ExternalInput").ap()
    xofm_d = nc.dram_tensor("xofm", [4, 128, 8, 512], F32, kind="ExternalInput").ap()
    kft_d = nc.dram_tensor("kft", [KSTRIP, 128, 8, 512], F32, kind="ExternalInput").ap()
    vft_d = nc.dram_tensor("vft", [KSTRIP, 128, 8, 512], F32, kind="ExternalInput").ap()
    wq_d = nc.dram_tensor("wq", [128, 2048], F32, kind="ExternalInput").ap()
    wk_d = nc.dram_tensor("wk", [128, 2048], F32, kind="ExternalInput").ap()
    wv_d = nc.dram_tensor("wv", [128, 2048], F32, kind="ExternalInput").ap()
    qb_d = nc.dram_tensor("qb", [128, 2], F32, kind="ExternalInput").ap()
    kb_d = nc.dram_tensor("kb", [128, 2], F32, kind="ExternalInput").ap()
    vb_d = nc.dram_tensor("vb", [1, 256], F32, kind="ExternalInput").ap()
    w1_d = nc.dram_tensor("w1p", [32, 128, 1024], F32, kind="ExternalInput").ap()
    w2_d = nc.dram_tensor("w2p", [32, 128, 1024], F32, kind="ExternalInput").ap()
    h1_d = nc.dram_tensor("h1p", [32, 128, 4096], F32, kind="ExternalInput").ap()
    h2_d = nc.dram_tensor("h2p", [32, 128, 4096], F32, kind="ExternalInput").ap()
    w3_d = nc.dram_tensor("w3p", [8, 128, 4096], F32, kind="ExternalInput").ap()
    s1_d = nc.dram_tensor("s1c", [128, 32], F32, kind="ExternalInput").ap()
    s2_d = nc.dram_tensor("s2c", [128, 32], F32, kind="ExternalInput").ap()
    sh1_d = nc.dram_tensor("sh1c", [128, 32], F32, kind="ExternalInput").ap()
    sh2_d = nc.dram_tensor("sh2c", [128, 32], F32, kind="ExternalInput").ap()
    s3_d = nc.dram_tensor("s3c", [128, 8], F32, kind="ExternalInput").ap()

    out_d = nc.dram_tensor("out", [TOK, D], F32, kind="ExternalOutput").ap()
    scratch_kind = "ExternalOutput" if taps else "Internal"
    av_scr = nc.dram_tensor("av_scr", [4, 64, SQ], F32, kind=scratch_kind).ap()
    stat_scr = nc.dram_tensor("stat_scr", [2, TOK], F32, kind="Internal").ap()
    ffn_scr = nc.dram_tensor("ffn_scr", [TOK, D], F32, kind=scratch_kind).ap()
    if taps:
        qtap_d = nc.dram_tensor("q_tap", [128, 2 * SQ], BF16, kind="ExternalOutput").ap()
        xntap_d = nc.dram_tensor("xn_tap", [128, 8 * TOK], BF16, kind="ExternalOutput").ap()
        ktap_d = nc.dram_tensor("k_tap", [128, 2 * SK], BF16, kind="ExternalOutput").ap()
        vtap_d = nc.dram_tensor("v_tap", [128, 4 * 32 * 65], BF16, kind="ExternalOutput").ap()
        mtap_d = nc.dram_tensor("m_tap", [1, 2 * TOK], F32, kind="ExternalOutput").ap()
        gtap_d = nc.dram_tensor("g_tap", [128, 32 * 1024], BF16, kind="ExternalOutput").ap()
        hmtap_d = nc.dram_tensor("hm_tap", [128, 32 * 1024], BF16, kind="ExternalOutput").ap()
        g2tap_d = nc.dram_tensor("g2_tap", [128, 32 * 1024], BF16, kind="ExternalOutput").ap()
        ctap_d = nc.dram_tensor("c_tap", [128, 48], F32, kind="ExternalOutput").ap()

    from contextlib import ExitStack
    with ExitStack() as ctx:
        sems = {s: ctx.enter_context(nc.semaphore(s)) for s in SEMS}
        P = Plan(sems)

        xn_own = ctx.enter_context(nc.sbuf_tensor("xn_own", [128, 8 * TOK], BF16))
        ident = ctx.enter_context(nc.sbuf_tensor("ident", [128, 128], F32))
        ones_col = ctx.enter_context(nc.sbuf_tensor("ones_col", [128, 1], BF16))
        ones_row = ctx.enter_context(nc.sbuf_tensor("ones_row", [1, 128], F32))
        eps1 = ctx.enter_context(nc.sbuf_tensor("eps1", [1, 1], F32))
        qb_sb = ctx.enter_context(nc.sbuf_tensor("qb_sb", [128, 2], F32))
        kb_sb = ctx.enter_context(nc.sbuf_tensor("kb_sb", [128, 2], F32))
        vb_sb = ctx.enter_context(nc.sbuf_tensor("vb_sb", [1, 256], F32))
        s1_sb = ctx.enter_context(nc.sbuf_tensor("s1_sb", [128, 32], F32))
        s2_sb = ctx.enter_context(nc.sbuf_tensor("s2_sb", [128, 32], F32))
        sh1_sb = ctx.enter_context(nc.sbuf_tensor("sh1_sb", [128, 32], F32))
        sh2_sb = ctx.enter_context(nc.sbuf_tensor("sh2_sb", [128, 32], F32))
        s3_sb = ctx.enter_context(nc.sbuf_tensor("s3_sb", [128, 8], F32))
        ones2 = ctx.enter_context(nc.sbuf_tensor("ones2", [128, 128], F32))

        block = ctx.enter_context(nc.Block())

        def full_barrier():
            w = P.barrier_waits()
            waits = tuple(w.items())
            for e in ENGINES:
                P.emit(e, lambda eng: eng.nop(), waits=waits)

        P.emit("sync", lambda sy: sy.dma_start(out=qb_sb[:, :], in_=qb_d), inc="LD", inc_n=16)
        P.emit("sync", lambda sy: sy.dma_start(out=kb_sb[:, :], in_=kb_d), inc="LD", inc_n=16)
        P.emit("sync", lambda sy: sy.dma_start(out=vb_sb[:, :], in_=vb_d), inc="LD", inc_n=16)
        P.emit("sync", lambda sy: sy.dma_start(out=s1_sb[:, :], in_=s1_d), inc="LD", inc_n=16)
        P.emit("sync", lambda sy: sy.dma_start(out=s2_sb[:, :], in_=s2_d), inc="LD", inc_n=16)
        P.emit("sync", lambda sy: sy.dma_start(out=sh1_sb[:, :], in_=sh1_d), inc="LD", inc_n=16)
        P.emit("sync", lambda sy: sy.dma_start(out=sh2_sb[:, :], in_=sh2_d), inc="LD", inc_n=16)
        P.emit("sync", lambda sy: sy.dma_start(out=s3_sb[:, :], in_=s3_d), inc="LD", inc_n=16)
        P.emit("vector", lambda v: v.memset(ones_col[:, :], 1.0), inc="VC")
        P.emit("vector", lambda v: v.memset(ones_row[:, :], 1.0), inc="VC")
        P.emit("vector", lambda v: v.memset(eps1[:, :], 1e-5), inc="VC")
        P.emit("vector", lambda v: v.memset(ones2[:, :], 1.0), inc="VC")
        P.emit("gpsimd", lambda g: g.memset(ident[:, :], 0.0), inc="WL")
        P.emit("gpsimd", lambda g: g.affine_select(
            out=ident[:, :], in_=ident[:, :], compare_op=ALU.not_equal,
            fill=1.0, base=0, pattern=[[-1, 128]], channel_multiplier=1), inc="WL")
        ld_bias = P.n("LD")
        vc_init = P.n("VC")

        with (
            nc.sbuf_tensor("q_fm", [128, 2 * SQ], BF16) as q_fm,
            nc.sbuf_tensor("k_fm", [128, 2 * SK], BF16) as k_fm,
            nc.sbuf_tensor("v_tm", [128, 4 * 32 * 65], BF16) as v_tm,
        ):
            with (
                nc.sbuf_tensor("xbuf", [128, 2 * 8192], BF16) as xbuf,
                nc.sbuf_tensor("x2buf", [128, 8192], BF16) as x2buf,
                nc.sbuf_tensor("wq_sb", [128, 2048], BF16) as wq_sb,
                nc.sbuf_tensor("wk_sb", [128, 2048], BF16) as wk_sb,
                nc.sbuf_tensor("wv_sb", [128, 2048], BF16) as wv_sb,
                nc.sbuf_tensor("rows", [128, 6144], F32) as rows,
                nc.sbuf_tensor("rows2", [128, 2 * TOK], F32) as rows2,
                nc.sbuf_tensor("rbc", [128, 2 * 512], BF16) as rbc,
                nc.sbuf_tensor("mrbc", [128, 2 * 512], BF16) as mrbc,
                nc.sbuf_tensor("tmpb", [128, 512], BF16) as tmpb,
                nc.psum_tensor("psQ", [128, 1024], F32) as psQ,
            ):
                m_own = rows2[0:1, 0:TOK]
                rr_own = rows2[0:1, TOK:2 * TOK]
                P.emit("gpsimd", lambda g: g.dma_start(out=wq_sb[:, :], in_=wq_d), inc="WL", inc_n=16)
                P.emit("gpsimd", lambda g: g.dma_start(out=wk_sb[:, :], in_=wk_d), inc="WL", inc_n=16)
                P.emit("gpsimd", lambda g: g.dma_start(out=wv_sb[:, :], in_=wv_d), inc="WL", inc_n=16)
                wl_w = P.n("WL")

                mm_last_reader = {}
                vc_evacS = {}
                mm_sums = {}
                sc_qevac = {}
                strip_ev = {}

                with (
                    nc.psum_tensor("psS", [1, 2048], F32) as psS,
                    nc.psum_tensor("psB", [128, 1024], F32) as psB,
                ):
                    def emit_chain(c):
                        own = c >= 8
                        slot = c % 2
                        base = slot * 8192
                        srcs = []
                        for st in range(2):
                            if own:
                                srcs.append(xofm_d[2 * (c - 8) + st, :, :, :].rearrange(
                                    "p k c -> p (k c)"))
                            else:
                                srcs.append(xfm_d[2 * c + st, :, :, :].rearrange(
                                    "p k c -> p (k c)"))
                        P.emit("gpsimd", lambda g, src=srcs[0], base=base: g.dma_start(
                            out=xbuf[:, base:base + 4096], in_=src),
                            waits=[("MM", mm_last_reader.get(c - 2, 0))], inc="WL", inc_n=16)
                        P.emit("gpsimd", lambda g, src=srcs[1], base=base: g.dma_start(
                            out=xbuf[:, base + 4096:base + 8192], in_=src),
                            inc="WL", inc_n=16)
                        wl_c = P.n("WL")
                        P.emit("scalar", lambda sc, base=base: sc.activation(
                            x2buf[:, :], xbuf[:, base:base + 8192], AF.Square),
                            waits=[("WL", wl_c), ("MM", mm_sums.get(c - 1, 0))], inc="SC")
                        sc_x2 = P.n("SC")

                        def mk_sums(t, base=base):
                            for st in range(2):
                                for ch in range(8):
                                    t.matmul(psS[:, st * 512:(st + 1) * 512], ones_col[:, :],
                                             xbuf[:, base + st * 4096 + ch * 512:
                                                  base + st * 4096 + (ch + 1) * 512],
                                             start=(ch == 0), stop=(ch == 7))
                            for st in range(2):
                                for ch in range(8):
                                    r = t.matmul(psS[:, 1024 + st * 512: 1024 + (st + 1) * 512],
                                                 ones_col[:, :],
                                                 x2buf[:, st * 4096 + ch * 512:
                                                       st * 4096 + (ch + 1) * 512],
                                                 start=(ch == 0), stop=(ch == 7))
                            return r
                        P.emit("tensor", mk_sums,
                               waits=[("WL", wl_c), ("SC", sc_x2), ("VC", vc_evacS.get(c - 1, 0))],
                               inc="MM")
                        mm_sums[c] = P.n("MM")
                        P.emit("vector", lambda v, slot=slot: v.tensor_copy(
                            rows[0:1, slot * 2048:(slot + 1) * 2048], psS[:, :]),
                            waits=[("MM", mm_sums[c])], inc="VC")
                        vc_evacS[c] = P.n("VC")

                        def mk_stats(v, slot=slot, own=own, c=c):
                            sb_ = slot * 2048
                            so_ = slot * 1024
                            mdst = m_own[:, (c - 8) * 1024:(c - 7) * 1024] if own \
                                else rows[64:65, so_:so_ + 1024]
                            msq = rows[32:33, so_:so_ + 1024]
                            ex2 = rows[32:33, 2048 + so_:2048 + so_ + 1024]
                            var = rows[32:33, 4096 + so_:4096 + so_ + 1024]
                            v.tensor_scalar_mul(mdst, rows[0:1, sb_:sb_ + 1024], 1.0 / 1024)
                            v.tensor_scalar_mul(ex2, rows[0:1, sb_ + 1024:sb_ + 2048], 1.0 / 1024)
                            v.tensor_mul(msq, mdst, mdst)
                            v.tensor_sub(var, ex2, msq)
                            return v.tensor_scalar_add(var, var, 1e-5)
                        P.emit("vector", mk_stats, waits=[("VC", vc_init)], inc="VC")
                        vc_stats = P.n("VC")
                        P.emit("scalar", lambda sc, slot=slot: sc.activation(
                            rows[64:65, 2048 + slot * 1024: 2048 + (slot + 1) * 1024],
                            rows[32:33, 4096 + slot * 1024: 4096 + (slot + 1) * 1024], AF.Sqrt),
                            waits=[("VC", vc_stats)], inc="SC")
                        sc_sd = P.n("SC")

                        def mk_rr(v, slot=slot, own=own, c=c):
                            so_ = slot * 1024
                            rdst = rr_own[:, (c - 8) * 1024:(c - 7) * 1024] if own \
                                else rows[64:65, 4096 + so_:4096 + so_ + 1024]
                            mdst = m_own[:, (c - 8) * 1024:(c - 7) * 1024] if own \
                                else rows[64:65, so_:so_ + 1024]
                            v.reciprocal(rdst, rows[64:65, 2048 + so_:2048 + so_ + 1024])
                            return v.tensor_mul(rows[0:1, 4096 + so_:4096 + so_ + 1024],
                                                mdst, rdst)
                        P.emit("vector", mk_rr, waits=[("SC", sc_sd)], inc="VC")
                        vc_rr = P.n("VC")

                        mm_last = 0
                        for st in range(2):
                            sg = (c - 8) * 2 + st if own else 2 * c + st
                            s2_ = (2 * c + st) % 2

                            def mk_bcast(t, slot=slot, st=st, own=own, c=c):
                                so_ = slot * 1024
                                if own:
                                    rsrc = rr_own[:, (c - 8) * 1024 + st * 512:
                                                  (c - 8) * 1024 + (st + 1) * 512]
                                    lh = ones_row[:, :]
                                else:
                                    rsrc = rows[64:65, 4096 + so_ + st * 512:
                                                4096 + so_ + (st + 1) * 512]
                                    lh = ones2[64:65, 0:128]
                                t.matmul(psB[:, 0:512], lh, rsrc, start=True, stop=True)
                                return t.matmul(psB[:, 512:1024], ones_row[:, :],
                                                rows[0:1, 4096 + so_ + st * 512:
                                                     4096 + so_ + (st + 1) * 512],
                                                start=True, stop=True)
                            P.emit("tensor", mk_bcast,
                                   waits=[("VC", vc_rr), ("VC", strip_ev.get(("evB", c, st - 1),
                                          strip_ev.get(("evB", c - 1, 1), 0)))], inc="MM")
                            mm_bc = P.n("MM")
                            P.emit("vector", lambda v, s2_=s2_: (
                                v.tensor_copy(rbc[:, s2_ * 512:(s2_ + 1) * 512], psB[:, 0:512]),
                                v.tensor_copy(mrbc[:, s2_ * 512:(s2_ + 1) * 512],
                                              psB[:, 512:1024]))[-1],
                                waits=[("MM", mm_bc)], inc="VC")
                            strip_ev[("evB", c, st)] = P.n("VC")

                            def mk_xn(v, base=base, st=st, s2_=s2_, own=own, c=c):
                                for ch in range(8):
                                    xsl = xbuf[:, base + st * 4096 + ch * 512:
                                               base + st * 4096 + (ch + 1) * 512]
                                    v.tensor_mul(tmpb[:, :], xsl, rbc[:, s2_ * 512:(s2_ + 1) * 512])
                                    if own:
                                        tpos = (c - 8) * 1024 + st * 512
                                        dst = xn_own[:, ch * TOK + tpos: ch * TOK + tpos + 512]
                                    else:
                                        dst = xsl
                                    r = v.tensor_sub(dst, tmpb[:, :],
                                                     mrbc[:, s2_ * 512:(s2_ + 1) * 512])
                                return r
                            P.emit("vector", mk_xn, waits=[("VC", strip_ev[("evB", c, st)])],
                                   inc="VC")
                            vc_xn = P.n("VC")
                            strip_ev[("xn", c, st)] = vc_xn
                            if not own:
                                def mk_q(t, base=base, st=st):
                                    for mc in range(2):
                                        for kc in range(8):
                                            r = t.matmul(
                                                psQ[:, mc * 512:(mc + 1) * 512],
                                                wq_sb[:, kc * 256 + mc * 128: kc * 256 + (mc + 1) * 128],
                                                xbuf[:, base + st * 4096 + kc * 512:
                                                     base + st * 4096 + (kc + 1) * 512],
                                                start=(kc == 0), stop=(kc == 7))
                                    return r
                                P.emit("tensor", mk_q,
                                       waits=[("VC", vc_xn), ("WL", wl_w),
                                              ("SC", sc_qevac.get(sg - 1, 0))], inc="MM")
                                mm_last = P.n("MM")

                                def mk_qevac(sc, sg=sg):
                                    for mc in range(2):
                                        r = sc.activation(
                                            q_fm[:, mc * SQ + sg * 512: mc * SQ + (sg + 1) * 512],
                                            psQ[:, mc * 512:(mc + 1) * 512], AF.Relu,
                                            bias=qb_sb[:, mc:mc + 1])
                                    return r
                                P.emit("scalar", mk_qevac,
                                       waits=[("MM", mm_last), ("LD", ld_bias)], inc="SC")
                                sc_qevac[sg] = P.n("SC")
                        mm_last_reader[c] = P.n("MM") if own else mm_last
                        if own:
                            mm_last_reader[c] = mm_sums[c]
                            strip_ev[("ownxn", c)] = P.n("VC")


                    for c in range(10):
                        emit_chain(c)

                    own_xn_vc = P.n("VC")
                    sc_q_all = P.n("SC")

                sc_kevac = {}
                mm_kread = {}
                with nc.psum_tensor("psV", [128, 256], F32) as psV:
                    for c in range(4):
                        slot = c % 2
                        base = slot * 8192
                        w = [("MM", mm_kread.get(c - 2, 0))]
                        if c < 2:
                            w = [("VC", own_xn_vc), ("MM", mm_sums[8 + c])]
                        P.emit("gpsimd", lambda g, c=c, base=base: g.dma_start(
                            out=xbuf[:, base:base + 4096],
                            in_=kft_d[2 * c, :, :, :].rearrange("p k c -> p (k c)")),
                            waits=w, inc="WL", inc_n=16)
                        P.emit("gpsimd", lambda g, c=c, base=base: g.dma_start(
                            out=xbuf[:, base + 4096:base + 8192],
                            in_=kft_d[2 * c + 1, :, :, :].rearrange("p k c -> p (k c)")),
                            inc="WL", inc_n=16)
                        wl_c = P.n("WL")
                        for st in range(2):
                            sg = 2 * c + st

                            def mk_k(t, base=base, st=st):
                                for mc in range(2):
                                    for kc in range(8):
                                        r = t.matmul(
                                            psQ[:, mc * 512:(mc + 1) * 512],
                                            wk_sb[:, kc * 256 + mc * 128: kc * 256 + (mc + 1) * 128],
                                            xbuf[:, base + st * 4096 + kc * 512:
                                                 base + st * 4096 + (kc + 1) * 512],
                                            start=(kc == 0), stop=(kc == 7))
                                return r
                            P.emit("tensor", mk_k,
                                   waits=[("WL", wl_c),
                                          ("SC", sc_kevac.get(sg - 1, sc_q_all))], inc="MM")
                            mm_k = P.n("MM")

                            def mk_kevac(sc, sg=sg):
                                for mc in range(2):
                                    r = sc.activation(
                                        k_fm[:, mc * SK + sg * 512: mc * SK + (sg + 1) * 512],
                                        psQ[:, mc * 512:(mc + 1) * 512], AF.Relu,
                                        bias=kb_sb[:, mc:mc + 1])
                                return r
                            P.emit("scalar", mk_kevac, waits=[("MM", mm_k)], inc="SC")
                            sc_kevac[sg] = P.n("SC")
                        mm_kread[c] = P.n("MM")

                    vc_vevac = {}
                    mm_vread = {}
                    for c in range(4):
                        slot = c % 2
                        base = slot * 8192
                        w = [("MM", mm_vread.get(c - 2, mm_kread.get(2 + c, 0)))]
                        P.emit("gpsimd", lambda g, c=c, base=base: g.dma_start(
                            out=xbuf[:, base:base + 4096],
                            in_=vft_d[2 * c, :, :, :].rearrange("p k c -> p (k c)")),
                            waits=w, inc="WL", inc_n=16)
                        P.emit("gpsimd", lambda g, c=c, base=base: g.dma_start(
                            out=xbuf[:, base + 4096:base + 8192],
                            in_=vft_d[2 * c + 1, :, :, :].rearrange("p k c -> p (k c)")),
                            inc="WL", inc_n=16)
                        wl_c = P.n("WL")
                        for tsub in range(8):
                            kt = c * 8 + tsub
                            off = base + (tsub // 4) * 4096 + (tsub % 4 * 0)

                            def mk_v(t, base=base, tsub=tsub):
                                stp = tsub // 4
                                sub = tsub % 4
                                for kc in range(8):
                                    t.matmul(psV[:, :],
                                             xbuf[:, base + stp * 4096 + kc * 512 + sub * 128:
                                                  base + stp * 4096 + kc * 512 + (sub + 1) * 128],
                                             wv_sb[:, kc * 256:(kc + 1) * 256],
                                             start=(kc == 0), stop=False)
                                return t.matmul(psV[:, :], ones_row[:, :], vb_sb[:, :],
                                                start=False, stop=True)
                            P.emit("tensor", mk_v,
                                   waits=[("WL", wl_c), ("LD", ld_bias),
                                          ("VC", vc_vevac.get(kt - 1, 0))], inc="MM")
                            mm_v = P.n("MM")

                            def mk_vevac(v, kt=kt):
                                for hl in range(4):
                                    r = v.tensor_relu(
                                        v_tm[:, hl * 2080 + kt * 65: hl * 2080 + kt * 65 + 64],
                                        psV[:, hl * 64:(hl + 1) * 64])
                                return r
                            P.emit("vector", mk_vevac, waits=[("MM", mm_v)], inc="VC")
                            vc_vevac[kt] = P.n("VC")
                        mm_vread[c] = P.n("MM")

                    def mk_vones(v):
                        for hl in range(4):
                            r = v.memset(v_tm[:, hl * 2080 + 64: (hl + 1) * 2080: 65], 1.0)
                        return r
                    P.emit("vector", mk_vones, inc="VC")

                P.emit("gpsimd", lambda g: g.dma_start(out=stat_scr[0, :], in_=m_own[:, :]),
                       waits=[("VC", P.n("VC"))], inc="SP", inc_n=16)
                P.emit("gpsimd", lambda g: g.dma_start(out=stat_scr[1, :], in_=rr_own[:, :]),
                       inc="SP", inc_n=16)

                if taps:
                    w = P.barrier_waits()
                    P.emit("gpsimd", lambda g: g.dma_start(out=qtap_d, in_=q_fm[:, :]),
                           waits=tuple(w.items()), inc="SP", inc_n=16)
                    P.emit("gpsimd", lambda g: g.dma_start(out=xntap_d, in_=xn_own[:, :]),
                           inc="SP", inc_n=16)
                    P.emit("gpsimd", lambda g: g.dma_start(out=ktap_d, in_=k_fm[:, :]),
                           inc="SP", inc_n=16)
                    P.emit("gpsimd", lambda g: g.dma_start(out=vtap_d, in_=v_tm[:, :]),
                           inc="SP", inc_n=16)
                    P.emit("gpsimd", lambda g: g.dma_start(
                        out=mtap_d[:, 0:TOK], in_=m_own[:, :]), inc="SP", inc_n=16)
                    P.emit("gpsimd", lambda g: g.dma_start(
                        out=mtap_d[:, TOK:2 * TOK], in_=rr_own[:, :]), inc="SP", inc_n=16)

                full_barrier()
                P.flush(block)

            with (
                nc.sbuf_tensor("ring", [128, 6 * 1536], BF16) as ring,
                nc.sbuf_tensor("av_stage", [64, 4 * 512], F32) as av_stage,
                nc.sbuf_tensor("rec_row", [1, 1024], F32) as rec_row,
                nc.sbuf_tensor("recb_sb", [64, 1024], F32) as recb_sb,
                nc.psum_tensor("psA", [128, 1536], F32) as psA,
                nc.psum_tensor("psBq", [128, 1536], F32) as psBq,
                nc.psum_tensor("psAvA", [128, 512], F32) as psAvA,
                nc.psum_tensor("psAvB", [128, 512], F32) as psAvB,
            ):
                GROUPS = [list(range(3 * i, min(3 * i + 3, 32))) for i in range(11)]
                sc_expA = {}
                sc_expB = {}
                mm_scoreA = {}
                mm_scoreB = {}
                mm_avA = {}
                mm_avB = {}
                vc_avtt_prev = 0
                vc_psA_free = 0
                scA_prev = 0
                scB_prev = 0
                sp_unit = {}
                for u, (pi, qt) in enumerate([(pi, qt) for pi in range(2) for qt in range(16)]):
                    hlA, hlB = 2 * pi, 2 * pi + 1
                    for gi, grp in enumerate(GROUPS):
                        def mk_scoreA(t, pi=pi, qt=qt, grp=grp):
                            for j, kc in enumerate(grp):
                                r = t.matmul(psA[:, j * 512:(j + 1) * 512],
                                             k_fm[0:64, pi * SK + kc * 128: pi * SK + (kc + 1) * 128],
                                             q_fm[0:64, pi * SQ + qt * 512: pi * SQ + (qt + 1) * 512],
                                             start=True, stop=True, tile_position=(0, 0))
                            return r
                        def mk_scoreB(t, pi=pi, qt=qt, grp=grp):
                            for j, kc in enumerate(grp):
                                r = t.matmul(psBq[:, j * 512:(j + 1) * 512],
                                             k_fm[64:128, pi * SK + kc * 128: pi * SK + (kc + 1) * 128],
                                             q_fm[64:128, pi * SQ + qt * 512: pi * SQ + (qt + 1) * 512],
                                             start=True, stop=True, tile_position=(64, 0))
                            return r
                        P.emit("tensor", mk_scoreA,
                               waits=[("SC", sc_expA.get((u, gi - 1), scA_prev)),
                                      ("VC", vc_psA_free)], inc="MM")
                        mm_scoreA[(u, gi)] = P.n("MM")
                        if gi >= 1:
                            gprev = gi - 1
                            def mk_avA(t, gprev=gprev, hlA=hlA):
                                for j, kc in enumerate(GROUPS[gprev]):
                                    sA = (gprev % 3)
                                    r = t.matmul(psAvA[0:65, :],
                                                 v_tm[:, hlA * 2080 + kc * 65: hlA * 2080 + (kc + 1) * 65],
                                                 ring[:, sA * 1536 + j * 512: sA * 1536 + (j + 1) * 512],
                                                 start=(kc == 0), stop=(kc == 31))
                                return r
                            def mk_avB(t, gprev=gprev, hlB=hlB):
                                for j, kc in enumerate(GROUPS[gprev]):
                                    sB = 3 + (gprev % 3)
                                    r = t.matmul(psAvB[0:65, :],
                                                 v_tm[:, hlB * 2080 + kc * 65: hlB * 2080 + (kc + 1) * 65],
                                                 ring[:, sB * 1536 + j * 512: sB * 1536 + (j + 1) * 512],
                                                 start=(kc == 0), stop=(kc == 31))
                                return r
                            P.emit("tensor", mk_avA,
                                   waits=[("SC", sc_expA[(u, gprev)]),
                                          ("VC", vc_avtt_prev)], inc="MM")
                            mm_avA[(u, gprev)] = P.n("MM")
                        P.emit("tensor", mk_scoreB,
                               waits=[("SC", sc_expB.get((u, gi - 1), scB_prev))], inc="MM")
                        mm_scoreB[(u, gi)] = P.n("MM")
                        if gi >= 1:
                            gprev = gi - 1
                            P.emit("tensor", mk_avB,
                                   waits=[("SC", sc_expB[(u, gprev)]),
                                          ("VC", vc_avtt_prev)], inc="MM")
                            mm_avB[(u, gprev)] = P.n("MM")
                        width = len(grp) * 512
                        def mk_expA(sc, gi=gi, width=width):
                            return sc.activation(
                                ring[:, (gi % 3) * 1536: (gi % 3) * 1536 + width],
                                psA[:, 0:width], AF.Exp, scale=0.125)
                        def mk_expB(sc, gi=gi, width=width):
                            return sc.activation(
                                ring[:, (3 + gi % 3) * 1536: (3 + gi % 3) * 1536 + width],
                                psBq[:, 0:width], AF.Exp, scale=0.125)
                        P.emit("scalar", mk_expA,
                               waits=[("MM", max(mm_scoreA[(u, gi)],
                                                 mm_avA.get((u, gi - 3), 0)))], inc="SC")
                        sc_expA[(u, gi)] = P.n("SC")
                        P.emit("scalar", mk_expB,
                               waits=[("MM", max(mm_scoreB[(u, gi)],
                                                 mm_avB.get((u, gi - 3), 0)))], inc="SC")
                        sc_expB[(u, gi)] = P.n("SC")
                    gl = len(GROUPS) - 1
                    def mk_avlA(t, gl=gl, hlA=hlA):
                        for j, kc in enumerate(GROUPS[gl]):
                            sA = (gl % 3)
                            r = t.matmul(psAvA[0:65, :],
                                         v_tm[:, hlA * 2080 + kc * 65: hlA * 2080 + (kc + 1) * 65],
                                         ring[:, sA * 1536 + j * 512: sA * 1536 + (j + 1) * 512],
                                         start=(kc == 0), stop=(kc == 31))
                        return r
                    def mk_avlB(t, gl=gl, hlB=hlB):
                        for j, kc in enumerate(GROUPS[gl]):
                            sB = 3 + (gl % 3)
                            r = t.matmul(psAvB[0:65, :],
                                         v_tm[:, hlB * 2080 + kc * 65: hlB * 2080 + (kc + 1) * 65],
                                         ring[:, sB * 1536 + j * 512: sB * 1536 + (j + 1) * 512],
                                         start=(kc == 0), stop=(kc == 31))
                        return r
                    P.emit("tensor", mk_avlA, waits=[("SC", sc_expA[(u, gl)])], inc="MM")
                    mm_avA[(u, gl)] = P.n("MM")
                    P.emit("tensor", mk_avlB, waits=[("SC", sc_expB[(u, gl)])], inc="MM")
                    mm_avB[(u, gl)] = P.n("MM")
                    def mk_rec(v):
                        v.reciprocal(rec_row[:, 0:512], psAvA[64:65, :])
                        return v.reciprocal(rec_row[:, 512:1024], psAvB[64:65, :])
                    P.emit("vector", mk_rec, waits=[("MM", mm_avB[(u, gl)])], inc="VC")
                    vc_rec = P.n("VC")
                    def mk_recb(t):
                        t.matmul(psA[0:64, 0:512], ones_row[:, 0:64], rec_row[:, 0:512],
                                 start=True, stop=True)
                        return t.matmul(psA[0:64, 512:1024], ones_row[:, 0:64], rec_row[:, 512:1024],
                                        start=True, stop=True)
                    P.emit("tensor", mk_recb,
                           waits=[("VC", vc_rec), ("SC", sc_expB[(u, gl)])], inc="MM")
                    mm_recb = P.n("MM")
                    def mk_recb_copy(v):
                        v.tensor_copy(recb_sb[:, 0:512], psA[0:64, 0:512])
                        return v.tensor_copy(recb_sb[:, 512:1024], psA[0:64, 512:1024])
                    P.emit("vector", mk_recb_copy, waits=[("MM", mm_recb)], inc="VC")
                    vc_rcopy = P.n("VC")
                    ping = u % 2
                    def mk_avtt(v, ping=ping):
                        v.tensor_mul(av_stage[:, ping * 1024: ping * 1024 + 512],
                                     psAvA[0:64, :], recb_sb[:, 0:512])
                        return v.tensor_mul(av_stage[:, ping * 1024 + 512: ping * 1024 + 1024],
                                            psAvB[0:64, :], recb_sb[:, 512:1024])
                    P.emit("vector", mk_avtt, waits=[("SP", sp_unit.get(u - 2, 0))], inc="VC")
                    vc_avtt_prev = P.n("VC")
                    vc_psA_free = vc_rcopy
                    scA_prev = sc_expA[(u, gl)]
                    scB_prev = sc_expB[(u, gl)]
                    def mk_spillA(g, qt=qt, ping=ping, hlA=hlA):
                        return g.dma_start(out=av_scr[hlA, :, qt * 512:(qt + 1) * 512],
                                           in_=av_stage[:, ping * 1024: ping * 1024 + 512])
                    def mk_spillB(g, qt=qt, ping=ping, hlB=hlB):
                        return g.dma_start(out=av_scr[hlB, :, qt * 512:(qt + 1) * 512],
                                           in_=av_stage[:, ping * 1024 + 512: ping * 1024 + 1024])
                    P.emit("gpsimd", mk_spillA, waits=[("VC", vc_avtt_prev)], inc="SP", inc_n=16)
                    P.emit("gpsimd", mk_spillB, inc="SP", inc_n=16)
                    sp_unit[u] = P.n("SP")

                sc_p2 = P.n("SC")
                vc_p2 = P.n("VC")
                mm_p2 = P.n("MM")
                sp_p2 = P.n("SP")
                P.flush(block)

        with (
            nc.sbuf_tensor("g_sb", [128, 32 * 1024], BF16) as g_sb,
            nc.sbuf_tensor("t1_sb", [128, 2 * 1024], F32) as t1_sb,
            nc.sbuf_tensor("w1buf", [128, 2 * 1024], BF16) as w1buf,
            nc.sbuf_tensor("w2buf", [128, 2 * 1024], BF16) as w2buf,
            nc.sbuf_tensor("hbuf", [128, 2 * 4096], BF16) as hbuf,
        ):
            sc_gate = sc_p2
            vc_gate = vc_p2
            sp_ffn = {}
            for hf in range(2):
                with (
                    nc.psum_tensor(f"psW1_{hf}", [128, 2048], F32) as psW1,
                    nc.psum_tensor(f"psW2_{hf}", [128, 2048], F32) as psW2,
                ):
                    mm_a = {}
                    sc_a = {}
                    vc_a = {}
                    for m in range(32):
                        slot = m % 2
                        wld = [("MM", mm_a.get(m - 2, 0))]
                        if hf == 0 and m == 0:
                            wld.append(("SP", sp_p2))
                        P.emit("gpsimd", lambda g, m=m, slot=slot: g.dma_start(
                            out=w1buf[:, slot * 1024:(slot + 1) * 1024], in_=w1_d[m, :, :]),
                            waits=wld, inc="WL", inc_n=16)
                        P.emit("gpsimd", lambda g, m=m, slot=slot: g.dma_start(
                            out=w2buf[:, slot * 1024:(slot + 1) * 1024], in_=w2_d[m, :, :]),
                            inc="WL", inc_n=16)
                        wl_m = P.n("WL")
                        def mk_w12(t, slot=slot, hf=hf):
                            for nt in range(2):
                                for kc in range(8):
                                    t.matmul(psW1[:, slot * 1024 + nt * 512: slot * 1024 + (nt + 1) * 512],
                                             w1buf[:, slot * 1024 + kc * 128: slot * 1024 + (kc + 1) * 128],
                                             xn_own[:, kc * TOK + hf * 1024 + nt * 512:
                                                    kc * TOK + hf * 1024 + (nt + 1) * 512],
                                             start=(kc == 0), stop=(kc == 7))
                            for nt in range(2):
                                for kc in range(8):
                                    r = t.matmul(psW2[:, slot * 1024 + nt * 512: slot * 1024 + (nt + 1) * 512],
                                                 w2buf[:, slot * 1024 + kc * 128: slot * 1024 + (kc + 1) * 128],
                                                 xn_own[:, kc * TOK + hf * 1024 + nt * 512:
                                                        kc * TOK + hf * 1024 + (nt + 1) * 512],
                                                 start=(kc == 0), stop=(kc == 7))
                            return r
                        P.emit("tensor", mk_w12,
                               waits=[("WL", wl_m), ("SC", max(sc_a.get(m - 2, 0), sc_gate)),
                                      ("VC", max(vc_a.get(m - 2, 0), vc_gate))], inc="MM")
                        mm_a[m] = P.n("MM")
                        P.emit("scalar", lambda sc, m=m, slot=slot: sc.activation(
                            t1_sb[:, slot * 1024:(slot + 1) * 1024],
                            psW1[:, slot * 1024:(slot + 1) * 1024], AF.Silu,
                            scale=s1_sb[:, m:m + 1]),
                            waits=[("MM", mm_a[m])], inc="SC")
                        sc_a[m] = P.n("SC")
                        P.emit("vector", lambda v, m=m, slot=slot: v.scalar_tensor_tensor(
                            g_sb[:, m * 1024:(m + 1) * 1024],
                            psW2[:, slot * 1024:(slot + 1) * 1024], s2_sb[:, m:m + 1],
                            t1_sb[:, slot * 1024:(slot + 1) * 1024], ALU.mult, ALU.mult),
                            waits=[("SC", sc_a[m])], inc="VC")
                        vc_a[m] = P.n("VC")
                    sc_gate = sc_a[31]
                    vc_gate = vc_a[31]
                    if taps and hf == 0:
                        P.emit("gpsimd", lambda g: g.dma_start(out=gtap_d, in_=g_sb[:, :]),
                               waits=[("VC", vc_gate)], inc="SP", inc_n=16)

                with nc.sbuf_tensor(f"hm_sb_{hf}", [128, 32 * 1024], BF16) as hm_sb:
                    with nc.psum_tensor(f"psH_{hf}", [128, 2048], F32) as psH:
                        mm_hb_prev = {}
                        for stage, (wd, src, dst, ssb) in enumerate((
                                (h1_d, g_sb, hm_sb, sh1_sb),
                                (h2_d, hm_sb, g_sb, sh2_sb))):
                            if stage == 1:
                                sc_gate = sc_b[31]
                                mm_hb_prev = mm_b
                            mm_b = {}
                            sc_b = {}
                            for m in range(32):
                                slot = m % 2
                                P.emit("gpsimd", lambda g, m=m, slot=slot, wd=wd: g.dma_start(
                                    out=hbuf[:, slot * 4096:(slot + 1) * 4096], in_=wd[m, :, :]),
                                    waits=[("MM", max(mm_b.get(m - 2, 0),
                                                      mm_hb_prev.get(30 + m, 0)))],
                                    inc="WL", inc_n=16)
                                wl_m = P.n("WL")
                                def mk_h(t, slot=slot, src=src):
                                    for nt in range(2):
                                        for kc in range(32):
                                            r = t.matmul(
                                                psH[:, slot * 1024 + nt * 512: slot * 1024 + (nt + 1) * 512],
                                                hbuf[:, slot * 4096 + kc * 128: slot * 4096 + (kc + 1) * 128],
                                                src[:, kc * 1024 + nt * 512: kc * 1024 + (nt + 1) * 512],
                                                start=(kc == 0), stop=(kc == 31))
                                    return r
                                P.emit("tensor", mk_h,
                                       waits=[("WL", wl_m),
                                              ("SC", max(sc_b.get(m - 2, 0), sc_gate)),
                                              ("VC", vc_gate)], inc="MM")
                                mm_b[m] = P.n("MM")
                                P.emit("scalar", lambda sc, m=m, slot=slot, dst=dst, ssb=ssb: sc.activation(
                                    dst[:, m * 1024:(m + 1) * 1024],
                                    psH[:, slot * 1024:(slot + 1) * 1024], AF.Gelu,
                                    scale=ssb[:, m:m + 1]),
                                    waits=[("MM", mm_b[m])], inc="SC")
                                sc_b[m] = P.n("SC")
                            if taps and hf == 0:
                                tap = hmtap_d if stage == 0 else g2tap_d
                                P.emit("gpsimd", lambda g, tap=tap, src=dst: g.dma_start(
                                    out=tap, in_=src[:, :]),
                                    waits=[("SC", sc_b[31])], inc="SP", inc_n=16)
                            sc_gate = sc_b[31]

                with (
                    nc.sbuf_tensor(f"ffn_fm_{hf}", [128, 8192], F32) as ffn_fm,
                    nc.sbuf_tensor(f"ffn_stage_{hf}", [128, 2 * 1024], F32) as ffn_stage,
                ):
                    with nc.psum_tensor(f"psW3_{hf}", [128, 2048], F32) as psW3:
                        mm_d = {}
                        vc_d = {}
                        for m3 in range(8):
                            slot = m3 % 2
                            P.emit("gpsimd", lambda g, m3=m3, slot=slot: g.dma_start(
                                out=hbuf[:, slot * 4096:(slot + 1) * 4096], in_=w3_d[m3, :, :]),
                                waits=[("MM", max(mm_d.get(m3 - 2, 0),
                                                  mm_b.get(30 + m3, 0)))], inc="WL", inc_n=16)
                            wl_m = P.n("WL")
                            def mk_w3(t, slot=slot):
                                for nt in range(2):
                                    for kc in range(32):
                                        r = t.matmul(
                                            psW3[:, slot * 1024 + nt * 512: slot * 1024 + (nt + 1) * 512],
                                            hbuf[:, slot * 4096 + kc * 128: slot * 4096 + (kc + 1) * 128],
                                            g_sb[:, kc * 1024 + nt * 512: kc * 1024 + (nt + 1) * 512],
                                            start=(kc == 0), stop=(kc == 31))
                                return r
                            P.emit("tensor", mk_w3,
                                   waits=[("WL", wl_m),
                                          ("SC", sc_gate),
                                          ("VC", max(vc_d.get(m3 - 2, 0), vc_gate))], inc="MM")
                            mm_d[m3] = P.n("MM")
                            P.emit("vector", lambda v, m3=m3, slot=slot: v.tensor_scalar_mul(
                                ffn_fm[:, m3 * 1024:(m3 + 1) * 1024],
                                psW3[:, slot * 1024:(slot + 1) * 1024], s3_sb[:, m3:m3 + 1]),
                                waits=[("MM", mm_d[m3])], inc="VC")
                            vc_d[m3] = P.n("VC")
                        vc_gate = vc_d[7]

                    with nc.psum_tensor(f"psT_{hf}", [128, 2048], F32) as psT:
                        vc_e = {}
                        sp_e = {}
                        for tt in range(8):
                            slot = tt % 2
                            def mk_tr(t, tt=tt, slot=slot):
                                for fc in range(8):
                                    r = t.transpose(
                                        psT[:, slot * 1024 + fc * 128: slot * 1024 + (fc + 1) * 128],
                                        ffn_fm[:, fc * 1024 + tt * 128: fc * 1024 + (tt + 1) * 128],
                                        ident[:, :])
                                return r
                            P.emit("tensor", mk_tr,
                                   waits=[("VC", max(vc_e.get(tt - 2, 0), vc_gate))], inc="MM")
                            mm_tt = P.n("MM")
                            P.emit("vector", lambda v, slot=slot: v.tensor_copy(
                                ffn_stage[:, slot * 1024:(slot + 1) * 1024],
                                psT[:, slot * 1024:(slot + 1) * 1024]),
                                waits=[("MM", mm_tt), ("SP", sp_e.get(tt - 2, 0))], inc="VC")
                            vc_e[tt] = P.n("VC")
                            P.emit("gpsimd", lambda g, tt=tt, slot=slot, hf=hf: g.dma_start(
                                out=ffn_scr[hf * 1024 + tt * 128: hf * 1024 + (tt + 1) * 128, :],
                                in_=ffn_stage[:, slot * 1024:(slot + 1) * 1024]),
                                waits=[("VC", vc_e[tt])], inc="SP", inc_n=16)
                            sp_e[tt] = P.n("SP")
                        vc_gate = vc_e[7]
                        sc_gate = P.n("SC")
                        sp_ffn[hf] = P.n("SP")
            P.flush(block)

        with (
            nc.sbuf_tensor("xtile", [128, 2 * 1024], F32) as xtile,
            nc.sbuf_tensor("ffntile", [128, 2 * 1024], F32) as ffntile,
            nc.sbuf_tensor("avtile", [128, 2 * 1024], F32) as avtile,
            nc.sbuf_tensor("x2t", [128, 1024], F32) as x2t,
            nc.sbuf_tensor("outstage", [128, 2 * 1024], F32) as outstage,
            nc.sbuf_tensor("cols", [128, 8], F32) as cols,
            nc.sbuf_tensor("junk", [128, 1024], F32) as junk,
        ):
            def filler(v):
                return v.tensor_copy(junk[:, 0:512], junk[:, 512:1024])
            full_barrier()
            P.emit("vector", lambda v: v.memset(junk[:, :], 0.0), inc="VC")

            vc_out = {}
            st_out = {}
            for ot in range(16):
                slot = ot % 2
                hl = ot // 4
                d0 = 16 * (ot % 4)
                w = [("VC", vc_out.get(ot - 2, 0)), ("SP", sp_ffn[ot // 8])]
                P.emit("sync", lambda sy, ot=ot, slot=slot: sy.dma_start(
                    out=xtile[:, slot * 1024:(slot + 1) * 1024],
                    in_=xtm_d[ot * 128:(ot + 1) * 128, :]), waits=w, inc="LD", inc_n=16)
                P.emit("sync", lambda sy, ot=ot, slot=slot: sy.dma_start(
                    out=ffntile[:, slot * 1024:(slot + 1) * 1024],
                    in_=ffn_scr[ot * 128:(ot + 1) * 128, :]), inc="LD", inc_n=16)
                P.emit("sync", lambda sy, slot=slot, hl=hl, d0=d0: sy.dma_start(
                    out=avtile[:, slot * 1024:(slot + 1) * 1024],
                    in_=av_scr[hl, d0:d0 + 16, :].rearrange("i (j c) -> (i j) c", c=1024)),
                    inc="LD", inc_n=16)
                ld_t = P.n("LD")
                P.emit("vector", lambda v, slot=slot: v.reduce_sum(
                    cols[:, 0:1], xtile[:, slot * 1024:(slot + 1) * 1024],
                    axis=mybir.AxisListType.X), waits=[("LD", ld_t)], inc="VC")
                vc_s1 = P.n("VC")
                P.emit("scalar", lambda sc, slot=slot: sc.activation(
                    x2t[:, :], xtile[:, slot * 1024:(slot + 1) * 1024], AF.Square),
                    waits=[("LD", ld_t), ("VC", vc_s1)], inc="SC")
                sc_sq = P.n("SC")
                def mk_stats4(v, slot=slot):
                    v.reduce_sum(cols[:, 1:2], x2t[:, :], axis=mybir.AxisListType.X)
                    v.tensor_scalar_mul(cols[:, 2:3], cols[:, 0:1], 1.0 / 1024)
                    v.tensor_scalar_mul(cols[:, 3:4], cols[:, 1:2], 1.0 / 1024)
                    filler(v)
                    v.tensor_mul(cols[:, 4:5], cols[:, 2:3], cols[:, 2:3])
                    filler(v)
                    v.tensor_sub(cols[:, 3:4], cols[:, 3:4], cols[:, 4:5])
                    filler(v)
                    return v.tensor_scalar_add(cols[:, 3:4], cols[:, 3:4], 1e-5)
                P.emit("vector", mk_stats4, waits=[("SC", sc_sq)], inc="VC")
                vc_v = P.n("VC")
                P.emit("scalar", lambda sc: sc.activation(
                    cols[:, 4:5], cols[:, 3:4], AF.Sqrt), waits=[("VC", vc_v)], inc="SC")
                sc_sd = P.n("SC")
                def mk_r4(v):
                    v.reciprocal(cols[:, 5:6], cols[:, 4:5])
                    filler(v)
                    return v.scalar_tensor_tensor(cols[:, 6:7], cols[:, 2:3], -1.0,
                                                  cols[:, 5:6], ALU.mult, ALU.mult)
                P.emit("vector", mk_r4, waits=[("SC", sc_sd)], inc="VC")
                vc_r = P.n("VC")
                P.emit("scalar", lambda sc, slot=slot: sc.activation(
                    x2t[:, :], xtile[:, slot * 1024:(slot + 1) * 1024], AF.Identity,
                    bias=cols[:, 6:7], scale=cols[:, 5:6]), waits=[("VC", vc_r)], inc="SC")
                sc_xn = P.n("SC")
                def mk_out(v, slot=slot):
                    v.tensor_add(outstage[:, slot * 1024:(slot + 1) * 1024],
                                 xtile[:, slot * 1024:(slot + 1) * 1024], x2t[:, :])
                    v.tensor_add(outstage[:, slot * 1024:(slot + 1) * 1024],
                                 outstage[:, slot * 1024:(slot + 1) * 1024],
                                 avtile[:, slot * 1024:(slot + 1) * 1024])
                    return v.tensor_add(outstage[:, slot * 1024:(slot + 1) * 1024],
                                        outstage[:, slot * 1024:(slot + 1) * 1024],
                                        ffntile[:, slot * 1024:(slot + 1) * 1024])
                P.emit("vector", mk_out,
                       waits=[("SC", sc_xn), ("ST", st_out.get(ot - 2, 0))], inc="VC")
                vc_out[ot] = P.n("VC")
                P.emit("gpsimd", lambda g, ot=ot, slot=slot: g.dma_start(
                    out=out_d[ot * 128:(ot + 1) * 128, :],
                    in_=outstage[:, slot * 1024:(slot + 1) * 1024]),
                    waits=[("VC", vc_out[ot])], inc="ST", inc_n=16)
                st_out[ot] = P.n("ST")
            P.emit("gpsimd", lambda g: g.nop(), waits=[("ST", P.n("ST"))])
            P.flush(block)

    return nc


def _prep_inputs(inputs):
    x = np.ascontiguousarray(inputs["L0_attn_queries"])
    keys = np.ascontiguousarray(inputs["L0_attn_keys"][0])
    vals = np.ascontiguousarray(inputs["L0_attn_values"][0])

    def strips(aT, nstrip):
        return np.ascontiguousarray(
            aT.reshape(8, 128, nstrip, 512).transpose(2, 1, 0, 3))

    kft = strips(keys.T, KSTRIP)
    vft = strips(vals.T, KSTRIP)

    def wtile(wT, KD, MD):
        return np.ascontiguousarray(
            wT.reshape(KD // 128, 128, MD // 128, 128).transpose(2, 1, 0, 3)
        ).reshape(MD // 128, 128, KD)

    w1p = wtile(inputs["L0_ffn_w1_w"].T, D, DHID)
    w2p = wtile(inputs["L0_ffn_w2_w"].T, D, DHID)
    h1p = wtile(inputs["L0_ffn_h1_w"].T, DHID, DHID)
    h2p = wtile(inputs["L0_ffn_h2_w"].T, DHID, DHID)
    w3p = wtile(inputs["L0_ffn_w3_w"].T, DHID, D)
    scol = lambda s, n: np.ascontiguousarray(np.asarray(s).reshape(n, 128).T)
    s1c = scol(inputs["L0_ffn_w1_s"], 32)
    s2c = scol(inputs["L0_ffn_w2_s"], 32)
    sh1c = scol(inputs["L0_ffn_h1_s"], 32)
    sh2c = scol(inputs["L0_ffn_h2_s"], 32)
    s3c = scol(inputs["L0_ffn_w3_s"], 8)

    WqT = np.ascontiguousarray(inputs["L0_attn_Wq_w"].T)
    WkT = np.ascontiguousarray(inputs["L0_attn_Wk_w"].T)
    WvT = np.ascontiguousarray(inputs["L0_attn_Wv_w"].T)

    in_maps = []
    for c in range(8):
        b, g = c // 4, c % 4
        xT = x[b].T
        xfm = strips(xT, NSTRIP)
        wslice = lambda WT: np.ascontiguousarray(
            WT[:, 256 * g:256 * (g + 1)].reshape(8, 128, 256).transpose(1, 0, 2)
        ).reshape(128, 2048)
        bias_col = lambda bvec: np.ascontiguousarray(
            np.asarray(bvec)[256 * g:256 * (g + 1)].reshape(2, 128).T)
        in_maps.append(dict(
            xfm=np.ascontiguousarray(xfm),
            xofm=np.ascontiguousarray(xfm[4 * g: 4 * g + 4]),
            xtm=np.ascontiguousarray(x[b, TOK * g: TOK * (g + 1)]),
            kft=kft, vft=vft,
            wq=wslice(WqT), wk=wslice(WkT), wv=wslice(WvT),
            qb=bias_col(inputs["L0_attn_Wq_b"]),
            kb=bias_col(inputs["L0_attn_Wk_b"]),
            vb=np.ascontiguousarray(
                np.asarray(inputs["L0_attn_Wv_b"])[256 * g:256 * (g + 1)].reshape(1, 256)),
            w1p=w1p, w2p=w2p, h1p=h1p, h2p=h2p, w3p=w3p,
            s1c=s1c, s2c=s2c, sh1c=sh1c, sh2c=sh2c, s3c=s3c,
        ))
    return in_maps


def kernel(**inputs):
    in_maps = _prep_inputs(inputs)
    nc = build_nc(taps=False)
    res = run_bass_kernel_spmd(nc, in_maps, core_ids=list(range(8)))
    out = np.empty((2, SQ, D), dtype=np.float32)
    for c in range(8):
        b, g = c // 4, c % 4
        out[b, TOK * g: TOK * (g + 1)] = res.results[c]["out"]
    return out


# version 10 (final)
# speedup vs baseline: 1.3151x; 1.1641x over previous
import numpy as np
import concourse.bass as bass
import concourse.mybir as mybir
from concourse.bass_utils import run_bass_kernel_spmd

F32 = mybir.dt.float32
BF16 = mybir.dt.bfloat16
AF = mybir.ActivationFunctionType
ALU = mybir.AluOpType

D = 1024
SQ = 8192
SK = 4096
TOK = 2048
NSTRIP = 16
KSTRIP = 8
DHID = 4096

ENGINES = ("sync", "gpsimd", "tensor", "scalar", "vector")
SEMS = ("LD", "WL", "SP", "ST", "MM", "SC", "VC")


class Plan:

    def __init__(self, sems):
        self.sems = sems
        self.ev = {e: [] for e in ENGINES}
        self.cnt = {k: 0 for k in SEMS}

    def n(self, sem):
        return self.cnt[sem]

    def emit(self, eng, fn, waits=(), inc=None, inc_n=1):
        ws = tuple((s, v) for (s, v) in waits if v > 0)
        self.ev[eng].append((fn, ws, inc, inc_n))
        if inc is not None:
            self.cnt[inc] += inc_n

    def barrier_waits(self):
        return {s: self.cnt[s] for s in SEMS}

    def flush(self, block):
        for e in ENGINES:
            evs = self.ev[e]
            if not evs:
                continue
            self.ev[e] = []

            def body(eng, evs=evs):
                last_wait = {}
                for fn, waits, inc, inc_n in evs:
                    for s, v in waits:
                        if last_wait.get(s, -1) < v:
                            eng.wait_ge(self.sems[s], v)
                            last_wait[s] = v
                    r = fn(eng)
                    if inc is not None:
                        r.then_inc(self.sems[inc], inc_n)
            getattr(block, e)(body)


def build_nc(taps=False):
    nc = bass.Bass("TRN2", target_bir_lowering=False)

    xfm_d = nc.dram_tensor("xfm", [NSTRIP, 128, 8, 512], F32, kind="ExternalInput").ap()
    xtm_d = nc.dram_tensor("xtm", [TOK, D], F32, kind="ExternalInput").ap()
    xofm_d = nc.dram_tensor("xofm", [4, 128, 8, 512], F32, kind="ExternalInput").ap()
    kft_d = nc.dram_tensor("kft", [KSTRIP, 128, 8, 512], F32, kind="ExternalInput").ap()
    vft_d = nc.dram_tensor("vft", [KSTRIP, 128, 8, 512], F32, kind="ExternalInput").ap()
    wq_d = nc.dram_tensor("wq", [128, 2048], F32, kind="ExternalInput").ap()
    wk_d = nc.dram_tensor("wk", [128, 2048], F32, kind="ExternalInput").ap()
    wv_d = nc.dram_tensor("wv", [128, 2048], F32, kind="ExternalInput").ap()
    qb_d = nc.dram_tensor("qb", [128, 2], F32, kind="ExternalInput").ap()
    kb_d = nc.dram_tensor("kb", [128, 2], F32, kind="ExternalInput").ap()
    vb_d = nc.dram_tensor("vb", [1, 256], F32, kind="ExternalInput").ap()
    w1_d = nc.dram_tensor("w1p", [32, 128, 1024], F32, kind="ExternalInput").ap()
    w2_d = nc.dram_tensor("w2p", [32, 128, 1024], F32, kind="ExternalInput").ap()
    h1_d = nc.dram_tensor("h1p", [32, 128, 4096], F32, kind="ExternalInput").ap()
    h2_d = nc.dram_tensor("h2p", [32, 128, 4096], F32, kind="ExternalInput").ap()
    w3_d = nc.dram_tensor("w3p", [8, 128, 4096], F32, kind="ExternalInput").ap()
    s1_d = nc.dram_tensor("s1c", [128, 32], F32, kind="ExternalInput").ap()
    s2_d = nc.dram_tensor("s2c", [128, 32], F32, kind="ExternalInput").ap()
    sh1_d = nc.dram_tensor("sh1c", [128, 32], F32, kind="ExternalInput").ap()
    sh2_d = nc.dram_tensor("sh2c", [128, 32], F32, kind="ExternalInput").ap()
    s3_d = nc.dram_tensor("s3c", [128, 8], F32, kind="ExternalInput").ap()

    out_d = nc.dram_tensor("out", [TOK, D], F32, kind="ExternalOutput").ap()
    scratch_kind = "ExternalOutput" if taps else "Internal"
    av_scr = nc.dram_tensor("av_scr", [4, 64, SQ], F32, kind=scratch_kind).ap()
    stat_scr = nc.dram_tensor("stat_scr", [2, TOK], F32, kind="Internal").ap()
    ffn_scr = nc.dram_tensor("ffn_scr", [TOK, D], F32, kind=scratch_kind).ap()
    if taps:
        qtap_d = nc.dram_tensor("q_tap", [128, 2 * SQ], BF16, kind="ExternalOutput").ap()
        xntap_d = nc.dram_tensor("xn_tap", [128, 8 * TOK], BF16, kind="ExternalOutput").ap()
        ktap_d = nc.dram_tensor("k_tap", [128, 2 * SK], BF16, kind="ExternalOutput").ap()
        vtap_d = nc.dram_tensor("v_tap", [128, 4 * 32 * 65], BF16, kind="ExternalOutput").ap()
        mtap_d = nc.dram_tensor("m_tap", [1, 2 * TOK], F32, kind="ExternalOutput").ap()
        gtap_d = nc.dram_tensor("g_tap", [128, 32 * 1024], BF16, kind="ExternalOutput").ap()
        hmtap_d = nc.dram_tensor("hm_tap", [128, 32 * 1024], BF16, kind="ExternalOutput").ap()
        g2tap_d = nc.dram_tensor("g2_tap", [128, 32 * 1024], BF16, kind="ExternalOutput").ap()
        ctap_d = nc.dram_tensor("c_tap", [128, 48], F32, kind="ExternalOutput").ap()

    from contextlib import ExitStack
    with ExitStack() as ctx:
        sems = {s: ctx.enter_context(nc.semaphore(s)) for s in SEMS}
        P = Plan(sems)

        xn_own = ctx.enter_context(nc.sbuf_tensor("xn_own", [128, 8 * TOK], BF16))
        ident = ctx.enter_context(nc.sbuf_tensor("ident", [128, 128], F32))
        ones_col = ctx.enter_context(nc.sbuf_tensor("ones_col", [128, 1], BF16))
        ones_row = ctx.enter_context(nc.sbuf_tensor("ones_row", [1, 128], F32))
        eps1 = ctx.enter_context(nc.sbuf_tensor("eps1", [1, 1], F32))
        qb_sb = ctx.enter_context(nc.sbuf_tensor("qb_sb", [128, 2], F32))
        kb_sb = ctx.enter_context(nc.sbuf_tensor("kb_sb", [128, 2], F32))
        vb_sb = ctx.enter_context(nc.sbuf_tensor("vb_sb", [1, 256], F32))
        s1_sb = ctx.enter_context(nc.sbuf_tensor("s1_sb", [128, 32], F32))
        s2_sb = ctx.enter_context(nc.sbuf_tensor("s2_sb", [128, 32], F32))
        sh1_sb = ctx.enter_context(nc.sbuf_tensor("sh1_sb", [128, 32], F32))
        sh2_sb = ctx.enter_context(nc.sbuf_tensor("sh2_sb", [128, 32], F32))
        s3_sb = ctx.enter_context(nc.sbuf_tensor("s3_sb", [128, 8], F32))
        ones2 = ctx.enter_context(nc.sbuf_tensor("ones2", [128, 128], F32))

        block = ctx.enter_context(nc.Block())

        def full_barrier():
            w = P.barrier_waits()
            waits = tuple(w.items())
            for e in ENGINES:
                P.emit(e, lambda eng: eng.nop(), waits=waits)

        P.emit("sync", lambda sy: sy.dma_start(out=qb_sb[:, :], in_=qb_d), inc="LD", inc_n=16)
        P.emit("sync", lambda sy: sy.dma_start(out=kb_sb[:, :], in_=kb_d), inc="LD", inc_n=16)
        P.emit("sync", lambda sy: sy.dma_start(out=vb_sb[:, :], in_=vb_d), inc="LD", inc_n=16)
        P.emit("sync", lambda sy: sy.dma_start(out=s1_sb[:, :], in_=s1_d), inc="LD", inc_n=16)
        P.emit("sync", lambda sy: sy.dma_start(out=s2_sb[:, :], in_=s2_d), inc="LD", inc_n=16)
        P.emit("sync", lambda sy: sy.dma_start(out=sh1_sb[:, :], in_=sh1_d), inc="LD", inc_n=16)
        P.emit("sync", lambda sy: sy.dma_start(out=sh2_sb[:, :], in_=sh2_d), inc="LD", inc_n=16)
        P.emit("sync", lambda sy: sy.dma_start(out=s3_sb[:, :], in_=s3_d), inc="LD", inc_n=16)
        P.emit("vector", lambda v: v.memset(ones_col[:, :], 1.0), inc="VC")
        P.emit("vector", lambda v: v.memset(ones_row[:, :], 1.0), inc="VC")
        P.emit("vector", lambda v: v.memset(eps1[:, :], 1e-5), inc="VC")
        P.emit("vector", lambda v: v.memset(ones2[:, :], 1.0), inc="VC")
        P.emit("gpsimd", lambda g: g.memset(ident[:, :], 0.0), inc="WL")
        P.emit("gpsimd", lambda g: g.affine_select(
            out=ident[:, :], in_=ident[:, :], compare_op=ALU.not_equal,
            fill=1.0, base=0, pattern=[[-1, 128]], channel_multiplier=1), inc="WL")
        ld_bias = P.n("LD")
        vc_init = P.n("VC")

        with (
            nc.sbuf_tensor("q_fm", [128, 2 * SQ], BF16) as q_fm,
            nc.sbuf_tensor("k_fm", [128, 2 * SK], BF16) as k_fm,
            nc.sbuf_tensor("v_tm", [128, 4 * 32 * 65], BF16) as v_tm,
        ):
            with (
                nc.sbuf_tensor("xbuf", [128, 2 * 8192], BF16) as xbuf,
                nc.sbuf_tensor("x2buf", [128, 8192], BF16) as x2buf,
                nc.sbuf_tensor("wq_sb", [128, 2048], BF16) as wq_sb,
                nc.sbuf_tensor("wk_sb", [128, 2048], BF16) as wk_sb,
                nc.sbuf_tensor("wv_sb", [128, 2048], BF16) as wv_sb,
                nc.sbuf_tensor("rows", [128, 6144], F32) as rows,
                nc.sbuf_tensor("rows2", [128, 2 * TOK], F32) as rows2,
                nc.sbuf_tensor("rbc", [128, 2 * 512], BF16) as rbc,
                nc.sbuf_tensor("mrbc", [128, 2 * 512], BF16) as mrbc,
                nc.sbuf_tensor("tmpb", [128, 512], BF16) as tmpb,
                nc.psum_tensor("psQ", [128, 1024], F32) as psQ,
            ):
                m_own = rows2[0:1, 0:TOK]
                rr_own = rows2[0:1, TOK:2 * TOK]
                P.emit("gpsimd", lambda g: g.dma_start(out=wq_sb[:, :], in_=wq_d), inc="WL", inc_n=16)
                P.emit("gpsimd", lambda g: g.dma_start(out=wk_sb[:, :], in_=wk_d), inc="WL", inc_n=16)
                P.emit("gpsimd", lambda g: g.dma_start(out=wv_sb[:, :], in_=wv_d), inc="WL", inc_n=16)
                wl_w = P.n("WL")

                mm_last_reader = {}
                vc_evacS = {}
                mm_sums = {}
                sc_qevac = {}
                strip_ev = {}

                with (
                    nc.psum_tensor("psS", [1, 2048], F32) as psS,
                    nc.psum_tensor("psB", [128, 1024], F32) as psB,
                ):
                    def emit_chain(c):
                        own = c >= 8
                        slot = c % 2
                        base = slot * 8192
                        srcs = []
                        for st in range(2):
                            if own:
                                srcs.append(xofm_d[2 * (c - 8) + st, :, :, :].rearrange(
                                    "p k c -> p (k c)"))
                            else:
                                srcs.append(xfm_d[2 * c + st, :, :, :].rearrange(
                                    "p k c -> p (k c)"))
                        P.emit("gpsimd", lambda g, src=srcs[0], base=base: g.dma_start(
                            out=xbuf[:, base:base + 4096], in_=src),
                            waits=[("MM", mm_last_reader.get(c - 2, 0))], inc="WL", inc_n=16)
                        P.emit("gpsimd", lambda g, src=srcs[1], base=base: g.dma_start(
                            out=xbuf[:, base + 4096:base + 8192], in_=src),
                            inc="WL", inc_n=16)
                        wl_c = P.n("WL")
                        P.emit("scalar", lambda sc, base=base: sc.activation(
                            x2buf[:, :], xbuf[:, base:base + 8192], AF.Square),
                            waits=[("WL", wl_c), ("MM", mm_sums.get(c - 1, 0))], inc="SC")
                        sc_x2 = P.n("SC")

                        def mk_sums(t, base=base):
                            for st in range(2):
                                for ch in range(8):
                                    t.matmul(psS[:, st * 512:(st + 1) * 512], ones_col[:, :],
                                             xbuf[:, base + st * 4096 + ch * 512:
                                                  base + st * 4096 + (ch + 1) * 512],
                                             start=(ch == 0), stop=(ch == 7))
                            for st in range(2):
                                for ch in range(8):
                                    r = t.matmul(psS[:, 1024 + st * 512: 1024 + (st + 1) * 512],
                                                 ones_col[:, :],
                                                 x2buf[:, st * 4096 + ch * 512:
                                                       st * 4096 + (ch + 1) * 512],
                                                 start=(ch == 0), stop=(ch == 7))
                            return r
                        P.emit("tensor", mk_sums,
                               waits=[("WL", wl_c), ("SC", sc_x2), ("VC", vc_evacS.get(c - 1, 0))],
                               inc="MM")
                        mm_sums[c] = P.n("MM")
                        P.emit("vector", lambda v, slot=slot: v.tensor_copy(
                            rows[0:1, slot * 2048:(slot + 1) * 2048], psS[:, :]),
                            waits=[("MM", mm_sums[c])], inc="VC")
                        vc_evacS[c] = P.n("VC")

                        def mk_stats(v, slot=slot, own=own, c=c):
                            sb_ = slot * 2048
                            so_ = slot * 1024
                            mdst = m_own[:, (c - 8) * 1024:(c - 7) * 1024] if own \
                                else rows[64:65, so_:so_ + 1024]
                            msq = rows[32:33, so_:so_ + 1024]
                            ex2 = rows[32:33, 2048 + so_:2048 + so_ + 1024]
                            var = rows[32:33, 4096 + so_:4096 + so_ + 1024]
                            v.tensor_scalar_mul(mdst, rows[0:1, sb_:sb_ + 1024], 1.0 / 1024)
                            v.tensor_scalar_mul(ex2, rows[0:1, sb_ + 1024:sb_ + 2048], 1.0 / 1024)
                            v.tensor_mul(msq, mdst, mdst)
                            v.tensor_sub(var, ex2, msq)
                            return v.tensor_scalar_add(var, var, 1e-5)
                        P.emit("vector", mk_stats, waits=[("VC", vc_init)], inc="VC")
                        vc_stats = P.n("VC")
                        P.emit("scalar", lambda sc, slot=slot: sc.activation(
                            rows[64:65, 2048 + slot * 1024: 2048 + (slot + 1) * 1024],
                            rows[32:33, 4096 + slot * 1024: 4096 + (slot + 1) * 1024], AF.Sqrt),
                            waits=[("VC", vc_stats)], inc="SC")
                        sc_sd = P.n("SC")

                        def mk_rr(v, slot=slot, own=own, c=c):
                            so_ = slot * 1024
                            rdst = rr_own[:, (c - 8) * 1024:(c - 7) * 1024] if own \
                                else rows[64:65, 4096 + so_:4096 + so_ + 1024]
                            mdst = m_own[:, (c - 8) * 1024:(c - 7) * 1024] if own \
                                else rows[64:65, so_:so_ + 1024]
                            v.reciprocal(rdst, rows[64:65, 2048 + so_:2048 + so_ + 1024])
                            return v.tensor_mul(rows[0:1, 4096 + so_:4096 + so_ + 1024],
                                                mdst, rdst)
                        P.emit("vector", mk_rr, waits=[("SC", sc_sd)], inc="VC")
                        vc_rr = P.n("VC")

                        mm_last = 0
                        for st in range(2):
                            sg = (c - 8) * 2 + st if own else 2 * c + st
                            s2_ = (2 * c + st) % 2

                            def mk_bcast(t, slot=slot, st=st, own=own, c=c):
                                so_ = slot * 1024
                                if own:
                                    rsrc = rr_own[:, (c - 8) * 1024 + st * 512:
                                                  (c - 8) * 1024 + (st + 1) * 512]
                                    lh = ones_row[:, :]
                                else:
                                    rsrc = rows[64:65, 4096 + so_ + st * 512:
                                                4096 + so_ + (st + 1) * 512]
                                    lh = ones2[64:65, 0:128]
                                t.matmul(psB[:, 0:512], lh, rsrc, start=True, stop=True)
                                return t.matmul(psB[:, 512:1024], ones_row[:, :],
                                                rows[0:1, 4096 + so_ + st * 512:
                                                     4096 + so_ + (st + 1) * 512],
                                                start=True, stop=True)
                            P.emit("tensor", mk_bcast,
                                   waits=[("VC", vc_rr), ("VC", strip_ev.get(("evB", c, st - 1),
                                          strip_ev.get(("evB", c - 1, 1), 0)))], inc="MM")
                            mm_bc = P.n("MM")
                            P.emit("vector", lambda v, s2_=s2_: (
                                v.tensor_copy(rbc[:, s2_ * 512:(s2_ + 1) * 512], psB[:, 0:512]),
                                v.tensor_copy(mrbc[:, s2_ * 512:(s2_ + 1) * 512],
                                              psB[:, 512:1024]))[-1],
                                waits=[("MM", mm_bc)], inc="VC")
                            strip_ev[("evB", c, st)] = P.n("VC")

                            def mk_xn(v, base=base, st=st, s2_=s2_, own=own, c=c):
                                for ch in range(8):
                                    xsl = xbuf[:, base + st * 4096 + ch * 512:
                                               base + st * 4096 + (ch + 1) * 512]
                                    v.tensor_mul(tmpb[:, :], xsl, rbc[:, s2_ * 512:(s2_ + 1) * 512])
                                    if own:
                                        tpos = (c - 8) * 1024 + st * 512
                                        dst = xn_own[:, ch * TOK + tpos: ch * TOK + tpos + 512]
                                    else:
                                        dst = xsl
                                    r = v.tensor_sub(dst, tmpb[:, :],
                                                     mrbc[:, s2_ * 512:(s2_ + 1) * 512])
                                return r
                            P.emit("vector", mk_xn, waits=[("VC", strip_ev[("evB", c, st)])],
                                   inc="VC")
                            vc_xn = P.n("VC")
                            strip_ev[("xn", c, st)] = vc_xn
                            if not own:
                                def mk_q(t, base=base, st=st):
                                    for mc in range(2):
                                        for kc in range(8):
                                            r = t.matmul(
                                                psQ[:, mc * 512:(mc + 1) * 512],
                                                wq_sb[:, kc * 256 + mc * 128: kc * 256 + (mc + 1) * 128],
                                                xbuf[:, base + st * 4096 + kc * 512:
                                                     base + st * 4096 + (kc + 1) * 512],
                                                start=(kc == 0), stop=(kc == 7))
                                    return r
                                P.emit("tensor", mk_q,
                                       waits=[("VC", vc_xn), ("WL", wl_w),
                                              ("SC", sc_qevac.get(sg - 1, 0))], inc="MM")
                                mm_last = P.n("MM")

                                def mk_qevac(sc, sg=sg):
                                    for mc in range(2):
                                        r = sc.activation(
                                            q_fm[:, mc * SQ + sg * 512: mc * SQ + (sg + 1) * 512],
                                            psQ[:, mc * 512:(mc + 1) * 512], AF.Relu,
                                            bias=qb_sb[:, mc:mc + 1])
                                    return r
                                P.emit("scalar", mk_qevac,
                                       waits=[("MM", mm_last), ("LD", ld_bias)], inc="SC")
                                sc_qevac[sg] = P.n("SC")
                        mm_last_reader[c] = P.n("MM") if own else mm_last
                        if own:
                            mm_last_reader[c] = mm_sums[c]
                            strip_ev[("ownxn", c)] = P.n("VC")


                    for c in range(10):
                        emit_chain(c)

                    own_xn_vc = P.n("VC")
                    sc_q_all = P.n("SC")

                sc_kevac = {}
                mm_kread = {}
                with nc.psum_tensor("psV", [128, 256], F32) as psV:
                    for c in range(4):
                        slot = c % 2
                        base = slot * 8192
                        w = [("MM", mm_kread.get(c - 2, 0))]
                        if c < 2:
                            w = [("VC", own_xn_vc), ("MM", mm_sums[8 + c])]
                        P.emit("gpsimd", lambda g, c=c, base=base: g.dma_start(
                            out=xbuf[:, base:base + 4096],
                            in_=kft_d[2 * c, :, :, :].rearrange("p k c -> p (k c)")),
                            waits=w, inc="WL", inc_n=16)
                        P.emit("gpsimd", lambda g, c=c, base=base: g.dma_start(
                            out=xbuf[:, base + 4096:base + 8192],
                            in_=kft_d[2 * c + 1, :, :, :].rearrange("p k c -> p (k c)")),
                            inc="WL", inc_n=16)
                        wl_c = P.n("WL")
                        for st in range(2):
                            sg = 2 * c + st

                            def mk_k(t, base=base, st=st):
                                for mc in range(2):
                                    for kc in range(8):
                                        r = t.matmul(
                                            psQ[:, mc * 512:(mc + 1) * 512],
                                            wk_sb[:, kc * 256 + mc * 128: kc * 256 + (mc + 1) * 128],
                                            xbuf[:, base + st * 4096 + kc * 512:
                                                 base + st * 4096 + (kc + 1) * 512],
                                            start=(kc == 0), stop=(kc == 7))
                                return r
                            P.emit("tensor", mk_k,
                                   waits=[("WL", wl_c),
                                          ("SC", sc_kevac.get(sg - 1, sc_q_all))], inc="MM")
                            mm_k = P.n("MM")

                            def mk_kevac(sc, sg=sg):
                                for mc in range(2):
                                    r = sc.activation(
                                        k_fm[:, mc * SK + sg * 512: mc * SK + (sg + 1) * 512],
                                        psQ[:, mc * 512:(mc + 1) * 512], AF.Relu,
                                        bias=kb_sb[:, mc:mc + 1])
                                return r
                            P.emit("scalar", mk_kevac, waits=[("MM", mm_k)], inc="SC")
                            sc_kevac[sg] = P.n("SC")
                        mm_kread[c] = P.n("MM")

                    vc_vevac = {}
                    mm_vread = {}
                    for c in range(4):
                        slot = c % 2
                        base = slot * 8192
                        w = [("MM", mm_vread.get(c - 2, mm_kread.get(2 + c, 0)))]
                        P.emit("gpsimd", lambda g, c=c, base=base: g.dma_start(
                            out=xbuf[:, base:base + 4096],
                            in_=vft_d[2 * c, :, :, :].rearrange("p k c -> p (k c)")),
                            waits=w, inc="WL", inc_n=16)
                        P.emit("gpsimd", lambda g, c=c, base=base: g.dma_start(
                            out=xbuf[:, base + 4096:base + 8192],
                            in_=vft_d[2 * c + 1, :, :, :].rearrange("p k c -> p (k c)")),
                            inc="WL", inc_n=16)
                        wl_c = P.n("WL")
                        for tsub in range(8):
                            kt = c * 8 + tsub
                            off = base + (tsub // 4) * 4096 + (tsub % 4 * 0)

                            def mk_v(t, base=base, tsub=tsub):
                                stp = tsub // 4
                                sub = tsub % 4
                                for kc in range(8):
                                    t.matmul(psV[:, :],
                                             xbuf[:, base + stp * 4096 + kc * 512 + sub * 128:
                                                  base + stp * 4096 + kc * 512 + (sub + 1) * 128],
                                             wv_sb[:, kc * 256:(kc + 1) * 256],
                                             start=(kc == 0), stop=False)
                                return t.matmul(psV[:, :], ones_row[:, :], vb_sb[:, :],
                                                start=False, stop=True)
                            P.emit("tensor", mk_v,
                                   waits=[("WL", wl_c), ("LD", ld_bias),
                                          ("VC", vc_vevac.get(kt - 1, 0))], inc="MM")
                            mm_v = P.n("MM")

                            def mk_vevac(v, kt=kt):
                                for hl in range(4):
                                    r = v.tensor_relu(
                                        v_tm[:, hl * 2080 + kt * 65: hl * 2080 + kt * 65 + 64],
                                        psV[:, hl * 64:(hl + 1) * 64])
                                return r
                            P.emit("vector", mk_vevac, waits=[("MM", mm_v)], inc="VC")
                            vc_vevac[kt] = P.n("VC")
                        mm_vread[c] = P.n("MM")

                    def mk_vones(v):
                        for hl in range(4):
                            r = v.memset(v_tm[:, hl * 2080 + 64: (hl + 1) * 2080: 65], 1.0)
                        return r
                    P.emit("vector", mk_vones, inc="VC")

                P.emit("gpsimd", lambda g: g.dma_start(out=stat_scr[0, :], in_=m_own[:, :]),
                       waits=[("VC", P.n("VC"))], inc="SP", inc_n=16)
                P.emit("gpsimd", lambda g: g.dma_start(out=stat_scr[1, :], in_=rr_own[:, :]),
                       inc="SP", inc_n=16)

                if taps:
                    w = P.barrier_waits()
                    P.emit("gpsimd", lambda g: g.dma_start(out=qtap_d, in_=q_fm[:, :]),
                           waits=tuple(w.items()), inc="SP", inc_n=16)
                    P.emit("gpsimd", lambda g: g.dma_start(out=xntap_d, in_=xn_own[:, :]),
                           inc="SP", inc_n=16)
                    P.emit("gpsimd", lambda g: g.dma_start(out=ktap_d, in_=k_fm[:, :]),
                           inc="SP", inc_n=16)
                    P.emit("gpsimd", lambda g: g.dma_start(out=vtap_d, in_=v_tm[:, :]),
                           inc="SP", inc_n=16)
                    P.emit("gpsimd", lambda g: g.dma_start(
                        out=mtap_d[:, 0:TOK], in_=m_own[:, :]), inc="SP", inc_n=16)
                    P.emit("gpsimd", lambda g: g.dma_start(
                        out=mtap_d[:, TOK:2 * TOK], in_=rr_own[:, :]), inc="SP", inc_n=16)

                full_barrier()
                P.flush(block)

            with (
                nc.sbuf_tensor("ring", [128, 6 * 1536], BF16) as ring,
                nc.sbuf_tensor("av_stage", [64, 4 * 512], F32) as av_stage,
                nc.sbuf_tensor("rec_row", [1, 1024], F32) as rec_row,
                nc.sbuf_tensor("recb_sb", [64, 1024], F32) as recb_sb,
                nc.psum_tensor("psA", [128, 1536], F32) as psA,
                nc.psum_tensor("psBq", [128, 1536], F32) as psBq,
                nc.psum_tensor("psAvA", [128, 512], F32) as psAvA,
                nc.psum_tensor("psAvB", [128, 512], F32) as psAvB,
            ):
                GROUPS = [list(range(3 * i, min(3 * i + 3, 32))) for i in range(11)]
                sc_expA = {}
                sc_expB = {}
                mm_scoreA = {}
                mm_scoreB = {}
                mm_avA = {}
                mm_avB = {}
                vc_avtt_prev = 0
                vc_rcopy_prev = 0
                scA_prev = 0
                scB_prev = 0
                sp_unit = {}

                def emit_tail(u, pi, qt, hlA, hlB, sc_guard):
                    nonlocal vc_avtt_prev, vc_rcopy_prev
                    gl = len(GROUPS) - 1
                    def mk_avlA(t, gl=gl, hlA=hlA):
                        for j, kc in enumerate(GROUPS[gl]):
                            sA = (gl % 3)
                            r = t.matmul(psAvA[0:65, :],
                                         v_tm[:, hlA * 2080 + kc * 65: hlA * 2080 + (kc + 1) * 65],
                                         ring[:, sA * 1536 + j * 512: sA * 1536 + (j + 1) * 512],
                                         start=(kc == 0), stop=(kc == 31))
                        return r
                    def mk_avlB(t, gl=gl, hlB=hlB):
                        for j, kc in enumerate(GROUPS[gl]):
                            sB = 3 + (gl % 3)
                            r = t.matmul(psAvB[0:65, :],
                                         v_tm[:, hlB * 2080 + kc * 65: hlB * 2080 + (kc + 1) * 65],
                                         ring[:, sB * 1536 + j * 512: sB * 1536 + (j + 1) * 512],
                                         start=(kc == 0), stop=(kc == 31))
                        return r
                    P.emit("tensor", mk_avlA, waits=[("SC", sc_expA[(u, gl)])], inc="MM")
                    mm_avA[(u, gl)] = P.n("MM")
                    P.emit("tensor", mk_avlB, waits=[("SC", sc_expB[(u, gl)])], inc="MM")
                    mm_avB[(u, gl)] = P.n("MM")
                    def mk_rec(v):
                        v.reciprocal(rec_row[:, 0:512], psAvA[64:65, :])
                        return v.reciprocal(rec_row[:, 512:1024], psAvB[64:65, :])
                    P.emit("vector", mk_rec, waits=[("MM", mm_avB[(u, gl)])], inc="VC")
                    vc_rec = P.n("VC")
                    def mk_recb(t):
                        t.matmul(psA[0:64, 0:512], ones_row[:, 0:64], rec_row[:, 0:512],
                                 start=True, stop=True)
                        return t.matmul(psA[0:64, 512:1024], ones_row[:, 0:64], rec_row[:, 512:1024],
                                        start=True, stop=True)
                    P.emit("tensor", mk_recb,
                           waits=[("VC", vc_rec), ("SC", sc_guard)], inc="MM")
                    mm_recb = P.n("MM")
                    def mk_recb_copy(v):
                        v.tensor_copy(recb_sb[:, 0:512], psA[0:64, 0:512])
                        return v.tensor_copy(recb_sb[:, 512:1024], psA[0:64, 512:1024])
                    P.emit("vector", mk_recb_copy, waits=[("MM", mm_recb)], inc="VC")
                    vc_rcopy_prev = P.n("VC")
                    ping = u % 2
                    def mk_avtt(v, ping=ping):
                        v.tensor_mul(av_stage[:, ping * 1024: ping * 1024 + 512],
                                     psAvA[0:64, :], recb_sb[:, 0:512])
                        return v.tensor_mul(av_stage[:, ping * 1024 + 512: ping * 1024 + 1024],
                                            psAvB[0:64, :], recb_sb[:, 512:1024])
                    P.emit("vector", mk_avtt, waits=[("SP", sp_unit.get(u - 2, 0))], inc="VC")
                    vc_avtt_prev = P.n("VC")
                    def mk_spillA(g, qt=qt, ping=ping, hlA=hlA):
                        return g.dma_start(out=av_scr[hlA, :, qt * 512:(qt + 1) * 512],
                                           in_=av_stage[:, ping * 1024: ping * 1024 + 512])
                    def mk_spillB(g, qt=qt, ping=ping, hlB=hlB):
                        return g.dma_start(out=av_scr[hlB, :, qt * 512:(qt + 1) * 512],
                                           in_=av_stage[:, ping * 1024 + 512: ping * 1024 + 1024])
                    P.emit("gpsimd", mk_spillA, waits=[("VC", vc_avtt_prev)], inc="SP", inc_n=16)
                    P.emit("gpsimd", mk_spillB, inc="SP", inc_n=16)
                    sp_unit[u] = P.n("SP")

                pending_tail = None
                for u, (pi, qt) in enumerate([(pi, qt) for pi in range(2) for qt in range(16)]):
                    hlA, hlB = 2 * pi, 2 * pi + 1
                    for gi, grp in enumerate(GROUPS):
                        def mk_scoreA(t, pi=pi, qt=qt, grp=grp):
                            for j, kc in enumerate(grp):
                                r = t.matmul(psA[:, j * 512:(j + 1) * 512],
                                             k_fm[0:64, pi * SK + kc * 128: pi * SK + (kc + 1) * 128],
                                             q_fm[0:64, pi * SQ + qt * 512: pi * SQ + (qt + 1) * 512],
                                             start=True, stop=True, tile_position=(0, 0))
                            return r
                        def mk_scoreB(t, pi=pi, qt=qt, grp=grp):
                            for j, kc in enumerate(grp):
                                r = t.matmul(psBq[:, j * 512:(j + 1) * 512],
                                             k_fm[64:128, pi * SK + kc * 128: pi * SK + (kc + 1) * 128],
                                             q_fm[64:128, pi * SQ + qt * 512: pi * SQ + (qt + 1) * 512],
                                             start=True, stop=True, tile_position=(64, 0))
                            return r
                        P.emit("tensor", mk_scoreA,
                               waits=[("SC", sc_expA.get((u, gi - 1), scA_prev)),
                                      ("VC", vc_rcopy_prev)], inc="MM")
                        mm_scoreA[(u, gi)] = P.n("MM")
                        P.emit("tensor", mk_scoreB,
                               waits=[("SC", sc_expB.get((u, gi - 1), scB_prev))], inc="MM")
                        mm_scoreB[(u, gi)] = P.n("MM")
                        if gi >= 1:
                            gprev = gi - 1
                            def mk_avA(t, gprev=gprev, hlA=hlA):
                                for j, kc in enumerate(GROUPS[gprev]):
                                    sA = (gprev % 3)
                                    r = t.matmul(psAvA[0:65, :],
                                                 v_tm[:, hlA * 2080 + kc * 65: hlA * 2080 + (kc + 1) * 65],
                                                 ring[:, sA * 1536 + j * 512: sA * 1536 + (j + 1) * 512],
                                                 start=(kc == 0), stop=(kc == 31))
                                return r
                            def mk_avB(t, gprev=gprev, hlB=hlB):
                                for j, kc in enumerate(GROUPS[gprev]):
                                    sB = 3 + (gprev % 3)
                                    r = t.matmul(psAvB[0:65, :],
                                                 v_tm[:, hlB * 2080 + kc * 65: hlB * 2080 + (kc + 1) * 65],
                                                 ring[:, sB * 1536 + j * 512: sB * 1536 + (j + 1) * 512],
                                                 start=(kc == 0), stop=(kc == 31))
                                return r
                            P.emit("tensor", mk_avA,
                                   waits=[("SC", sc_expA[(u, gprev)]),
                                          ("VC", vc_avtt_prev)], inc="MM")
                            mm_avA[(u, gprev)] = P.n("MM")
                            P.emit("tensor", mk_avB,
                                   waits=[("SC", sc_expB[(u, gprev)]),
                                          ("VC", vc_avtt_prev)], inc="MM")
                            mm_avB[(u, gprev)] = P.n("MM")
                        width = len(grp) * 512
                        def mk_expA(sc, gi=gi, width=width):
                            return sc.activation(
                                ring[:, (gi % 3) * 1536: (gi % 3) * 1536 + width],
                                psA[:, 0:width], AF.Exp, scale=0.125)
                        def mk_expB(sc, gi=gi, width=width):
                            return sc.activation(
                                ring[:, (3 + gi % 3) * 1536: (3 + gi % 3) * 1536 + width],
                                psBq[:, 0:width], AF.Exp, scale=0.125)
                        P.emit("scalar", mk_expA,
                               waits=[("MM", max(mm_scoreA[(u, gi)],
                                                 mm_avA.get((u, gi - 3), 0)))], inc="SC")
                        sc_expA[(u, gi)] = P.n("SC")
                        P.emit("scalar", mk_expB,
                               waits=[("MM", max(mm_scoreB[(u, gi)],
                                                 mm_avB.get((u, gi - 3), 0)))], inc="SC")
                        sc_expB[(u, gi)] = P.n("SC")
                        if gi == 0 and pending_tail is not None:
                            emit_tail(*pending_tail, sc_guard=sc_expA[(u, 0)])
                            pending_tail = None
                    scA_prev = sc_expA[(u, len(GROUPS) - 1)]
                    scB_prev = sc_expB[(u, len(GROUPS) - 1)]
                    pending_tail = (u, pi, qt, hlA, hlB)

                emit_tail(*pending_tail, sc_guard=scA_prev)
                pending_tail = None

                sc_p2 = P.n("SC")
                vc_p2 = P.n("VC")
                mm_p2 = P.n("MM")
                sp_p2 = P.n("SP")
                P.flush(block)

        with (
            nc.sbuf_tensor("g_sb", [128, 32 * 1024], BF16) as g_sb,
            nc.sbuf_tensor("t1_sb", [128, 2 * 1024], F32) as t1_sb,
            nc.sbuf_tensor("w1buf", [128, 2 * 1024], BF16) as w1buf,
            nc.sbuf_tensor("w2buf", [128, 2 * 1024], BF16) as w2buf,
            nc.sbuf_tensor("hbuf", [128, 2 * 4096], BF16) as hbuf,
        ):
            sc_gate = sc_p2
            vc_gate = vc_p2
            sp_ffn = {}
            for hf in range(2):
                with (
                    nc.psum_tensor(f"psW1_{hf}", [128, 2048], F32) as psW1,
                    nc.psum_tensor(f"psW2_{hf}", [128, 2048], F32) as psW2,
                ):
                    mm_a = {}
                    sc_a = {}
                    vc_a = {}
                    for m in range(32):
                        slot = m % 2
                        wld = [("MM", mm_a.get(m - 2, 0))]
                        if hf == 0 and m == 0:
                            wld.append(("SP", sp_p2))
                        P.emit("gpsimd", lambda g, m=m, slot=slot: g.dma_start(
                            out=w1buf[:, slot * 1024:(slot + 1) * 1024], in_=w1_d[m, :, :]),
                            waits=wld, inc="WL", inc_n=16)
                        P.emit("gpsimd", lambda g, m=m, slot=slot: g.dma_start(
                            out=w2buf[:, slot * 1024:(slot + 1) * 1024], in_=w2_d[m, :, :]),
                            inc="WL", inc_n=16)
                        wl_m = P.n("WL")
                        def mk_w12(t, slot=slot, hf=hf):
                            for nt in range(2):
                                for kc in range(8):
                                    t.matmul(psW1[:, slot * 1024 + nt * 512: slot * 1024 + (nt + 1) * 512],
                                             w1buf[:, slot * 1024 + kc * 128: slot * 1024 + (kc + 1) * 128],
                                             xn_own[:, kc * TOK + hf * 1024 + nt * 512:
                                                    kc * TOK + hf * 1024 + (nt + 1) * 512],
                                             start=(kc == 0), stop=(kc == 7))
                            for nt in range(2):
                                for kc in range(8):
                                    r = t.matmul(psW2[:, slot * 1024 + nt * 512: slot * 1024 + (nt + 1) * 512],
                                                 w2buf[:, slot * 1024 + kc * 128: slot * 1024 + (kc + 1) * 128],
                                                 xn_own[:, kc * TOK + hf * 1024 + nt * 512:
                                                        kc * TOK + hf * 1024 + (nt + 1) * 512],
                                                 start=(kc == 0), stop=(kc == 7))
                            return r
                        P.emit("tensor", mk_w12,
                               waits=[("WL", wl_m), ("SC", max(sc_a.get(m - 2, 0), sc_gate)),
                                      ("VC", max(vc_a.get(m - 2, 0), vc_gate))], inc="MM")
                        mm_a[m] = P.n("MM")
                        P.emit("scalar", lambda sc, m=m, slot=slot: sc.activation(
                            t1_sb[:, slot * 1024:(slot + 1) * 1024],
                            psW1[:, slot * 1024:(slot + 1) * 1024], AF.Silu,
                            scale=s1_sb[:, m:m + 1]),
                            waits=[("MM", mm_a[m])], inc="SC")
                        sc_a[m] = P.n("SC")
                        P.emit("vector", lambda v, m=m, slot=slot: v.scalar_tensor_tensor(
                            g_sb[:, m * 1024:(m + 1) * 1024],
                            psW2[:, slot * 1024:(slot + 1) * 1024], s2_sb[:, m:m + 1],
                            t1_sb[:, slot * 1024:(slot + 1) * 1024], ALU.mult, ALU.mult),
                            waits=[("SC", sc_a[m])], inc="VC")
                        vc_a[m] = P.n("VC")
                    sc_gate = sc_a[31]
                    vc_gate = vc_a[31]
                    if taps and hf == 0:
                        P.emit("gpsimd", lambda g: g.dma_start(out=gtap_d, in_=g_sb[:, :]),
                               waits=[("VC", vc_gate)], inc="SP", inc_n=16)

                with nc.sbuf_tensor(f"hm_sb_{hf}", [128, 32 * 1024], BF16) as hm_sb:
                    with nc.psum_tensor(f"psH_{hf}", [128, 2048], F32) as psH:
                        mm_hb_prev = {}
                        for stage, (wd, src, dst, ssb) in enumerate((
                                (h1_d, g_sb, hm_sb, sh1_sb),
                                (h2_d, hm_sb, g_sb, sh2_sb))):
                            if stage == 1:
                                sc_gate = sc_b[31]
                                mm_hb_prev = mm_b
                            mm_b = {}
                            sc_b = {}
                            for m in range(32):
                                slot = m % 2
                                P.emit("gpsimd", lambda g, m=m, slot=slot, wd=wd: g.dma_start(
                                    out=hbuf[:, slot * 4096:(slot + 1) * 4096], in_=wd[m, :, :]),
                                    waits=[("MM", max(mm_b.get(m - 2, 0),
                                                      mm_hb_prev.get(30 + m, 0)))],
                                    inc="WL", inc_n=16)
                                wl_m = P.n("WL")
                                def mk_h(t, slot=slot, src=src):
                                    for nt in range(2):
                                        for kc in range(32):
                                            r = t.matmul(
                                                psH[:, slot * 1024 + nt * 512: slot * 1024 + (nt + 1) * 512],
                                                hbuf[:, slot * 4096 + kc * 128: slot * 4096 + (kc + 1) * 128],
                                                src[:, kc * 1024 + nt * 512: kc * 1024 + (nt + 1) * 512],
                                                start=(kc == 0), stop=(kc == 31))
                                    return r
                                P.emit("tensor", mk_h,
                                       waits=[("WL", wl_m),
                                              ("SC", max(sc_b.get(m - 2, 0), sc_gate)),
                                              ("VC", vc_gate)], inc="MM")
                                mm_b[m] = P.n("MM")
                                P.emit("scalar", lambda sc, m=m, slot=slot, dst=dst, ssb=ssb: sc.activation(
                                    dst[:, m * 1024:(m + 1) * 1024],
                                    psH[:, slot * 1024:(slot + 1) * 1024], AF.Gelu,
                                    scale=ssb[:, m:m + 1]),
                                    waits=[("MM", mm_b[m])], inc="SC")
                                sc_b[m] = P.n("SC")
                            if taps and hf == 0:
                                tap = hmtap_d if stage == 0 else g2tap_d
                                P.emit("gpsimd", lambda g, tap=tap, src=dst: g.dma_start(
                                    out=tap, in_=src[:, :]),
                                    waits=[("SC", sc_b[31])], inc="SP", inc_n=16)
                            sc_gate = sc_b[31]

                with (
                    nc.sbuf_tensor(f"ffn_fm_{hf}", [128, 8192], F32) as ffn_fm,
                    nc.sbuf_tensor(f"ffn_stage_{hf}", [128, 2 * 1024], F32) as ffn_stage,
                ):
                    with nc.psum_tensor(f"psW3_{hf}", [128, 2048], F32) as psW3:
                        mm_d = {}
                        vc_d = {}
                        for m3 in range(8):
                            slot = m3 % 2
                            P.emit("gpsimd", lambda g, m3=m3, slot=slot: g.dma_start(
                                out=hbuf[:, slot * 4096:(slot + 1) * 4096], in_=w3_d[m3, :, :]),
                                waits=[("MM", max(mm_d.get(m3 - 2, 0),
                                                  mm_b.get(30 + m3, 0)))], inc="WL", inc_n=16)
                            wl_m = P.n("WL")
                            def mk_w3(t, slot=slot):
                                for nt in range(2):
                                    for kc in range(32):
                                        r = t.matmul(
                                            psW3[:, slot * 1024 + nt * 512: slot * 1024 + (nt + 1) * 512],
                                            hbuf[:, slot * 4096 + kc * 128: slot * 4096 + (kc + 1) * 128],
                                            g_sb[:, kc * 1024 + nt * 512: kc * 1024 + (nt + 1) * 512],
                                            start=(kc == 0), stop=(kc == 31))
                                return r
                            P.emit("tensor", mk_w3,
                                   waits=[("WL", wl_m),
                                          ("SC", sc_gate),
                                          ("VC", max(vc_d.get(m3 - 2, 0), vc_gate))], inc="MM")
                            mm_d[m3] = P.n("MM")
                            P.emit("vector", lambda v, m3=m3, slot=slot: v.tensor_scalar_mul(
                                ffn_fm[:, m3 * 1024:(m3 + 1) * 1024],
                                psW3[:, slot * 1024:(slot + 1) * 1024], s3_sb[:, m3:m3 + 1]),
                                waits=[("MM", mm_d[m3])], inc="VC")
                            vc_d[m3] = P.n("VC")
                        vc_gate = vc_d[7]

                    with nc.psum_tensor(f"psT_{hf}", [128, 2048], F32) as psT:
                        vc_e = {}
                        sp_e = {}
                        for tt in range(8):
                            slot = tt % 2
                            def mk_tr(t, tt=tt, slot=slot):
                                for fc in range(8):
                                    r = t.transpose(
                                        psT[:, slot * 1024 + fc * 128: slot * 1024 + (fc + 1) * 128],
                                        ffn_fm[:, fc * 1024 + tt * 128: fc * 1024 + (tt + 1) * 128],
                                        ident[:, :])
                                return r
                            P.emit("tensor", mk_tr,
                                   waits=[("VC", max(vc_e.get(tt - 2, 0), vc_gate))], inc="MM")
                            mm_tt = P.n("MM")
                            P.emit("vector", lambda v, slot=slot: v.tensor_copy(
                                ffn_stage[:, slot * 1024:(slot + 1) * 1024],
                                psT[:, slot * 1024:(slot + 1) * 1024]),
                                waits=[("MM", mm_tt), ("SP", sp_e.get(tt - 2, 0))], inc="VC")
                            vc_e[tt] = P.n("VC")
                            P.emit("gpsimd", lambda g, tt=tt, slot=slot, hf=hf: g.dma_start(
                                out=ffn_scr[hf * 1024 + tt * 128: hf * 1024 + (tt + 1) * 128, :],
                                in_=ffn_stage[:, slot * 1024:(slot + 1) * 1024]),
                                waits=[("VC", vc_e[tt])], inc="SP", inc_n=16)
                            sp_e[tt] = P.n("SP")
                        vc_gate = vc_e[7]
                        sc_gate = P.n("SC")
                        sp_ffn[hf] = P.n("SP")
            P.flush(block)

        with (
            nc.sbuf_tensor("xtile", [128, 2 * 1024], F32) as xtile,
            nc.sbuf_tensor("ffntile", [128, 2 * 1024], F32) as ffntile,
            nc.sbuf_tensor("avtile", [128, 2 * 1024], F32) as avtile,
            nc.sbuf_tensor("x2t", [128, 1024], F32) as x2t,
            nc.sbuf_tensor("outstage", [128, 2 * 1024], F32) as outstage,
            nc.sbuf_tensor("cols", [128, 8], F32) as cols,
            nc.sbuf_tensor("junk", [128, 1024], F32) as junk,
        ):
            def filler(v):
                return v.tensor_copy(junk[:, 0:512], junk[:, 512:1024])
            full_barrier()
            P.emit("vector", lambda v: v.memset(junk[:, :], 0.0), inc="VC")

            vc_out = {}
            st_out = {}
            for ot in range(16):
                slot = ot % 2
                hl = ot // 4
                d0 = 16 * (ot % 4)
                w = [("VC", vc_out.get(ot - 2, 0)), ("SP", sp_ffn[ot // 8])]
                P.emit("sync", lambda sy, ot=ot, slot=slot: sy.dma_start(
                    out=xtile[:, slot * 1024:(slot + 1) * 1024],
                    in_=xtm_d[ot * 128:(ot + 1) * 128, :]), waits=w, inc="LD", inc_n=16)
                P.emit("sync", lambda sy, ot=ot, slot=slot: sy.dma_start(
                    out=ffntile[:, slot * 1024:(slot + 1) * 1024],
                    in_=ffn_scr[ot * 128:(ot + 1) * 128, :]), inc="LD", inc_n=16)
                P.emit("sync", lambda sy, slot=slot, hl=hl, d0=d0: sy.dma_start(
                    out=avtile[:, slot * 1024:(slot + 1) * 1024],
                    in_=av_scr[hl, d0:d0 + 16, :].rearrange("i (j c) -> (i j) c", c=1024)),
                    inc="LD", inc_n=16)
                ld_t = P.n("LD")
                P.emit("vector", lambda v, slot=slot: v.reduce_sum(
                    cols[:, 0:1], xtile[:, slot * 1024:(slot + 1) * 1024],
                    axis=mybir.AxisListType.X), waits=[("LD", ld_t)], inc="VC")
                vc_s1 = P.n("VC")
                P.emit("scalar", lambda sc, slot=slot: sc.activation(
                    x2t[:, :], xtile[:, slot * 1024:(slot + 1) * 1024], AF.Square),
                    waits=[("LD", ld_t), ("VC", vc_s1)], inc="SC")
                sc_sq = P.n("SC")
                def mk_stats4(v, slot=slot):
                    v.reduce_sum(cols[:, 1:2], x2t[:, :], axis=mybir.AxisListType.X)
                    v.tensor_scalar_mul(cols[:, 2:3], cols[:, 0:1], 1.0 / 1024)
                    v.tensor_scalar_mul(cols[:, 3:4], cols[:, 1:2], 1.0 / 1024)
                    filler(v)
                    v.tensor_mul(cols[:, 4:5], cols[:, 2:3], cols[:, 2:3])
                    filler(v)
                    v.tensor_sub(cols[:, 3:4], cols[:, 3:4], cols[:, 4:5])
                    filler(v)
                    return v.tensor_scalar_add(cols[:, 3:4], cols[:, 3:4], 1e-5)
                P.emit("vector", mk_stats4, waits=[("SC", sc_sq)], inc="VC")
                vc_v = P.n("VC")
                P.emit("scalar", lambda sc: sc.activation(
                    cols[:, 4:5], cols[:, 3:4], AF.Sqrt), waits=[("VC", vc_v)], inc="SC")
                sc_sd = P.n("SC")
                def mk_r4(v):
                    v.reciprocal(cols[:, 5:6], cols[:, 4:5])
                    filler(v)
                    return v.scalar_tensor_tensor(cols[:, 6:7], cols[:, 2:3], -1.0,
                                                  cols[:, 5:6], ALU.mult, ALU.mult)
                P.emit("vector", mk_r4, waits=[("SC", sc_sd)], inc="VC")
                vc_r = P.n("VC")
                P.emit("scalar", lambda sc, slot=slot: sc.activation(
                    x2t[:, :], xtile[:, slot * 1024:(slot + 1) * 1024], AF.Identity,
                    bias=cols[:, 6:7], scale=cols[:, 5:6]), waits=[("VC", vc_r)], inc="SC")
                sc_xn = P.n("SC")
                def mk_out(v, slot=slot):
                    v.tensor_add(outstage[:, slot * 1024:(slot + 1) * 1024],
                                 xtile[:, slot * 1024:(slot + 1) * 1024], x2t[:, :])
                    v.tensor_add(outstage[:, slot * 1024:(slot + 1) * 1024],
                                 outstage[:, slot * 1024:(slot + 1) * 1024],
                                 avtile[:, slot * 1024:(slot + 1) * 1024])
                    return v.tensor_add(outstage[:, slot * 1024:(slot + 1) * 1024],
                                        outstage[:, slot * 1024:(slot + 1) * 1024],
                                        ffntile[:, slot * 1024:(slot + 1) * 1024])
                P.emit("vector", mk_out,
                       waits=[("SC", sc_xn), ("ST", st_out.get(ot - 2, 0))], inc="VC")
                vc_out[ot] = P.n("VC")
                P.emit("gpsimd", lambda g, ot=ot, slot=slot: g.dma_start(
                    out=out_d[ot * 128:(ot + 1) * 128, :],
                    in_=outstage[:, slot * 1024:(slot + 1) * 1024]),
                    waits=[("VC", vc_out[ot])], inc="ST", inc_n=16)
                st_out[ot] = P.n("ST")
            P.emit("gpsimd", lambda g: g.nop(), waits=[("ST", P.n("ST"))])
            P.flush(block)

    return nc


def _prep_inputs(inputs):
    x = np.ascontiguousarray(inputs["L0_attn_queries"])
    keys = np.ascontiguousarray(inputs["L0_attn_keys"][0])
    vals = np.ascontiguousarray(inputs["L0_attn_values"][0])

    def strips(aT, nstrip):
        return np.ascontiguousarray(
            aT.reshape(8, 128, nstrip, 512).transpose(2, 1, 0, 3))

    kft = strips(keys.T, KSTRIP)
    vft = strips(vals.T, KSTRIP)

    def wtile(wT, KD, MD):
        return np.ascontiguousarray(
            wT.reshape(KD // 128, 128, MD // 128, 128).transpose(2, 1, 0, 3)
        ).reshape(MD // 128, 128, KD)

    w1p = wtile(inputs["L0_ffn_w1_w"].T, D, DHID)
    w2p = wtile(inputs["L0_ffn_w2_w"].T, D, DHID)
    h1p = wtile(inputs["L0_ffn_h1_w"].T, DHID, DHID)
    h2p = wtile(inputs["L0_ffn_h2_w"].T, DHID, DHID)
    w3p = wtile(inputs["L0_ffn_w3_w"].T, DHID, D)
    scol = lambda s, n: np.ascontiguousarray(np.asarray(s).reshape(n, 128).T)
    s1c = scol(inputs["L0_ffn_w1_s"], 32)
    s2c = scol(inputs["L0_ffn_w2_s"], 32)
    sh1c = scol(inputs["L0_ffn_h1_s"], 32)
    sh2c = scol(inputs["L0_ffn_h2_s"], 32)
    s3c = scol(inputs["L0_ffn_w3_s"], 8)

    WqT = np.ascontiguousarray(inputs["L0_attn_Wq_w"].T)
    WkT = np.ascontiguousarray(inputs["L0_attn_Wk_w"].T)
    WvT = np.ascontiguousarray(inputs["L0_attn_Wv_w"].T)

    in_maps = []
    for c in range(8):
        b, g = c // 4, c % 4
        xT = x[b].T
        xfm = strips(xT, NSTRIP)
        wslice = lambda WT: np.ascontiguousarray(
            WT[:, 256 * g:256 * (g + 1)].reshape(8, 128, 256).transpose(1, 0, 2)
        ).reshape(128, 2048)
        bias_col = lambda bvec: np.ascontiguousarray(
            np.asarray(bvec)[256 * g:256 * (g + 1)].reshape(2, 128).T)
        in_maps.append(dict(
            xfm=np.ascontiguousarray(xfm),
            xofm=np.ascontiguousarray(xfm[4 * g: 4 * g + 4]),
            xtm=np.ascontiguousarray(x[b, TOK * g: TOK * (g + 1)]),
            kft=kft, vft=vft,
            wq=wslice(WqT), wk=wslice(WkT), wv=wslice(WvT),
            qb=bias_col(inputs["L0_attn_Wq_b"]),
            kb=bias_col(inputs["L0_attn_Wk_b"]),
            vb=np.ascontiguousarray(
                np.asarray(inputs["L0_attn_Wv_b"])[256 * g:256 * (g + 1)].reshape(1, 256)),
            w1p=w1p, w2p=w2p, h1p=h1p, h2p=h2p, w3p=w3p,
            s1c=s1c, s2c=s2c, sh1c=sh1c, sh2c=sh2c, s3c=s3c,
        ))
    return in_maps


def kernel(**inputs):
    in_maps = _prep_inputs(inputs)
    nc = build_nc(taps=False)
    res = run_bass_kernel_spmd(nc, in_maps, core_ids=list(range(8)))
    out = np.empty((2, SQ, D), dtype=np.float32)
    for c in range(8):
        b, g = c // 4, c % 4
        out[b, TOK * g: TOK * (g + 1)] = res.results[c]["out"]
    return out
